# version 23
# baseline (speedup 1.0000x reference)
import numpy as np
import concourse.bass as bass
import concourse.mybir as mybir
from concourse.bass_utils import run_bass_kernel_spmd
from concourse.alu_op_type import AluOpType as OP

F32 = mybir.dt.float32
BF16 = mybir.dt.bfloat16
I32 = mybir.dt.int32
AF = mybir.ActivationFunctionType
AX = mybir.AxisListType.X

D = 1024
T_TOK = 4096
TT = 256
NT = T_TOK // TT
NSUB = 2
HD = 4096
NST = 256
SEL = 512
EPS = 1e-5
QMAGIC = 1597463007.0
RC = 12582912.0

_names = ("SUM1 SSQ1 M1 E1 R1 B1 SUM2 SSQ2 M2 E2 R2 B2 "
          "SUM3 SSQ3 M3 E3 R3 MX MN TA TB MABS SC1 A1 B1Q G1 "
          "SUMF SSQF MF EF RF BF GG MAB2 SC2 G2 QA QB QC PA PB PC").split()
C = {n: 2 * k for k, n in enumerate(_names)}
STT_COLS = 2 * len(_names)


class Q:
    ENGS = ("sy", "v", "a", "t", "g")

    def __init__(self):
        self.q = {e: [] for e in self.ENGS}
        self.cnt = {s: 0 for s in ("sm", "sx0", "sx1", "so", "sw0", "sw1",
                                   "swi", "gd", "gp", "pe", "dv", "ac",
                                   "ad", "vd")}

    def do(self, qn, f):
        self.q[qn].append(f)

    def op(self, qn, f, sem=None, k=1):
        if sem is None:
            self.q[qn].append(f)
            return None
        self.cnt[sem] += k
        self.q[qn].append(lambda e, S, f=f, sem=sem, k=k: f(e, S).then_inc(S[sem], k))
        return self.cnt[sem]

    def wait(self, qn, sem, v):
        if not v:
            return
        self.q[qn].append(lambda e, S, sem=sem, v=v: e.wait_ge(S[sem], v))

    def drain(self, qn):
        self.q[qn].append(lambda e, S: e.drain())


def build_program(ntr=NT):
    nc = bass.Bass("TRN2", target_bir_lowering=False)

    x_hbm = nc.dram_tensor("x", [T_TOK, D], F32, kind="ExternalInput").ap()
    winT_h = nc.dram_tensor("winT", [128, 8 * NST], F32, kind="ExternalInput").ap()
    wselT_h = nc.dram_tensor("wselT", [128, 8 * SEL], F32, kind="ExternalInput").ap()
    wsoT_h = nc.dram_tensor("wsoT", [128, 4 * D], F32, kind="ExternalInput").ap()
    woutT_h = nc.dram_tensor("woutT", [128, 2 * D], F32, kind="ExternalInput").ap()
    w1T_h = nc.dram_tensor("w1T", [128, 8 * HD], F32, kind="ExternalInput").ap()
    w2T_h = nc.dram_tensor("w2T", [128, 32 * D], F32, kind="ExternalInput").ap()
    dp_h = nc.dram_tensor("dp", [128, 2], F32, kind="ExternalInput").ap()
    cw_h = nc.dram_tensor("cw", [128, 96], F32, kind="ExternalInput").ap()
    cb_h = nc.dram_tensor("cb", [128, 32], F32, kind="ExternalInput").ap()
    out_hbm = nc.dram_tensor("out", [T_TOK, D], F32, kind="ExternalOutput").ap()
    w2s = nc.dram_tensor("w2s", [128, 32 * D], BF16, kind="Internal").ap()

    q = Q()
    from contextlib import ExitStack
    st_ = ExitStack()
    sb = lambda name, shape, dt: st_.enter_context(nc.sbuf_tensor(name, shape, dt))
    ps_ = lambda name, shape, dt: st_.enter_context(nc.psum_tensor(name, shape, dt))

    with st_:
        identf = sb("identf", [128, 128], F32)
        identb = sb("identb", [128, 128], BF16)
        ones = sb("ones", [128, 128], F32)
        winT = sb("winT_b", [128, 8 * NST], BF16)
        wselT = sb("wselT_b", [128, 8 * SEL], BF16)
        wsoT = sb("wsoT_b", [128, 4 * D], BF16)
        woutT = sb("woutT_b", [128, 2 * D], BF16)
        w1t = sb("w1t", [128, 8 * HD], BF16)
        dp2 = sb("dp2", [128, 2], F32)
        cwt = sb("cwt", [128, 96], F32)
        cbt = sb("cbt", [128, 32], F32)
        wcons = sb("wcons", [128, 4], F32)
        wtot = sb("wtot", [1, 4], F32)
        stgA = sb("stgA", [128, 1024], F32)
        stgF = sb("stgF", [128, 2048], F32)
        stgi = sb("stgi", [128, 512], I32)
        stgb = sb("stgb", [128, 2048], BF16)
        xin = sb("xin", [128, 2 * 2048], F32)
        x2 = sb("x2", [128, 5 * 1024], F32)
        xin1tm = sb("xin1tm", [128, 2048], BF16)
        xin1fm = sb("xin1fm", [128, 2048], BF16)
        xqw = sb("xqw", [128, 2 * 8 * 258], BF16)
        u_sb = sb("u_sb", [128, 512], BF16)
        tad = sb("tad", [128, 1024], F32)
        tbc = sb("tbc", [128, 1024], F32)
        hsc = sb("hsc", [128, 2 * 512], F32)
        ysb = sb("ysb", [128, 512], BF16)
        selh = sb("selh", [128, 1024], BF16)
        h4 = sb("h4", [128, 32 * 256], BF16)
        hpre = sb("hpre", [128, 5 * 258], BF16)
        k2i = sb("k2i", [128, 2 * 512], I32)
        bc2 = sb("bc2", [128, 256], BF16)
        accP = sb("accP", [128, 512], BF16)
        accN = sb("accN", [128, 512], BF16)
        accF = sb("accF", [128, 256], BF16)
        s2row = sb("s2row", [1, 256], F32)
        w2sb = sb("w2sb", [128, 2048], BF16)
        stt = sb("stt", [128, STT_COLS], F32)
        seedi = sb("seedi", [128, 2], I32)
        seedp = sb("seedp", [128, 2], I32)
        rcc = sb("rcc", [128, 1], F32)
        rccn = sb("rccn", [128, 1], F32)
        onec = sb("onec", [128, 1], F32)
        halfc = sb("halfc", [128, 1], F32)
        zeroc = sb("zeroc", [128, 1], F32)

        pB = ps_("pB", [128, 512], F32)
        pC = ps_("pC", [128, 1024], F32)
        pD = ps_("pD", [128, 1024], F32)
        pE = ps_("pE", [128, 1536], F32)

        sems = {}
        for s in ("sm", "sx0", "sx1", "so", "sw0", "sw1", "swi", "gd", "gp",
                  "pe", "dv", "ac", "ad", "vd"):
            sems[s] = st_.enter_context(nc.semaphore(s))

        xinv = xin.ap().rearrange("p (q u d) -> p q u d", q=2, u=2)
        fm1v = xin1fm.ap().rearrange("p (c u w) -> p c u w", c=8, u=2)
        pCv = pC.ap().rearrange("p (c w) -> p c w", c=8)
        xqv = xqw.ap().rearrange("p (b c w) -> p b c w", b=2, c=8)

        def sc(col, s=0, n=1):
            return stt[:, C[col] + s:C[col] + s + n]

        def x2slot(i, s):
            return (i % 2) if s == 0 else 2 + (i % 3)

        def x2s(i, s):
            slot = x2slot(i, s)
            return x2[:, 1024 * slot:1024 * (slot + 1)]

        def x2sh(i, s, h):
            slot = x2slot(i, s)
            return x2[:, 1024 * slot + 512 * h:1024 * slot + 512 * (h + 1)]

        q.do("g", lambda e, S: e.memset(ones[:, :], 1.0))
        q.do("g", lambda e, S: e.memset(rcc[:, :], RC))
        q.do("g", lambda e, S: e.memset(rccn[:, :], -RC))
        q.do("g", lambda e, S: e.memset(onec[:, :], 1.0))
        q.do("g", lambda e, S: e.memset(halfc[:, :], 0.5))
        q.do("g", lambda e, S: e.memset(zeroc[:, :], 0.0))
        q.drain("g")
        q.op("g", lambda e, S: e.affine_select(
            identf[:, :], ones[:, :], [[1, 128]], OP.is_equal, 0.0,
            base=0, channel_multiplier=-1), sem="gp")
        gp_id = q.cnt["gp"]
        q.wait("v", "gp", gp_id)
        dv_id = q.op("v", lambda e, S: e.tensor_copy(identb[:, :], identf[:, :]),
                     sem="dv")
        dv_pad0 = q.op("v", lambda e, S: e.memset(xqv[:, 0, :, 0:129], 0.0),
                       sem="dv")

        for dst, src in ((dp2, dp_h), (cwt, cw_h), (cbt, cb_h)):
            q.op("sy", lambda e, S, dst=dst, src=src: e.dma_start(
                out=dst[:, :], in_=src[:, :]), sem="sm", k=16)
        si_small = q.cnt["sm"]
        q.wait("v", "sm", si_small)
        q.drain("v")
        dv_dp2 = q.op("v", lambda e, S: e.tensor_scalar_mul(dp2[:, :], dp2[:, :], 0.5),
                      sem="dv")

        si_x = [None] * (NT + 2)

        def issue_x_dma(i, wait_gp=0, wait_ac=0):
            if i >= ntr:
                return
            par = i % 2
            src = x_hbm[i * TT:(i + 1) * TT, :].rearrange("(s p) d -> p s d", p=128)
            dst = xinv[:, par, :, :]
            q.wait("sy", "dv", wait_gp)
            q.wait("sy", "ac", wait_ac)
            sxn = "sx%d" % (i % 2)
            si_x[i] = (sxn, q.op("sy", lambda e, S, src=src, dst=dst:
                                 e.dma_start(out=dst, in_=src), sem=sxn, k=16))

        issue_x_dma(0)
        issue_x_dma(1)

        stg_gate = {0: None, 1: None}
        last_dma = {"gd": q.cnt["gd"], "sm": q.cnt["sm"]}

        def piece_dma(p, src_h, gate):
            qn, sem = ("g", "gd") if p % 2 == 0 else ("sy", "sm")
            if gate is None:
                gate = stg_gate[p % 2]
            if gate:
                q.wait(qn, gate[0], gate[1])
            q.wait(qn, sem, last_dma[sem])
            v = q.op(qn, lambda e, S, p=p, src_h=src_h: e.dma_start(
                out=stgF[:, 1024 * (p % 2):1024 * (p % 2 + 1)],
                in_=src_h[:, 1024 * p:1024 * (p + 1)]), sem=sem, k=16)
            last_dma[sem] = v
            return (sem, v)

        def stage_cast(dst, src_h, ncols):
            npc = (ncols + 1023) // 1024
            dmas = [None] * (npc + 2)
            cons = [0] * (npc + 2)
            dmas[0] = piece_dma(0, src_h, None)
            if npc > 1:
                dmas[1] = piece_dma(1, src_h, None)
            for p in range(npc):
                w = min(1024, ncols - 1024 * p)
                q.wait("v", *dmas[p])
                cons[p] = q.op("v", lambda e, S, dst=dst, p=p, w=w: e.tensor_copy(
                    dst[:, 1024 * p:1024 * p + w],
                    stgF[:, 1024 * (p % 2):1024 * (p % 2) + w]), sem="dv")
                stg_gate[p % 2] = ("dv", cons[p])
                if p + 2 < npc:
                    dmas[p + 2] = piece_dma(p + 2, src_h, ("dv", cons[p]))
            return q.cnt["dv"]

        stage_cast(winT, winT_h, 8 * NST)
        stage_cast(wselT, wselT_h, 8 * SEL)
        stage_cast(wsoT, wsoT_h, 4 * D)
        dv_attn_w = stage_cast(woutT, woutT_h, 2 * D)

        last_dma.update({"ad": 0, "vd": 0})

        class WP:
            pass

        c1 = WP()
        c1.src, c1.npc = w1T_h, 32
        c1.queues = (("g", "gd"), ("sy", "sm"))
        c1.slot = lambda j: stgF[:, 1024 * j:1024 * (j + 1)]
        c1.acc, c1.accs, c1.vs = "QA", "QB", "QC"
        c1.junk = xin1tm[:, 0:1024]
        c1.mean_col, c1.scale_col = 0, 1
        c1.gate = [stg_gate[0], stg_gate[1]]
        c2 = WP()
        c2.src, c2.npc = w2T_h, 32
        c2.queues = (("a", "ad"), ("a", "ad"))
        c2.slot = lambda j: (tad if j == 0 else tbc)[:, 0:1024]
        c2.acc, c2.accs, c2.vs = "PA", "PB", "PC"
        c2.junk = xin1fm[:, 0:1024]
        c2.mean_col, c2.scale_col = 2, 3
        c2.gate = [None, None]

        def wp_dma(ctx, p, gate):
            qn, sem = ctx.queues[p % 2]
            if gate is None:
                gate = ctx.gate[p % 2]
            if gate:
                q.wait(qn, gate[0], gate[1])
            q.wait(qn, sem, last_dma[sem])
            v = q.op(qn, lambda e, S, p=p, ctx=ctx: e.dma_start(
                out=ctx.slot(p % 2), in_=ctx.src[:, 1024 * p:1024 * (p + 1)]),
                sem=sem, k=16)
            last_dma[sem] = v
            return (sem, v)

        for ctx in (c1, c2):
            ctx.dmas = [None] * (ctx.npc + 2)
            ctx.addm = [0] * (ctx.npc + 2)
            ctx.acm = [0] * (ctx.npc + 2)
            ctx.dmas[0] = wp_dma(ctx, 0, None)
            ctx.dmas[1] = wp_dma(ctx, 1, None)
        for p in range(32):
            for ctx in (c1, c2):
                q.wait("a", *ctx.dmas[p])
                if p >= 2:
                    q.wait("a", "dv", ctx.addm[p - 2])
                q.drain("a")
                ctx.acm[p] = q.op("a", lambda e, S, p=p, ctx=ctx: e.activation(
                    ctx.junk, ctx.slot(p % 2), AF.Abs,
                    accum_out=sc(ctx.acc, p % 2)), sem="ac")
                ctx.gate[p % 2] = ("ac", ctx.acm[p])
                if p + 2 < ctx.npc:
                    ctx.dmas[p + 2] = wp_dma(ctx, p + 2, ("ac", ctx.acm[p]))
                q.wait("v", "ac", ctx.acm[p])
                if p == 0:
                    ctx.addm[p] = q.op("v", lambda e, S, ctx=ctx: e.tensor_copy(
                        sc(ctx.accs), sc(ctx.acc, 0)), sem="dv")
                else:
                    q.drain("v")
                    ctx.addm[p] = q.op("v", lambda e, S, p=p, ctx=ctx: e.tensor_tensor(
                        sc(ctx.accs), sc(ctx.accs), sc(ctx.acc, p % 2), OP.add),
                        sem="dv")
        for ctx in (c1, c2):
            q.drain("v")
            vsum = q.op("v", lambda e, S, ctx=ctx: e.tensor_copy(
                sc(ctx.vs), sc(ctx.accs)), sem="dv")
            q.wait("t", "dv", vsum)
            mm1 = q.op("t", lambda e, S, ctx=ctx: e.matmul(
                pC[0:1, 0:1], ones[:, 0:1], sc(ctx.vs), start=True, stop=True),
                sem="pe")
            q.wait("v", "pe", mm1)
            q.op("v", lambda e, S, ctx=ctx: e.tensor_scalar(
                wtot[0:1, ctx.mean_col:ctx.mean_col + 1], pC[0:1, 0:1],
                1.0 / (128.0 * 1024 * ctx.npc), 1e-5, OP.mult, OP.max))
            q.drain("v")
            vrec = q.op("v", lambda e, S, ctx=ctx: e.reciprocal(
                wtot[0:1, ctx.scale_col:ctx.scale_col + 1],
                wtot[0:1, ctx.mean_col:ctx.mean_col + 1]), sem="dv")
            q.drain("v")
            q.wait("t", "dv", vrec)
            mm2 = q.op("t", lambda e, S, ctx=ctx: e.matmul(
                pC[:, 0:2], ones[0:1, 0:128],
                wtot[0:1, ctx.mean_col:ctx.mean_col + 2],
                start=True, stop=True), sem="pe")
            q.wait("v", "pe", mm2)
            ctx.vbc = q.op("v", lambda e, S, ctx=ctx: e.tensor_copy(
                wcons[:, ctx.mean_col:ctx.mean_col + 2], pC[:, 0:2]), sem="dv")
            q.drain("v")

        for ctx in (c1, c2):
            ctx.dmas = [None] * (ctx.npc + 2)
            ctx.clip = [0] * (ctx.npc + 2)
            ctx.dmas[0] = wp_dma(ctx, 0, None)
            ctx.dmas[1] = wp_dma(ctx, 1, None)
        last_sw = [q.cnt["swi"], q.cnt["swi"]]
        for p in range(32):
            ctx = c1
            q.wait("a", *ctx.dmas[p])
            if p == 0:
                q.wait("a", "dv", ctx.vbc)
            if p >= 1:
                q.wait("a", "dv", ctx.clip[p - 1])
            a_an = q.op("a", lambda e, S, p=p, ctx=ctx: e.activation(
                k2i[:, 0:1024], ctx.slot(p % 2), AF.Identity,
                scale=wcons[:, ctx.scale_col:ctx.scale_col + 1],
                bias=rcc[:, 0:1]), sem="ac")
            ctx.gate[p % 2] = ("ac", a_an)
            if p + 2 < ctx.npc:
                ctx.dmas[p + 2] = wp_dma(ctx, p + 2, ("ac", a_an))
            q.wait("v", "ac", a_an)
            dstb = w1t[:, 1024 * p:1024 * (p + 1)]
            q.drain("v")
            q.op("v", lambda e, S, dstb=dstb: e.tensor_scalar(
                dstb, k2i[:, 0:1024], -RC, -1.0, OP.add, OP.max))
            q.drain("v")
            ctx.clip[p] = q.op("v", lambda e, S, dstb=dstb: e.tensor_scalar(
                dstb, dstb, 1.0, None, OP.min), sem="dv")
            ctx = c2
            q.wait("a", *ctx.dmas[p])
            if p == 0:
                q.wait("a", "dv", ctx.vbc)
            dstb = stgb[:, 0:1024] if p % 2 == 0 else selh[:, 0:1024]
            lastc = 0
            for h in range(2):
                if p >= 1 or h == 1:
                    q.wait("a", "dv", ctx.clip[p - 1] if h == 0 else lastc)
                a_an = q.op("a", lambda e, S, p=p, h=h, ctx=ctx: e.activation(
                    stgi[:, :], ctx.slot(p % 2)[:, 512 * h:512 * (h + 1)],
                    AF.Identity,
                    scale=wcons[:, ctx.scale_col:ctx.scale_col + 1],
                    bias=rcc[:, 0:1]), sem="ac")
                if h == 1:
                    ctx.gate[p % 2] = ("ac", a_an)
                    if p + 2 < ctx.npc:
                        ctx.dmas[p + 2] = wp_dma(ctx, p + 2, ("ac", a_an))
                q.wait("v", "ac", a_an)
                if h == 0 and p >= 2:
                    q.wait("v", "swi", last_sw[p % 2])
                dsl = dstb[:, 512 * h:512 * (h + 1)]
                q.drain("v")
                q.op("v", lambda e, S, dsl=dsl: e.tensor_scalar(
                    dsl, stgi[:, :], -RC, -1.0, OP.add, OP.max))
                q.drain("v")
                lastc = q.op("v", lambda e, S, dsl=dsl: e.tensor_scalar(
                    dsl, dsl, 1.0, None, OP.min), sem="dv")
            ctx.clip[p] = lastc
            q.wait("a", "dv", ctx.clip[p])
            q.wait("a", "swi", max(last_sw))
            last_sw[p % 2] = q.op("a", lambda e, S, p=p, dstb=dstb: e.dma_start(
                out=w2s[:, 1024 * p:1024 * (p + 1)], in_=dstb),
                sem="swi", k=16)
        dv_w1 = c1.clip[31]
        dv_w2 = c2.vbc
        sw_w2 = last_sw[1] if last_sw[1] > last_sw[0] else last_sw[0]

        def quake(qn, dst_ap, src_ap, seed_t, n, pre):
            qa, qb, qc = (sc(pre[0], 0, n), sc(pre[1], 0, n), sc(pre[2], 0, n))
            q.op(qn, lambda e, S: e.tensor_copy(qa, src_ap.bitcast(I32)))
            q.drain(qn)
            q.op(qn, lambda e, S: e.tensor_scalar(
                seed_t[:, 0:n], qa, -0.5, QMAGIC, OP.mult, OP.add))
            q.drain(qn)
            cur = seed_t[:, 0:n].bitcast(F32)
            for it in range(3):
                tgt = dst_ap if it == 2 else qa
                q.op(qn, lambda e, S, cur=cur: e.tensor_tensor(qb, cur, cur, OP.mult))
                q.drain(qn)
                q.op(qn, lambda e, S: e.tensor_tensor(qc, src_ap, qb, OP.mult))
                q.drain(qn)
                q.op(qn, lambda e, S: e.tensor_scalar(qc, qc, -0.5, 1.5, OP.mult, OP.add))
                q.drain(qn)
                q.op(qn, lambda e, S, cur=cur, tgt=tgt: e.tensor_tensor(
                    tgt, cur, qc, OP.mult))
                q.drain(qn)
                cur = qa

        def stat_chain(qn, pre, n0, nn, var_scale_col=None):
            scr = ("PA", "PB", "PC") if qn == "g" else ("QA", "QB", "QC")
            seed_t = seedp if qn == "g" else seedi
            sem = "gp" if qn == "g" else "dv"
            Mc = sc("M" + pre, n0, nn)
            Ec = sc("E" + pre, n0, nn)
            q.drain(qn)
            q.op(qn, lambda e, S: e.tensor_scalar(
                Mc, sc("SUM" + pre, n0, nn), -1.0 / D, None, OP.mult))
            q.op(qn, lambda e, S: e.tensor_scalar(
                Ec, sc("SSQ" + pre, n0, nn), 1.0 / D, None, OP.mult))
            q.drain(qn)
            q.op(qn, lambda e, S: e.tensor_tensor(
                sc(scr[1], n0, nn), Mc, Mc, OP.mult))
            q.drain(qn)
            q.op(qn, lambda e, S: e.tensor_tensor(Ec, Ec, sc(scr[1], n0, nn),
                                                  OP.subtract))
            q.drain(qn)
            if var_scale_col is not None:
                q.op(qn, lambda e, S: e.tensor_tensor(
                    sc("GG", n0, nn), sc(var_scale_col, n0, nn),
                    sc(var_scale_col, n0, nn), OP.mult))
                q.drain(qn)
                q.op(qn, lambda e, S: e.tensor_tensor(
                    Ec, Ec, sc("GG", n0, nn), OP.mult))
                q.drain(qn)
            q.op(qn, lambda e, S: e.tensor_scalar(Ec, Ec, EPS, None, OP.add))
            q.drain(qn)
            quake(qn, sc("R" + pre, n0, nn), Ec, seed_t, nn, scr)
            q.drain(qn)
            if var_scale_col is not None:
                q.op(qn, lambda e, S: e.tensor_tensor(
                    sc("R" + pre, n0, nn), sc("R" + pre, n0, nn),
                    sc(var_scale_col, n0, nn), OP.mult))
                q.drain(qn)
            q.op(qn, lambda e, S: e.tensor_tensor(
                sc("B" + pre, n0, nn), Mc, sc("R" + pre, n0, nn), OP.mult))
            return q.op(qn, lambda e, S: e.tensor_copy(
                sc(scr[2], 0, 1), sc("M" + pre, n0, 1)), sem=sem)

        Z = lambda: [0] * (NT + 2)
        Z2 = lambda: [[0, 0] for _ in range(NT + 2)]
        PE_T1, A_evT1, PE_u, PE_sel = Z2(), Z(), Z(), Z()
        PE_g1, PE_g2, A_u, A_selh, A_tad, A_tbc, A_aff, V_y = (
            Z(), Z(), Z(), Z(), Z(), Z(), Z(), Z())
        PE_ao, A_aoev, A_t, P_x2 = Z2(), Z2(), Z2(), Z2()
        PE_T2, A_evT2 = Z2(), Z2()
        A_xin1, V_q1sc, V_halo = Z(), Z(), Z()
        P_uq1 = [[0] * 4 for _ in range(NT + 2)]
        PE_w1 = [[0] * 32 for _ in range(NT + 2)]
        A_evh = [[0] * 32 for _ in range(NT + 2)]
        V_cv = [[0] * 32 for _ in range(NT + 2)]
        A_sil = [[0] * 32 for _ in range(NT + 2)]
        A_anc = [[0] * 16 for _ in range(NT + 2)]
        P_uq2 = [[0] * 16 for _ in range(NT + 2)]
        PE_w2c = [[0] * 32 for _ in range(NT + 2)]
        SW_w2t = [[None] * 32 for _ in range(NT + 2)]
        PE_w2, V_bc2, A_ffevD, A_ffevE, V_chF = Z(), Z(), Z(), Z(), Z()
        P_out, SO_cnt = Z2(), Z()

        def emit_ln1_prep(i, wait_dv=0):
            if i >= ntr:
                return
            q.wait("a", *si_x[i])
            q.wait("a", "dv", wait_dv)
            jw = PE_T1[i - 1][1] if i > 0 else 0
            if jw:
                q.wait("a", "pe", jw)
            q.drain("a")
            for s in range(NSUB):
                q.op("a", lambda e, S, s=s, i=i: e.activation(
                    xin1tm[:, 1024 * s:1024 * (s + 1)], xinv[:, i % 2, s, :],
                    AF.Identity, accum_out=sc("SUM1", s)), sem="ac")
            q.drain("a")
            for s in range(NSUB):
                q.op("a", lambda e, S, s=s, i=i: e.activation(
                    xin1tm[:, 1024 * s:1024 * (s + 1)], xinv[:, i % 2, s, :],
                    AF.Square, accum_out=sc("SSQ1", s)), sem="ac")
            a_sums = q.cnt["ac"]
            q.wait("v", "ac", a_sums)
            if i == 0:
                q.wait("v", "dv", dv_dp2)
            g_ch = stat_chain("v", "1", 0, 2)
            q.wait("a", "dv", g_ch)
            for s in range(NSUB):
                q.op("a", lambda e, S, s=s, i=i: e.activation(
                    xin1tm[:, 1024 * s:1024 * (s + 1)], xinv[:, i % 2, s, :],
                    AF.Identity, bias=sc("B1", s), scale=sc("R1", s)), sem="ac")
            A_xin1[i] = q.cnt["ac"]

        def attn_gen(i):
            par = i % 2

            if i == 0:
                emit_ln1_prep(0)
                q.wait("t", "dv", dv_id)
                q.wait("t", "dv", dv_attn_w)
            q.wait("t", "ac", A_xin1[i])
            if i > 0:
                q.wait("t", "ac", A_evT2[i - 1][1])
            ev0 = [0]
            for s in range(NSUB):
                if s == 1:
                    q.wait("t", "ac", ev0[0])
                for cc in range(8):
                    PE_T1[i][s] = q.op("t", lambda e, S, s=s, cc=cc: e.matmul(
                        pC[:, 128 * cc:128 * (cc + 1)],
                        xin1tm[:, 1024 * s + 128 * cc:1024 * s + 128 * (cc + 1)],
                        identb[:, :], start=True, stop=True),
                        sem="pe" if cc == 7 else None)
                q.wait("a", "pe", PE_T1[i][s])
                if i > 0 and s == 0:
                    q.wait("a", "pe", PE_sel[i - 1])
                ev = q.op("a", lambda e, S, s=s: e.activation(
                    fm1v[:, :, s, :], pCv[:, :, :], AF.Identity), sem="ac")
                if s == 0:
                    ev0[0] = ev
            A_evT1[i] = q.cnt["ac"]
            yield

            q.wait("t", "ac", A_evT1[i])
            if i > 1:
                q.wait("t", "dv", V_bc2[i - 2])
            for n in range(2):
                for k in range(8):
                    PE_u[i] = q.op("t", lambda e, S, n=n, k=k: e.matmul(
                        pB[:, 256 * n:256 * (n + 1)],
                        winT[:, 256 * k + 128 * n:256 * k + 128 * (n + 1)],
                        xin1fm[:, 256 * k:256 * (k + 1)],
                        start=(k == 0), stop=(k == 7)),
                        sem="pe" if (n == 1 and k == 7) else None)
            for m in range(4):
                for k in range(8):
                    PE_sel[i] = q.op("t", lambda e, S, m=m, k=k: e.matmul(
                        pC[:, 256 * m:256 * (m + 1)],
                        wselT[:, 512 * k + 128 * m:512 * k + 128 * (m + 1)],
                        xin1fm[:, 256 * k:256 * (k + 1)],
                        start=(k == 0), stop=(k == 7)),
                        sem="pe" if (m == 3 and k == 7) else None)
            q.wait("a", "pe", PE_u[i])
            if i > 0:
                q.wait("a", "dv", V_y[i - 1])
            A_u[i] = q.op("a", lambda e, S: e.activation(
                u_sb[:, :], pB[:, :], AF.Identity), sem="ac")
            q.wait("a", "pe", PE_sel[i])
            if i == 0:
                q.wait("a", "swi", sw_w2)
            if i > 0:
                q.wait("a", "pe", PE_g2[i - 1])
            A_selh[i] = q.op("a", lambda e, S: e.activation(
                selh[:, :], pC[:, :], AF.Silu), sem="ac")
            yield

            q.wait("t", "ac", A_selh[i])
            for gi, m in enumerate((0, 1, 6, 7)):
                for k in range(4):
                    PE_g1[i] = q.op("t", lambda e, S, gi=gi, m=m, k=k: e.matmul(
                        pC[:, 256 * gi:256 * (gi + 1)],
                        wsoT[:, 1024 * k + 128 * m:1024 * k + 128 * (m + 1)],
                        selh[:, 256 * k:256 * (k + 1)],
                        start=(k == 0), stop=(k == 3)),
                        sem="pe" if (gi == 3 and k == 3) else None)
            yield

            q.wait("a", "pe", PE_g1[i])
            if i > 0:
                q.wait("a", "dv", V_y[i - 1])
            A_tad[i] = q.op("a", lambda e, S: e.activation(
                tad[:, :], pC[:, :], AF.Tanh, scale=0.5), sem="ac")
            q.drain("a")
            A_aff[i] = q.op("a", lambda e, S: e.activation(
                tad[:, 0:512], tad[:, 0:512], AF.Identity,
                scale=halfc[:, 0:1], bias=halfc[:, 0:1]),
                sem="ac")
            q.wait("t", "ac", A_tad[i])
            for gi, m in enumerate((2, 3, 4, 5)):
                for k in range(4):
                    PE_g2[i] = q.op("t", lambda e, S, gi=gi, m=m, k=k: e.matmul(
                        pC[:, 256 * gi:256 * (gi + 1)],
                        wsoT[:, 1024 * k + 128 * m:1024 * k + 128 * (m + 1)],
                        selh[:, 256 * k:256 * (k + 1)],
                        start=(k == 0), stop=(k == 3)),
                        sem="pe" if (gi == 3 and k == 3) else None)
            q.wait("a", "pe", PE_g2[i])
            A_tbc[i] = q.op("a", lambda e, S: e.activation(
                tbc[:, :], pC[:, :], AF.Tanh), sem="ac")
            q.wait("v", "ac", A_aff[i])
            q.wait("v", "ac", A_u[i])
            if i > 0:
                q.wait("v", "pe", PE_ao[i - 1][1])
            q.drain("v")
            q.op("v", lambda e, S: e.scalar_tensor_tensor(
                tad[:, 512:1024], tad[:, 512:1024], 1.0, u_sb[:, :], OP.add, OP.mult))
            q.wait("v", "ac", A_tbc[i])
            q.op("v", lambda e, S: e.tensor_tensor(
                tbc[:, 0:512], tbc[:, 0:512], u_sb[:, :], OP.mult))
            q.drain("v")
            hof, hofp = 512 * par, 512 * (1 - par)
            for n in range(2):
                init = 0.0 if i == 0 else hsc[:, hofp + 256 * n + 255:hofp + 256 * n + 256]
                q.op("v", lambda e, S, n=n, init=init, hof=hof: e.tensor_tensor_scan(
                    hsc[:, hof + 256 * n:hof + 256 * (n + 1)],
                    tad[:, 256 * n:256 * (n + 1)],
                    tbc[:, 256 * n:256 * (n + 1)],
                    init, OP.mult, OP.add))
            q.drain("v")
            q.op("v", lambda e, S, hof=hof: e.tensor_tensor(
                tbc[:, 512:1024], tbc[:, 512:1024], hsc[:, hof:hof + 512], OP.mult))
            q.drain("v")
            for n in range(2):
                V_y[i] = q.op("v", lambda e, S, n=n: e.scalar_tensor_tensor(
                    ysb[:, 256 * n:256 * (n + 1)],
                    tad[:, 512 + 256 * n:512 + 256 * (n + 1)],
                    dp2[:, n:n + 1],
                    tbc[:, 512 + 256 * n:512 + 256 * (n + 1)],
                    OP.mult, OP.add), sem="dv")
            yield

            q.wait("t", "dv", V_y[i])
            q.wait("t", "ac", A_tbc[i])
            for s in range(NSUB):
                bank = pC if s == 0 else pD
                if s == 1:
                    if i >= 3:
                        q.wait("t", "ac", A_ffevD[i - 2])
                    elif i >= 1:
                        q.wait("t", "ac", A_aoev[i - 1][1])
                for n in range(2):
                    for jj in range(2):
                        PE_ao[i][s] = q.op("t",
                            lambda e, S, bank=bank, s=s, n=n, jj=jj: e.matmul(
                                bank[:, 512 * jj:512 * (jj + 1)],
                                ysb[:, 256 * n + 128 * s:256 * n + 128 * (s + 1)],
                                woutT[:, 1024 * n + 512 * jj:1024 * n + 512 * (jj + 1)],
                                start=(n == 0), stop=(n == 1)),
                            sem="pe" if (n == 1 and jj == 1) else None)
            q.wait("a", "pe", PE_ao[i][0])
            if i > 0:
                q.wait("a", "dv", P_x2[i - 1][0])
            A_aoev[i][0] = q.op("a", lambda e, S: e.activation(
                stgA[:, :], pC[:, 0:1024], AF.Identity, accum_out=sc("SUM2", 0)),
                sem="ac")
            q.wait("a", "pe", PE_ao[i][1])
            if i > 1 and SO_cnt[i - 2]:
                q.wait("a", "so", SO_cnt[i - 2])
            if i > 0:
                q.wait("a", "dv", P_x2[i - 1][1])
            A_aoev[i][1] = q.op("a", lambda e, S: e.activation(
                stgF[:, 0:1024], pD[:, 0:1024], AF.Identity, accum_out=sc("SUM2", 1)),
                sem="ac")
            q.drain("a")
            q.op("a", lambda e, S: e.activation(
                xin1tm[:, 0:1024], stgA[:, :], AF.Square, accum_out=sc("SSQ2", 0)),
                sem="ac")
            q.op("a", lambda e, S: e.activation(
                xin1tm[:, 1024:2048], stgF[:, 0:1024], AF.Square,
                accum_out=sc("SSQ2", 1)), sem="ac")
            a_sq2 = q.cnt["ac"]
            q.wait("g", "ac", a_sq2)
            g_ch2 = stat_chain("g", "2", 0, 2)
            yield

            yield

            q.wait("a", "gp", g_ch2)
            for s in range(NSUB):
                dst = stgA[:, :] if s == 0 else stgF[:, 0:1024]
                A_t[i][s] = q.op("a", lambda e, S, s=s, dst=dst: e.activation(
                    dst, dst, AF.Identity, bias=sc("B2", s), scale=sc("R2", s)),
                    sem="ac")
            for s in range(NSUB):
                q.wait("v", "ac", A_t[i][s])
                if i > 1:
                    q.wait("v", "gp", P_out[i - 2][1 - s])
                q.drain("v")
                src = stgA[:, :] if s == 0 else stgF[:, 0:1024]
                P_x2[i][s] = q.op("v", lambda e, S, s=s, src=src, i=i:
                                  e.tensor_tensor(x2s(i, s), xinv[:, i % 2, s, :],
                                                  src, OP.add), sem="dv")
            issue_x_dma(i + 2, wait_gp=P_x2[i][1], wait_ac=A_xin1[i])
            q.drain("v")
            for s in range(NSUB):
                q.op("v", lambda e, S, s=s, i=i: e.reduce_max(
                    sc("MX", s), x2s(i, s), AX))
                q.op("v", lambda e, S, s=s, i=i: e.tensor_reduce(
                    sc("MN", s), x2s(i, s), AX, OP.min))
            for s in range(NSUB):
                q.wait("a", "dv", P_x2[i][s])
                q.op("a", lambda e, S, s=s, i=i: e.activation(
                    xin1tm[:, 1024 * s:1024 * (s + 1)], x2s(i, s), AF.Square,
                    accum_out=sc("SSQ3", s)), sem="ac")
            a_sum3 = q.cnt["ac"]
            q.drain("v")
            q.op("v", lambda e, S: e.tensor_tensor(
                sc("QB", 0, 2), sc("R2", 0, 2), sc("SUM2", 0, 2), OP.mult))
            q.drain("v")
            q.op("v", lambda e, S: e.tensor_tensor(
                sc("QB", 0, 2), sc("QB", 0, 2), sc("SUM1", 0, 2), OP.add))
            q.drain("v")
            q.op("v", lambda e, S: e.tensor_scalar(
                sc("QC", 0, 2), sc("B2", 0, 2), float(D), None, OP.mult))
            q.drain("v")
            v_sum3 = q.op("v", lambda e, S: e.tensor_tensor(
                sc("SUM3", 0, 2), sc("QB", 0, 2), sc("QC", 0, 2), OP.add), sem="dv")
            emit_ln1_prep(i + 1, wait_dv=v_sum3)
            q.wait("v", "ac", a_sum3)
            q.drain("v")
            q.op("v", lambda e, S: e.tensor_scalar(
                sc("M3", 0, 2), sc("SUM3", 0, 2), -1.0 / D, None, OP.mult))
            q.op("v", lambda e, S: e.tensor_scalar(
                sc("E3", 0, 2), sc("SSQ3", 0, 2), 1.0 / D, None, OP.mult))
            q.drain("v")
            q.op("v", lambda e, S: e.tensor_tensor(
                sc("QB", 0, 2), sc("M3", 0, 2), sc("M3", 0, 2), OP.mult))
            q.drain("v")
            q.op("v", lambda e, S: e.tensor_tensor(
                sc("E3", 0, 2), sc("E3", 0, 2), sc("QB", 0, 2), OP.subtract))
            q.drain("v")
            q.op("v", lambda e, S: e.tensor_scalar(
                sc("E3", 0, 2), sc("E3", 0, 2), EPS, None, OP.add))
            q.drain("v")
            quake("v", sc("R3", 0, 2), sc("E3", 0, 2), seedi, 2, ("QA", "QB", "QC"))
            q.drain("v")
            q.op("v", lambda e, S: e.tensor_tensor(
                sc("TA", 0, 2), sc("MX", 0, 2), sc("M3", 0, 2), OP.add))
            q.op("v", lambda e, S: e.tensor_tensor(
                sc("TB", 0, 2), sc("MN", 0, 2), sc("M3", 0, 2), OP.add))
            q.drain("v")
            q.op("v", lambda e, S: e.tensor_tensor(
                sc("TA", 0, 2), sc("TA", 0, 2), sc("R3", 0, 2), OP.mult))
            q.op("v", lambda e, S: e.tensor_scalar(
                sc("TB", 0, 2), sc("TB", 0, 2), -1.0, None, OP.mult))
            q.drain("v")
            q.op("v", lambda e, S: e.tensor_tensor(
                sc("TB", 0, 2), sc("TB", 0, 2), sc("R3", 0, 2), OP.mult))
            q.drain("v")
            q.op("v", lambda e, S: e.tensor_tensor(
                sc("MABS", 0, 2), sc("TA", 0, 2), sc("TB", 0, 2), OP.max))
            q.drain("v")
            q.op("v", lambda e, S: e.tensor_scalar(
                sc("MABS", 0, 2), sc("MABS", 0, 2), 1e-5, None, OP.max))
            q.drain("v")
            q.op("v", lambda e, S: e.reciprocal(sc("SC1", 0, 2), sc("MABS", 0, 2)))
            q.drain("v")
            q.op("v", lambda e, S: e.tensor_scalar(
                sc("SC1", 0, 2), sc("SC1", 0, 2), 127.0, None, OP.mult))
            q.drain("v")
            q.op("v", lambda e, S: e.tensor_tensor(
                sc("A1", 0, 2), sc("R3", 0, 2), sc("SC1", 0, 2), OP.mult))
            q.drain("v")
            q.op("v", lambda e, S: e.tensor_tensor(
                sc("B1Q", 0, 2), sc("M3", 0, 2), sc("A1", 0, 2), OP.mult))
            q.drain("v")
            q.op("v", lambda e, S: e.tensor_scalar(
                sc("B1Q", 0, 2), sc("B1Q", 0, 2), RC, None, OP.add))
            if i == 0:
                q.wait("v", "dv", dv_w1)
            q.drain("v")
            V_q1sc[i] = q.op("v", lambda e, S: e.tensor_scalar(
                sc("G1", 0, 2), sc("MABS", 0, 2), wcons[:, 0:1], 1.0 / 127.0,
                OP.mult, OP.mult), sem="dv")
            yield

            q.wait("a", "dv", V_q1sc[i])
            q.wait("g", "dv", V_q1sc[i])
            if i > 0:
                q.wait("g", "pe", PE_T2[i - 1][1])
            if i == 0:
                q.wait("g", "swi", sw_w2)
            for s in range(NSUB):
                for h in range(2):
                    hi = 2 * s + h
                    if hi > 0:
                        q.wait("a", "gp", P_uq1[i][hi - 1])
                    a_an = q.op("a", lambda e, S, s=s, h=h, i=i: e.activation(
                        stgi[:, :], x2sh(i, s, h), AF.Identity,
                        scale=sc("A1", s), bias=sc("B1Q", s)), sem="ac")
                    q.wait("g", "ac", a_an)
                    q.drain("g")
                    bsl = stgb[:, 1024 * s + 512 * h:1024 * s + 512 * (h + 1)]
                    q.op("g", lambda e, S, bsl=bsl: e.tensor_scalar(
                        bsl, stgi[:, :],
                        onec[:, 0:1], rccn[:, 0:1], OP.mult, OP.add))
                    q.drain("g")
                    P_uq1[i][hi] = q.op("g", lambda e, S, s=s, bsl=bsl: e.tensor_scalar(
                        bsl, bsl,
                        sc("G1", s), zeroc[:, 0:1], OP.mult, OP.add), sem="gp")
                q.wait("t", "gp", P_uq1[i][2 * s + 1])
                q.wait("t", "ac", A_aoev[i][0] if s == 0 else A_evT2[i][0])
                for cc in range(8):
                    PE_T2[i][s] = q.op("t", lambda e, S, s=s, cc=cc: e.matmul(
                        pC[:, 128 * cc:128 * (cc + 1)],
                        stgb[:, 1024 * s + 128 * cc:1024 * s + 128 * (cc + 1)],
                        identb[:, :], start=True, stop=True),
                        sem="pe" if cc == 7 else None)
                q.wait("a", "pe", PE_T2[i][s])
                if s == 0 and i > 0:
                    q.wait("a", "pe", PE_w1[i - 1][31])
                if s == 0:
                    dst = xqv[:, par, :, 129:257]
                else:
                    dst = xqv[:, 1 - par, :, 1:129]
                q.op("a", lambda e, S, dst=dst: e.activation(
                    dst, pCv[:, :, :], AF.Identity), sem="ac")
                A_evT2[i][s] = q.cnt["ac"]
            q.wait("v", "ac", A_evT2[i][1])
            if i > 0:
                q.wait("v", "pe", PE_w1[i - 1][31])
            q.drain("v")
            q.op("v", lambda e, S, par=par: e.tensor_copy(
                xqv[:, 1 - par, :, 0], xqv[:, par, :, 256]))
            V_halo[i] = q.op("v", lambda e, S, par=par: e.tensor_copy(
                xqv[:, par, :, 257], xqv[:, 1 - par, :, 1]), sem="dv")
            if i == ntr - 1:
                q.drain("v")
                V_halo[i + 1] = q.op("v", lambda e, S: e.memset(
                    xqv[:, ntr % 2, :, 129:258], 0.0), sem="dv")

        def win_gen(f):
            pf = f % 2
            have_a = f > 0
            have_b = f < ntr
            subs = ([0] if have_a else []) + ([1] if have_b else [])
            emit_z8b(f - 1)

            def issue_w2t(c):
                if c >= 32:
                    return
                qn = "sy" if c % 2 == 0 else "g"
                if f == 0 and c < 2:
                    q.wait(qn, "swi", sw_w2)
                if c >= 2:
                    q.wait(qn, "pe", PE_w2c[f][c - 2])
                elif f > 0:
                    q.wait(qn, "pe", PE_w2c[f - 1][30 + c])
                swn = "sw%d" % (c % 2)
                SW_w2t[f][c] = (swn, q.op(qn, lambda e, S, c=c: e.dma_start(
                    out=w2sb[:, 1024 * (c % 2):1024 * (c % 2 + 1)],
                    in_=w2s[:, 1024 * c:1024 * (c + 1)]), sem=swn, k=16))

            issue_w2t(0)
            issue_w2t(1)

            q.wait("t", "dv", V_halo[f])
            if f == 0:
                q.wait("t", "dv", dv_w1)
                q.wait("t", "dv", dv_pad0)
            if f > 0:
                q.wait("t", "ac", A_ffevE[f - 1])
            for c in range(32):
                off = 512 * (c % 3)
                if c >= 3:
                    q.wait("t", "ac", A_evh[f][c - 3])
                for k in range(8):
                    PE_w1[f][c] = q.op("t",
                        lambda e, S, c=c, k=k, off=off, pf=pf: e.matmul(
                            pE[:, off:off + 258],
                            w1t[:, 4096 * k + 128 * c:4096 * k + 128 * (c + 1)],
                            xqv[:, pf, k, :],
                            start=(k == 0), stop=(k == 7)),
                        sem="pe" if k == 7 else None)
                q.wait("a", "pe", PE_w1[f][c])
                if c >= 5:
                    q.wait("a", "dv", V_cv[f][c - 5])
                q.op("a", lambda e, S, c=c, off=off: e.activation(
                    hpre[:, 258 * (c % 5):258 * (c % 5 + 1)], pE[:, off:off + 258],
                    AF.Identity), sem="ac")
                if c % 2 == 0:
                    if c == 0 and f > 0:
                        q.wait("a", "pe", PE_w2[f - 1])
                    q.op("a", lambda e, S, c=c, off=off: e.activation(
                        h4[:, 256 * c:256 * (c + 1)], pE[:, off + 1:off + 257],
                        AF.Identity, scale=cwt[:, 3 * c + 1:3 * c + 2],
                        bias=cbt[:, c:c + 1]), sem="ac")
                A_evh[f][c] = q.cnt["ac"]
                if c % 4 == 3:
                    if c >= 7:
                        cp = c - 4
                        q.wait("a", "dv", V_cv[f][cp])
                        A_sil[f][cp - 3] = A_sil[f][cp] = q.op(
                            "a", lambda e, S, cp=cp: e.activation(
                                h4[:, 256 * (cp - 3):256 * (cp + 1)],
                                h4[:, 256 * (cp - 3):256 * (cp + 1)], AF.Silu),
                            sem="ac")
                    q.wait("v", "ac", A_evh[f][c])
                    if c == 3 and f > 0:
                        q.wait("v", "pe", PE_w2[f - 1])
                    for cc in range(c - 2, c + 1, 2):
                        hs = hpre[:, 258 * (cc % 5):258 * (cc % 5 + 1)]
                        hd = h4[:, 256 * cc:256 * (cc + 1)]
                        q.op("v", lambda e, S, cc=cc, hs=hs, hd=hd: e.tensor_scalar(
                            hd, hs[:, 1:257], cwt[:, 3 * cc + 1:3 * cc + 2],
                            cbt[:, cc:cc + 1], OP.mult, OP.add))
                    q.drain("v")
                    for cc in range(c - 3, c + 1):
                        hs = hpre[:, 258 * (cc % 5):258 * (cc % 5 + 1)]
                        hd = h4[:, 256 * cc:256 * (cc + 1)]
                        q.op("v", lambda e, S, cc=cc, hs=hs, hd=hd:
                             e.scalar_tensor_tensor(
                                 hd, hs[:, 0:256], cwt[:, 3 * cc:3 * cc + 1], hd,
                                 OP.mult, OP.add))
                    q.drain("v")
                    for cc in range(c - 3, c + 1):
                        hs = hpre[:, 258 * (cc % 5):258 * (cc % 5 + 1)]
                        hd = h4[:, 256 * cc:256 * (cc + 1)]
                        q.op("v", lambda e, S, cc=cc, hs=hs, hd=hd:
                             e.scalar_tensor_tensor(
                                 hd, hs[:, 2:258], cwt[:, 3 * cc + 2:3 * cc + 3],
                                 hd, OP.mult, OP.add),
                             sem="dv" if cc == c else None)
                    for cc in range(c - 3, c + 1):
                        V_cv[f][cc] = q.cnt["dv"]
                    if c >= 7:
                        cp = c - 4
                        q.wait("v", "ac", A_sil[f][cp])
                        q.drain("v")
                        for hh in (0, 512):
                            gsl = h4[:, 256 * (cp - 3) + hh:256 * (cp - 3) + hh + 512]
                            if cp == 3 and hh == 0:
                                q.op("v", lambda e, S, gsl=gsl: e.tensor_copy(
                                    accP[:, :], gsl))
                                q.op("v", lambda e, S, gsl=gsl: e.tensor_copy(
                                    accN[:, :], gsl))
                                q.drain("v")
                            else:
                                q.op("v", lambda e, S, gsl=gsl: e.tensor_tensor(
                                    accP[:, :], accP[:, :], gsl, OP.max))
                                q.op("v", lambda e, S, gsl=gsl: e.tensor_tensor(
                                    accN[:, :], accN[:, :], gsl, OP.min))
                                q.drain("v")
                if c % 8 == 7:
                    yield

            q.wait("a", "dv", V_cv[f][31])
            A_sil[f][28] = A_sil[f][31] = q.op(
                "a", lambda e, S: e.activation(
                    h4[:, 256 * 28:256 * 32], h4[:, 256 * 28:256 * 32], AF.Silu),
                sem="ac")
            q.wait("v", "ac", A_sil[f][31])
            q.drain("v")
            for hh in (0, 512):
                gsl = h4[:, 256 * 28 + hh:256 * 28 + hh + 512]
                q.op("v", lambda e, S, gsl=gsl: e.tensor_tensor(
                    accP[:, :], accP[:, :], gsl, OP.max))
                q.op("v", lambda e, S, gsl=gsl: e.tensor_tensor(
                    accN[:, :], accN[:, :], gsl, OP.min))
                q.drain("v")
            q.op("v", lambda e, S: e.tensor_tensor(
                accP[:, 0:256], accP[:, 0:256], accP[:, 256:512], OP.max))
            q.op("v", lambda e, S: e.tensor_tensor(
                accN[:, 0:256], accN[:, 0:256], accN[:, 256:512], OP.min))
            q.drain("v")
            v_mab = q.op("v", lambda e, S: e.scalar_tensor_tensor(
                accF[:, :], accN[:, 0:256], -1.0, accP[:, 0:256], OP.mult, OP.max),
                sem="dv")
            q.wait("t", "dv", v_mab)
            if f + 1 < ntr:
                q.wait("t", "ac", A_u[f + 1])
            else:
                q.wait("t", "dv", V_bc2[f - 1])
            for s in range(2):
                pe_mabT = q.op("t", lambda e, S, s=s: e.matmul(
                    pB[:, 128 * s:128 * (s + 1)],
                    accF[:, 128 * s:128 * (s + 1)],
                    identb[:, :], start=True, stop=True),
                    sem="pe" if s == 1 else None)
            q.wait("v", "pe", pe_mabT)
            for s in range(2):
                q.op("v", lambda e, S, s=s: e.reduce_max(
                    sc("MAB2", s), pB[:, 128 * s:128 * (s + 1)], AX))
            q.drain("v")
            q.op("v", lambda e, S: e.tensor_scalar(
                sc("MAB2", 0, 2), sc("MAB2", 0, 2), 1e-5, None, OP.max))
            q.drain("v")
            q.op("v", lambda e, S: e.reciprocal(sc("SC2", 0, 2), sc("MAB2", 0, 2)))
            q.drain("v")
            q.op("v", lambda e, S: e.tensor_scalar(
                sc("SC2", 0, 2), sc("SC2", 0, 2), 127.0, None, OP.mult))
            if f == 0:
                q.wait("v", "dv", dv_w2)
            q.op("v", lambda e, S: e.tensor_scalar(
                sc("G2", 0, 2), sc("MAB2", 0, 2), wcons[:, 2:3], 1.0 / 127.0,
                OP.mult, OP.mult))
            q.drain("v")
            v_sc2 = q.op("v", lambda e, S: e.tensor_copy(sc("QC"), sc("SC2")),
                         sem="dv")
            q.wait("t", "dv", v_sc2)
            for s in range(2):
                pe_r = q.op("t", lambda e, S, s=s: e.matmul(
                    pB[0:1, 256 + 128 * s:256 + 128 * (s + 1)], sc("SC2", s),
                    identf[:, :], start=True, stop=True),
                    sem="pe" if s == 1 else None)
            q.wait("v", "pe", pe_r)
            v_row = q.op("v", lambda e, S: e.tensor_copy(
                s2row[0:1, :], pB[0:1, 256:512]), sem="dv")
            q.wait("t", "dv", v_row)
            pe_bc = q.op("t", lambda e, S: e.matmul(
                pB[:, 256:512], ones[0:1, 0:128], s2row[0:1, :],
                start=True, stop=True), sem="pe")
            q.wait("v", "pe", pe_bc)
            V_bc2[f] = q.op("v", lambda e, S: e.tensor_copy(bc2[:, :], pB[:, 256:512]),
                            sem="dv")
            yield

            q.wait("t", "dv", V_bc2[f])
            if f + 1 < ntr:
                q.wait("t", "ac", A_aoev[f + 1][1])
            elif f > 0:
                q.wait("t", "ac", A_ffevD[f - 1])
            q.wait("t", "ac", A_evh[f][31])
            def emit_pair(p2):
                c = 2 * p2
                if p2 == 0:
                    q.wait("v", "dv", V_bc2[f])
                    q.drain("v")
                vtt = 0
                for cc in (c, c + 1):
                    hsl4 = h4[:, 256 * cc:256 * (cc + 1)]
                    vtt = q.op("v", lambda e, S, hsl4=hsl4: e.tensor_tensor(
                        hsl4, hsl4, bc2[:, :], OP.mult),
                        sem="dv" if cc == c + 1 else None)
                q.wait("a", "dv", vtt)
                if p2 >= 2:
                    q.wait("a", "gp", P_uq2[f][p2 - 2])
                elif f > 0:
                    q.wait("a", "gp", P_uq2[f - 1][14 + p2])
                ksl = k2i[:, 512 * (p2 % 2):512 * (p2 % 2 + 1)]
                psl = h4[:, 512 * p2:512 * (p2 + 1)]
                A_anc[f][p2] = q.op("a", lambda e, S, ksl=ksl, psl=psl:
                                    e.activation(ksl, psl, AF.Identity,
                                                 bias=rcc[:, :]), sem="ac")
                q.wait("g", "ac", A_anc[f][p2])
                P_uq2[f][p2] = q.op("g", lambda e, S, ksl=ksl, psl=psl:
                                    e.tensor_scalar(psl, ksl, onec[:, 0:1],
                                                    rccn[:, 0:1], OP.mult, OP.add),
                                    sem="gp")

            for p2 in range(16):
                emit_pair(p2)
            for c in range(32):
                q.wait("t", "gp", P_uq2[f][c // 2])
                q.wait("t", *SW_w2t[f][c])
                for s in subs:
                    bank = pD if s == 0 else pE
                    for jj in range(2):
                        coff = 1024 * (c % 2) + 512 * jj
                        mm = q.op("t", lambda e, S, c=c, s=s, jj=jj, bank=bank,
                                  coff=coff: e.matmul(
                                      bank[:, 512 * jj:512 * (jj + 1)],
                                      h4[:, 256 * c + 128 * s:256 * c + 128 * (s + 1)],
                                      w2sb[:, coff:coff + 512],
                                      start=(c == 0), stop=(c == 31)),
                                  sem="pe" if (s == subs[-1] and jj == 1) else None)
                PE_w2c[f][c] = mm
                issue_w2t(c + 2)
                if c == 11:
                    yield
            PE_w2[f] = PE_w2c[f][31]
            yield

            q.wait("a", "pe", PE_w2[f])
            if SO_cnt[f - 1]:
                q.wait("a", "so", SO_cnt[f - 1])
            if have_b:
                q.wait("a", "dv", P_x2[f][1])
                A_ffevE[f] = q.op("a", lambda e, S: e.activation(
                    stgF[:, 1024:2048], pE[:, 0:1024], AF.Identity,
                    accum_out=sc("SUMF", 1)), sem="ac")
            if have_a:
                A_ffevD[f] = q.op("a", lambda e, S: e.activation(
                    stgF[:, 0:1024], pD[:, 0:1024], AF.Identity,
                    accum_out=sc("SUMF", 0)), sem="ac")
            q.drain("a")
            q.wait("a", "dv", P_x2[min(f + 1, ntr - 1)][0])
            for s in subs:
                src_ = stgF[:, 1024 * s:1024 * (s + 1)]
                q.op("a", lambda e, S, s=s, src_=src_: e.activation(
                    stgA[:, :], src_, AF.Square, accum_out=sc("SSQF", s)), sem="ac")
                q.drain("a")
            a_sqf = q.cnt["ac"]
            q.wait("v", "ac", a_sqf)
            V_chF[f] = stat_chain("v", "F", subs[0], len(subs), var_scale_col="G2")

        def emit_z8b(f):
            if f < 0:
                return
            have_a = f > 0
            have_b = f < ntr
            subs = ([0] if have_a else []) + ([1] if have_b else [])
            q.wait("a", "dv", V_chF[f])
            a_apF = 0
            for s in subs:
                dst = stgF[:, 1024 * s:1024 * (s + 1)]
                a_apF = q.op("a", lambda e, S, s=s, dst=dst: e.activation(
                    dst, dst, AF.Identity, bias=sc("BF", s), scale=sc("RF", s)),
                    sem="ac")
            q.wait("g", "ac", a_apF)
            if have_a:
                q.wait("g", "dv", P_x2[f - 1][1])
            if have_b:
                q.wait("g", "dv", P_x2[f][0])
            q.drain("g")
            for s in subs:
                dst = stgF[:, 1024 * s:1024 * (s + 1)]
                x2src = x2s(f - 1, 1) if s == 0 else x2s(f, 0)
                P_out[f][s] = q.op("g", lambda e, S, dst=dst, x2src=x2src:
                                   e.tensor_tensor(dst, x2src, dst, OP.add),
                                   sem="gp")
                rows = ((f - 1) * TT + 128) if s == 0 else f * TT
                q.wait("sy", "gp", P_out[f][s])
                dstH = out_hbm[rows:rows + 128, :]
                SO_cnt[f] = q.op("sy", lambda e, S, dstH=dstH, dst=dst: e.dma_start(
                    out=dstH, in_=dst), sem="so", k=16)

        def run_interleaved(g1, g2):
            gens = [g1, g2]
            while gens:
                for g in list(gens):
                    try:
                        next(g)
                    except StopIteration:
                        gens.remove(g)

        for i in range(ntr):
            if i > 0:
                run_interleaved(attn_gen(i), win_gen(i - 1))
            else:
                for _ in attn_gen(0):
                    pass
        if ntr > 0:
            for _ in win_gen(ntr - 1):
                pass
            for _ in win_gen(ntr):
                pass
            emit_z8b(ntr)
        for _s, _v in q.cnt.items():
            q.wait("sy", _s, _v)

        with nc.Block() as block:
            @block.sync
            def _(eng):
                for fn in q.q["sy"]:
                    fn(eng, sems)

            @block.vector
            def _(eng):
                for fn in q.q["v"]:
                    fn(eng, sems)

            @block.scalar
            def _(eng):
                for fn in q.q["a"]:
                    fn(eng, sems)

            @block.tensor
            def _(eng):
                for fn in q.q["t"]:
                    fn(eng, sems)

            @block.gpsimd
            def _(eng):
                for fn in q.q["g"]:
                    fn(eng, sems)

    return nc


def _host_layout(inputs):
    f32 = np.float32

    def fm(w, nchunk):
        i, o = w.shape
        return np.ascontiguousarray(
            w.reshape(nchunk, 128, o).transpose(1, 0, 2).reshape(128, nchunk * o))

    W_in = np.asarray(inputs["L0_attn_W_in"], f32)
    W_sel = np.asarray(inputs["L0_attn_W_sel_in"], f32)
    W_so = np.asarray(inputs["L0_attn_W_sel_out"], f32)
    W_out = np.asarray(inputs["L0_attn_W_out"], f32)
    w1 = np.asarray(inputs["L0_ffn_w1"], f32)
    w2 = np.asarray(inputs["L0_ffn_w2"], f32)
    d_param = np.asarray(inputs["L0_attn_d_param"], f32)
    cw = np.asarray(inputs["L0_ffn_conv_w"], f32)[:, 0, :]
    cb = np.asarray(inputs["L0_ffn_conv_b"], f32)
    return {
        "winT": fm(np.ascontiguousarray(W_in.T), 8),
        "wselT": fm(np.ascontiguousarray(W_sel.T), 8),
        "wsoT": fm(np.ascontiguousarray(W_so.T), 4),
        "woutT": fm(np.ascontiguousarray(W_out.T), 2),
        "w1T": fm(np.ascontiguousarray(w1.T), 8),
        "w2T": fm(np.ascontiguousarray(w2.T), 32),
        "dp": np.ascontiguousarray(d_param.reshape(2, 128).T),
        "cw": np.ascontiguousarray(cw.reshape(32, 128, 3).transpose(1, 0, 2).reshape(128, 96)),
        "cb": np.ascontiguousarray(cb.reshape(32, 128).T),
    }


def kernel(**inputs):
    x = np.ascontiguousarray(np.asarray(inputs["L0_attn_x"], np.float32))
    wd = _host_layout(inputs)
    nc = build_program()
    in_maps = []
    for c in range(8):
        m = {"x": np.ascontiguousarray(x[c])}
        m.update(wd)
        in_maps.append(m)
    res = run_bass_kernel_spmd(nc, in_maps, core_ids=list(range(8)))
    return np.stack([np.asarray(r["out"]) for r in res.results], axis=0)


# version 25
# speedup vs baseline: 1.1333x; 1.1333x over previous
import numpy as np
import concourse.bass as bass
import concourse.mybir as mybir
from concourse.bass_utils import run_bass_kernel_spmd
from concourse.alu_op_type import AluOpType as OP

F32 = mybir.dt.float32
BF16 = mybir.dt.bfloat16
I32 = mybir.dt.int32
AF = mybir.ActivationFunctionType
AX = mybir.AxisListType.X

D = 1024
T_TOK = 4096
TT = 256
NT = T_TOK // TT
NSUB = 2
HD = 4096
NST = 256
SEL = 512
EPS = 1e-5
QMAGIC = 1597463007.0
RC = 12582912.0

_names = ("SUM1 SSQ1 M1 E1 R1 B1 SUM2 SSQ2 M2 E2 R2 B2 "
          "SUM3 SSQ3 M3 E3 R3 MX MN TA TB MABS SC1 A1 B1Q G1 "
          "SUMF SSQF MF EF RF BF GG MAB2 SC2 G2 QA QB QC PA PB PC").split()
C = {n: 2 * k for k, n in enumerate(_names)}
STT_COLS = 2 * len(_names)


class Q:
    ENGS = ("sy", "v", "a", "t", "g")

    def __init__(self):
        self.q = {e: [] for e in self.ENGS}
        self.cnt = {s: 0 for s in ("sm", "sx0", "sx1", "so", "sw0", "sw1",
                                   "swi", "gd", "gp", "pe", "dv", "ac",
                                   "ad", "vd", "sw2")}

    def do(self, qn, f):
        self.q[qn].append(f)

    def op(self, qn, f, sem=None, k=1):
        if sem is None:
            self.q[qn].append(f)
            return None
        self.cnt[sem] += k
        self.q[qn].append(lambda e, S, f=f, sem=sem, k=k: f(e, S).then_inc(S[sem], k))
        return self.cnt[sem]

    def wait(self, qn, sem, v):
        if not v:
            return
        self.q[qn].append(lambda e, S, sem=sem, v=v: e.wait_ge(S[sem], v))

    def drain(self, qn):
        self.q[qn].append(lambda e, S: e.drain())


def build_program(ntr=NT):
    nc = bass.Bass("TRN2", target_bir_lowering=False)

    x_hbm = nc.dram_tensor("x", [T_TOK, D], F32, kind="ExternalInput").ap()
    winT_h = nc.dram_tensor("winT", [128, 8 * NST], F32, kind="ExternalInput").ap()
    wselT_h = nc.dram_tensor("wselT", [128, 8 * SEL], F32, kind="ExternalInput").ap()
    wsoT_h = nc.dram_tensor("wsoT", [128, 4 * D], F32, kind="ExternalInput").ap()
    woutT_h = nc.dram_tensor("woutT", [128, 2 * D], F32, kind="ExternalInput").ap()
    w1T_h = nc.dram_tensor("w1T", [128, 8 * HD], F32, kind="ExternalInput").ap()
    w2T_h = nc.dram_tensor("w2T", [128, 32 * D], F32, kind="ExternalInput").ap()
    dp_h = nc.dram_tensor("dp", [128, 2], F32, kind="ExternalInput").ap()
    cw_h = nc.dram_tensor("cw", [128, 96], F32, kind="ExternalInput").ap()
    cb_h = nc.dram_tensor("cb", [128, 32], F32, kind="ExternalInput").ap()
    out_hbm = nc.dram_tensor("out", [T_TOK, D], F32, kind="ExternalOutput").ap()
    w2s = nc.dram_tensor("w2s", [128, 32 * D], BF16, kind="Internal").ap()

    q = Q()
    from contextlib import ExitStack
    st_ = ExitStack()
    sb = lambda name, shape, dt: st_.enter_context(nc.sbuf_tensor(name, shape, dt))
    ps_ = lambda name, shape, dt: st_.enter_context(nc.psum_tensor(name, shape, dt))

    with st_:
        identf = sb("identf", [128, 128], F32)
        identb = sb("identb", [128, 128], BF16)
        ones = sb("ones", [128, 128], F32)
        winT = sb("winT_b", [128, 8 * NST], BF16)
        wselT = sb("wselT_b", [128, 8 * SEL], BF16)
        wsoT = sb("wsoT_b", [128, 4 * D], BF16)
        woutT = sb("woutT_b", [128, 2 * D], BF16)
        w1t = sb("w1t", [128, 8 * HD], BF16)
        dp2 = sb("dp2", [128, 2], F32)
        cwt = sb("cwt", [128, 96], F32)
        cbt = sb("cbt", [128, 32], F32)
        wcons = sb("wcons", [128, 4], F32)
        stgA = sb("stgA", [128, 1024], F32)
        stgF = sb("stgF", [128, 2048], F32)
        stgi = sb("stgi", [128, 512], I32)
        stgb = sb("stgb", [128, 2048], BF16)
        xin = sb("xin", [128, 2 * 2048], F32)
        x2 = sb("x2", [128, 5 * 1024], F32)
        xin1tm = sb("xin1tm", [128, 2048], BF16)
        xin1fm = sb("xin1fm", [128, 2048], BF16)
        xqw = sb("xqw", [128, 2 * 8 * 258], BF16)
        u_sb = sb("u_sb", [128, 512], BF16)
        tad = sb("tad", [128, 1024], F32)
        tbc = sb("tbc", [128, 1024], F32)
        hsc = sb("hsc", [128, 2 * 512], F32)
        ysb = sb("ysb", [128, 512], BF16)
        selh = sb("selh", [128, 1024], BF16)
        h4 = sb("h4", [128, 32 * 256], BF16)
        hpre = sb("hpre", [128, 4 * 258], BF16)
        k2i = sb("k2i", [128, 2 * 512], I32)
        bc2 = sb("bc2", [128, 256], BF16)
        accP = sb("accP", [128, 256], BF16)
        accN = sb("accN", [128, 256], BF16)
        w2sb = sb("w2sb", [128, 3072], BF16)
        stt = sb("stt", [128, STT_COLS], F32)
        wtot = stt.ap()[0:1, C["MX"]:C["MX"] + 4]
        seedib = sb("seedib", [128, 4], I32)
        seedi = seedib.ap()[:, 0:2]
        seedp = seedib.ap()[:, 2:4]
        consts = sb("consts", [128, 8], F32)
        rcc = consts.ap()[:, 0:1]
        rccn = consts.ap()[:, 1:2]
        onec = consts.ap()[:, 2:3]
        halfc = consts.ap()[:, 3:4]
        zeroc = consts.ap()[:, 4:5]

        pB = ps_("pB", [128, 512], F32)
        pC = ps_("pC", [128, 1024], F32)
        pD = ps_("pD", [128, 1024], F32)
        pE = ps_("pE", [128, 1536], F32)

        sems = {}
        for s in ("sm", "sx0", "sx1", "so", "sw0", "sw1", "swi", "gd", "gp",
                  "pe", "dv", "ac", "ad", "vd", "sw2"):
            sems[s] = st_.enter_context(nc.semaphore(s))

        xinv = xin.ap().rearrange("p (q u d) -> p q u d", q=2, u=2)
        fm1v = xin1fm.ap().rearrange("p (c u w) -> p c u w", c=8, u=2)
        pCv = pC.ap().rearrange("p (c w) -> p c w", c=8)
        xqv = xqw.ap().rearrange("p (b c w) -> p b c w", b=2, c=8)

        def sc(col, s=0, n=1):
            return stt[:, C[col] + s:C[col] + s + n]

        def x2slot(i, s):
            return (i % 2) if s == 0 else 2 + (i % 3)

        def x2s(i, s):
            slot = x2slot(i, s)
            return x2[:, 1024 * slot:1024 * (slot + 1)]

        def x2sh(i, s, h):
            slot = x2slot(i, s)
            return x2[:, 1024 * slot + 512 * h:1024 * slot + 512 * (h + 1)]

        q.do("g", lambda e, S: e.memset(ones[:, :], 1.0))
        q.do("g", lambda e, S: e.memset(rcc, RC))
        q.do("g", lambda e, S: e.memset(rccn, -RC))
        q.do("g", lambda e, S: e.memset(onec, 1.0))
        q.do("g", lambda e, S: e.memset(halfc, 0.5))
        q.do("g", lambda e, S: e.memset(zeroc, 0.0))
        q.drain("g")
        q.op("g", lambda e, S: e.affine_select(
            identf[:, :], ones[:, :], [[1, 128]], OP.is_equal, 0.0,
            base=0, channel_multiplier=-1), sem="gp")
        gp_id = q.cnt["gp"]
        q.wait("v", "gp", gp_id)
        dv_id = q.op("v", lambda e, S: e.tensor_copy(identb[:, :], identf[:, :]),
                     sem="dv")
        dv_pad0 = q.op("v", lambda e, S: e.memset(xqv[:, 0, :, 0:129], 0.0),
                       sem="dv")

        for dst, src in ((dp2, dp_h), (cwt, cw_h), (cbt, cb_h)):
            q.op("sy", lambda e, S, dst=dst, src=src: e.dma_start(
                out=dst[:, :], in_=src[:, :]), sem="sm", k=16)
        si_small = q.cnt["sm"]
        q.wait("v", "sm", si_small)
        q.drain("v")
        dv_dp2 = q.op("v", lambda e, S: e.tensor_scalar_mul(dp2[:, :], dp2[:, :], 0.5),
                      sem="dv")

        si_x = [None] * (NT + 2)

        def issue_x_dma(i, wait_gp=0, wait_ac=0):
            if i >= ntr:
                return
            par = i % 2
            src = x_hbm[i * TT:(i + 1) * TT, :].rearrange("(s p) d -> p s d", p=128)
            dst = xinv[:, par, :, :]
            q.wait("sy", "dv", wait_gp)
            q.wait("sy", "ac", wait_ac)
            sxn = "sx%d" % (i % 2)
            si_x[i] = (sxn, q.op("sy", lambda e, S, src=src, dst=dst:
                                 e.dma_start(out=dst, in_=src), sem=sxn, k=16))

        issue_x_dma(0)
        issue_x_dma(1)

        stg_gate = {0: None, 1: None}
        last_dma = {"gd": q.cnt["gd"], "sm": q.cnt["sm"]}

        def piece_dma(p, src_h, gate):
            qn, sem = ("g", "gd") if p % 2 == 0 else ("sy", "sm")
            if gate is None:
                gate = stg_gate[p % 2]
            if gate:
                q.wait(qn, gate[0], gate[1])
            q.wait(qn, sem, last_dma[sem])
            v = q.op(qn, lambda e, S, p=p, src_h=src_h: e.dma_start(
                out=stgF[:, 1024 * (p % 2):1024 * (p % 2 + 1)],
                in_=src_h[:, 1024 * p:1024 * (p + 1)]), sem=sem, k=16)
            last_dma[sem] = v
            return (sem, v)

        def stage_cast(dst, src_h, ncols):
            npc = (ncols + 1023) // 1024
            dmas = [None] * (npc + 2)
            cons = [0] * (npc + 2)
            dmas[0] = piece_dma(0, src_h, None)
            if npc > 1:
                dmas[1] = piece_dma(1, src_h, None)
            for p in range(npc):
                w = min(1024, ncols - 1024 * p)
                q.wait("v", *dmas[p])
                cons[p] = q.op("v", lambda e, S, dst=dst, p=p, w=w: e.tensor_copy(
                    dst[:, 1024 * p:1024 * p + w],
                    stgF[:, 1024 * (p % 2):1024 * (p % 2) + w]), sem="dv")
                stg_gate[p % 2] = ("dv", cons[p])
                if p + 2 < npc:
                    dmas[p + 2] = piece_dma(p + 2, src_h, ("dv", cons[p]))
            return q.cnt["dv"]

        stage_cast(winT, winT_h, 8 * NST)
        stage_cast(wselT, wselT_h, 8 * SEL)
        stage_cast(wsoT, wsoT_h, 4 * D)
        dv_attn_w = stage_cast(woutT, woutT_h, 2 * D)

        last_dma.update({"ad": 0, "vd": 0})

        class WP:
            pass

        c1 = WP()
        c1.src, c1.npc = w1T_h, 32
        c1.queues = (("g", "gd"), ("sy", "sm"))
        c1.slot = lambda j: stgF[:, 1024 * j:1024 * (j + 1)]
        c1.acc, c1.accs, c1.vs = "QA", "QB", "QC"
        c1.junk = xin1tm[:, 0:1024]
        c1.mean_col, c1.scale_col = 0, 1
        c1.gate = [stg_gate[0], stg_gate[1]]
        c2 = WP()
        c2.src, c2.npc = w2T_h, 32
        c2.queues = (("a", "ad"), ("a", "ad"))
        c2.slot = lambda j: (tad if j == 0 else tbc)[:, 0:1024]
        c2.acc, c2.accs, c2.vs = "PA", "PB", "PC"
        c2.junk = xin1fm[:, 0:1024]
        c2.mean_col, c2.scale_col = 2, 3
        c2.gate = [None, None]

        def wp_dma(ctx, p, gate):
            qn, sem = ctx.queues[p % 2]
            if gate is None:
                gate = ctx.gate[p % 2]
            if gate:
                q.wait(qn, gate[0], gate[1])
            q.wait(qn, sem, last_dma[sem])
            v = q.op(qn, lambda e, S, p=p, ctx=ctx: e.dma_start(
                out=ctx.slot(p % 2), in_=ctx.src[:, 1024 * p:1024 * (p + 1)]),
                sem=sem, k=16)
            last_dma[sem] = v
            return (sem, v)

        for ctx in (c1, c2):
            ctx.dmas = [None] * (ctx.npc + 2)
            ctx.addm = [0] * (ctx.npc + 2)
            ctx.acm = [0] * (ctx.npc + 2)
            ctx.dmas[0] = wp_dma(ctx, 0, None)
            ctx.dmas[1] = wp_dma(ctx, 1, None)
        for p in range(32):
            for ctx in (c1, c2):
                q.wait("a", *ctx.dmas[p])
                if p >= 2:
                    q.wait("a", "dv", ctx.addm[p - 2])
                q.drain("a")
                ctx.acm[p] = q.op("a", lambda e, S, p=p, ctx=ctx: e.activation(
                    ctx.junk, ctx.slot(p % 2), AF.Abs,
                    accum_out=sc(ctx.acc, p % 2)), sem="ac")
                ctx.gate[p % 2] = ("ac", ctx.acm[p])
                if p + 2 < ctx.npc:
                    ctx.dmas[p + 2] = wp_dma(ctx, p + 2, ("ac", ctx.acm[p]))
                q.wait("v", "ac", ctx.acm[p])
                if p == 0:
                    ctx.addm[p] = q.op("v", lambda e, S, ctx=ctx: e.tensor_copy(
                        sc(ctx.accs), sc(ctx.acc, 0)), sem="dv")
                else:
                    q.drain("v")
                    ctx.addm[p] = q.op("v", lambda e, S, p=p, ctx=ctx: e.tensor_tensor(
                        sc(ctx.accs), sc(ctx.accs), sc(ctx.acc, p % 2), OP.add),
                        sem="dv")
        for ctx in (c1, c2):
            q.drain("v")
            vsum = q.op("v", lambda e, S, ctx=ctx: e.tensor_copy(
                sc(ctx.vs), sc(ctx.accs)), sem="dv")
            q.wait("t", "dv", vsum)
            mm1 = q.op("t", lambda e, S, ctx=ctx: e.matmul(
                pC[0:1, 0:1], ones[:, 0:1], sc(ctx.vs), start=True, stop=True),
                sem="pe")
            q.wait("v", "pe", mm1)
            q.op("v", lambda e, S, ctx=ctx: e.tensor_scalar(
                wtot[0:1, ctx.mean_col:ctx.mean_col + 1], pC[0:1, 0:1],
                1.0 / (128.0 * 1024 * ctx.npc), 1e-5, OP.mult, OP.max))
            q.drain("v")
            vrec = q.op("v", lambda e, S, ctx=ctx: e.reciprocal(
                wtot[0:1, ctx.scale_col:ctx.scale_col + 1],
                wtot[0:1, ctx.mean_col:ctx.mean_col + 1]), sem="dv")
            q.drain("v")
            q.wait("t", "dv", vrec)
            mm2 = q.op("t", lambda e, S, ctx=ctx: e.matmul(
                pC[:, 0:2], ones[0:1, 0:128],
                wtot[0:1, ctx.mean_col:ctx.mean_col + 2],
                start=True, stop=True), sem="pe")
            q.wait("v", "pe", mm2)
            ctx.vbc = q.op("v", lambda e, S, ctx=ctx: e.tensor_copy(
                wcons[:, ctx.mean_col:ctx.mean_col + 2], pC[:, 0:2]), sem="dv")
            q.drain("v")

        for ctx in (c1, c2):
            ctx.dmas = [None] * (ctx.npc + 2)
            ctx.clip = [0] * (ctx.npc + 2)
            ctx.dmas[0] = wp_dma(ctx, 0, None)
            ctx.dmas[1] = wp_dma(ctx, 1, None)
        last_sw = [q.cnt["swi"], q.cnt["swi"]]
        for p in range(32):
            ctx = c1
            q.wait("a", *ctx.dmas[p])
            if p == 0:
                q.wait("a", "dv", ctx.vbc)
            if p >= 1:
                q.wait("a", "dv", ctx.clip[p - 1])
            a_an = q.op("a", lambda e, S, p=p, ctx=ctx: e.activation(
                k2i[:, 0:1024], ctx.slot(p % 2), AF.Identity,
                scale=wcons[:, ctx.scale_col:ctx.scale_col + 1],
                bias=rcc), sem="ac")
            ctx.gate[p % 2] = ("ac", a_an)
            if p + 2 < ctx.npc:
                ctx.dmas[p + 2] = wp_dma(ctx, p + 2, ("ac", a_an))
            q.wait("v", "ac", a_an)
            dstb = w1t[:, 1024 * p:1024 * (p + 1)]
            q.drain("v")
            q.op("v", lambda e, S, dstb=dstb: e.tensor_scalar(
                dstb, k2i[:, 0:1024], -RC, -1.0, OP.add, OP.max))
            q.drain("v")
            ctx.clip[p] = q.op("v", lambda e, S, dstb=dstb: e.tensor_scalar(
                dstb, dstb, 1.0, None, OP.min), sem="dv")
            ctx = c2
            q.wait("a", *ctx.dmas[p])
            if p == 0:
                q.wait("a", "dv", ctx.vbc)
            dstb = stgb[:, 0:1024] if p % 2 == 0 else selh[:, 0:1024]
            lastc = 0
            for h in range(2):
                if p >= 1 or h == 1:
                    q.wait("a", "dv", ctx.clip[p - 1] if h == 0 else lastc)
                a_an = q.op("a", lambda e, S, p=p, h=h, ctx=ctx: e.activation(
                    stgi[:, :], ctx.slot(p % 2)[:, 512 * h:512 * (h + 1)],
                    AF.Identity,
                    scale=wcons[:, ctx.scale_col:ctx.scale_col + 1],
                    bias=rcc), sem="ac")
                if h == 1:
                    ctx.gate[p % 2] = ("ac", a_an)
                    if p + 2 < ctx.npc:
                        ctx.dmas[p + 2] = wp_dma(ctx, p + 2, ("ac", a_an))
                q.wait("v", "ac", a_an)
                if h == 0 and p >= 2:
                    q.wait("v", "swi", last_sw[p % 2])
                dsl = dstb[:, 512 * h:512 * (h + 1)]
                q.drain("v")
                q.op("v", lambda e, S, dsl=dsl: e.tensor_scalar(
                    dsl, stgi[:, :], -RC, -1.0, OP.add, OP.max))
                q.drain("v")
                lastc = q.op("v", lambda e, S, dsl=dsl: e.tensor_scalar(
                    dsl, dsl, 1.0, None, OP.min), sem="dv")
            ctx.clip[p] = lastc
            q.wait("a", "dv", ctx.clip[p])
            q.wait("a", "swi", max(last_sw))
            last_sw[p % 2] = q.op("a", lambda e, S, p=p, dstb=dstb: e.dma_start(
                out=w2s[:, 1024 * p:1024 * (p + 1)], in_=dstb),
                sem="swi", k=16)
        dv_w1 = c1.clip[31]
        dv_w2 = c2.vbc
        sw_w2 = last_sw[1] if last_sw[1] > last_sw[0] else last_sw[0]

        def quake(qn, dst_ap, src_ap, seed_t, n, pre):
            qa, qb, qc = (sc(pre[0], 0, n), sc(pre[1], 0, n), sc(pre[2], 0, n))
            q.op(qn, lambda e, S: e.tensor_copy(qa, src_ap.bitcast(I32)))
            q.drain(qn)
            q.op(qn, lambda e, S: e.tensor_scalar(
                seed_t[:, 0:n], qa, -0.5, QMAGIC, OP.mult, OP.add))
            q.drain(qn)
            cur = seed_t[:, 0:n].bitcast(F32)
            for it in range(3):
                tgt = dst_ap if it == 2 else qa
                q.op(qn, lambda e, S, cur=cur: e.tensor_tensor(qb, cur, cur, OP.mult))
                q.drain(qn)
                q.op(qn, lambda e, S: e.tensor_tensor(qc, src_ap, qb, OP.mult))
                q.drain(qn)
                q.op(qn, lambda e, S: e.tensor_scalar(qc, qc, -0.5, 1.5, OP.mult, OP.add))
                q.drain(qn)
                q.op(qn, lambda e, S, cur=cur, tgt=tgt: e.tensor_tensor(
                    tgt, cur, qc, OP.mult))
                q.drain(qn)
                cur = qa

        def stat_chain(qn, pre, n0, nn, var_scale_col=None):
            scr = ("PA", "PB", "PC") if qn == "g" else ("QA", "QB", "QC")
            seed_t = seedp if qn == "g" else seedi
            sem = "gp" if qn == "g" else "dv"
            Mc = sc("M" + pre, n0, nn)
            Ec = sc("E" + pre, n0, nn)
            q.drain(qn)
            q.op(qn, lambda e, S: e.tensor_scalar(
                Mc, sc("SUM" + pre, n0, nn), -1.0 / D, None, OP.mult))
            q.op(qn, lambda e, S: e.tensor_scalar(
                Ec, sc("SSQ" + pre, n0, nn), 1.0 / D, None, OP.mult))
            q.drain(qn)
            q.op(qn, lambda e, S: e.tensor_tensor(
                sc(scr[1], n0, nn), Mc, Mc, OP.mult))
            q.drain(qn)
            q.op(qn, lambda e, S: e.tensor_tensor(Ec, Ec, sc(scr[1], n0, nn),
                                                  OP.subtract))
            q.drain(qn)
            if var_scale_col is not None:
                q.op(qn, lambda e, S: e.tensor_tensor(
                    sc("GG", n0, nn), sc(var_scale_col, n0, nn),
                    sc(var_scale_col, n0, nn), OP.mult))
                q.drain(qn)
                q.op(qn, lambda e, S: e.tensor_tensor(
                    Ec, Ec, sc("GG", n0, nn), OP.mult))
                q.drain(qn)
            q.op(qn, lambda e, S: e.tensor_scalar(Ec, Ec, EPS, None, OP.add))
            q.drain(qn)
            quake(qn, sc("R" + pre, n0, nn), Ec, seed_t, nn, scr)
            q.drain(qn)
            if var_scale_col is not None:
                q.op(qn, lambda e, S: e.tensor_tensor(
                    sc("R" + pre, n0, nn), sc("R" + pre, n0, nn),
                    sc(var_scale_col, n0, nn), OP.mult))
                q.drain(qn)
            q.op(qn, lambda e, S: e.tensor_tensor(
                sc("B" + pre, n0, nn), Mc, sc("R" + pre, n0, nn), OP.mult))
            return q.op(qn, lambda e, S: e.tensor_copy(
                sc(scr[2], 0, 1), sc("M" + pre, n0, 1)), sem=sem)

        Z = lambda: [0] * (NT + 2)
        Z2 = lambda: [[0, 0] for _ in range(NT + 2)]
        PE_T1, A_evT1, PE_u, PE_sel = Z2(), Z(), Z(), Z()
        PE_g1, PE_g2, A_u, A_selh, A_tad, A_tbc, A_aff, V_y = (
            Z(), Z(), Z(), Z(), Z(), Z(), Z(), Z())
        PE_ao, A_aoev, A_t, P_x2 = Z2(), Z2(), Z2(), Z2()
        PE_T2, A_evT2 = Z2(), Z2()
        A_xin1, V_q1sc, V_halo = Z(), Z(), Z()
        P_uq1 = [[0] * 4 for _ in range(NT + 2)]
        PE_w1 = [[0] * 32 for _ in range(NT + 2)]
        A_evh = [[0] * 32 for _ in range(NT + 2)]
        V_cv = [[0] * 32 for _ in range(NT + 2)]
        A_sil = [[0] * 32 for _ in range(NT + 2)]
        A_anc = [[0] * 16 for _ in range(NT + 2)]
        P_uq2 = [[0] * 16 for _ in range(NT + 2)]
        PE_w2c = [[0] * 32 for _ in range(NT + 2)]
        SW_w2t = [[None] * 32 for _ in range(NT + 2)]
        PE_w2, V_bc2, A_ffevD, A_ffevE, V_chF = Z(), Z(), Z(), Z(), Z()
        P_out, SO_cnt = Z2(), Z()

        def emit_ln1_prep(i, wait_dv=0):
            if i >= ntr:
                return
            q.wait("a", *si_x[i])
            q.wait("a", "dv", wait_dv)
            jw = PE_T1[i - 1][1] if i > 0 else 0
            if jw:
                q.wait("a", "pe", jw)
            q.drain("a")
            for s in range(NSUB):
                q.op("a", lambda e, S, s=s, i=i: e.activation(
                    xin1tm[:, 1024 * s:1024 * (s + 1)], xinv[:, i % 2, s, :],
                    AF.Identity, accum_out=sc("SUM1", s)), sem="ac")
            q.drain("a")
            for s in range(NSUB):
                q.op("a", lambda e, S, s=s, i=i: e.activation(
                    xin1tm[:, 1024 * s:1024 * (s + 1)], xinv[:, i % 2, s, :],
                    AF.Square, accum_out=sc("SSQ1", s)), sem="ac")
            a_sums = q.cnt["ac"]
            q.wait("v", "ac", a_sums)
            if i == 0:
                q.wait("v", "dv", dv_dp2)
            g_ch = stat_chain("v", "1", 0, 2)
            q.wait("a", "dv", g_ch)
            for s in range(NSUB):
                q.op("a", lambda e, S, s=s, i=i: e.activation(
                    xin1tm[:, 1024 * s:1024 * (s + 1)], xinv[:, i % 2, s, :],
                    AF.Identity, bias=sc("B1", s), scale=sc("R1", s)), sem="ac")
            A_xin1[i] = q.cnt["ac"]

        def attn_gen(i):
            par = i % 2

            if i == 0:
                emit_ln1_prep(0)
                q.wait("t", "dv", dv_id)
                q.wait("t", "dv", dv_attn_w)
            q.wait("t", "ac", A_xin1[i])
            if i > 0:
                q.wait("t", "ac", A_evT2[i - 1][1])
            ev0 = [0]
            for s in range(NSUB):
                if s == 1:
                    q.wait("t", "ac", ev0[0])
                for cc in range(8):
                    PE_T1[i][s] = q.op("t", lambda e, S, s=s, cc=cc: e.matmul(
                        pC[:, 128 * cc:128 * (cc + 1)],
                        xin1tm[:, 1024 * s + 128 * cc:1024 * s + 128 * (cc + 1)],
                        identb[:, :], start=True, stop=True),
                        sem="pe" if cc == 7 else None)
                q.wait("a", "pe", PE_T1[i][s])
                if i > 0 and s == 0:
                    q.wait("a", "pe", PE_sel[i - 1])
                ev = q.op("a", lambda e, S, s=s: e.activation(
                    fm1v[:, :, s, :], pCv[:, :, :], AF.Identity), sem="ac")
                if s == 0:
                    ev0[0] = ev
            A_evT1[i] = q.cnt["ac"]
            yield

            q.wait("t", "ac", A_evT1[i])
            if i > 1:
                q.wait("t", "dv", V_bc2[i - 2])
            for n in range(2):
                for k in range(8):
                    PE_u[i] = q.op("t", lambda e, S, n=n, k=k: e.matmul(
                        pB[:, 256 * n:256 * (n + 1)],
                        winT[:, 256 * k + 128 * n:256 * k + 128 * (n + 1)],
                        xin1fm[:, 256 * k:256 * (k + 1)],
                        start=(k == 0), stop=(k == 7)),
                        sem="pe" if (n == 1 and k == 7) else None)
            for m in range(4):
                for k in range(8):
                    PE_sel[i] = q.op("t", lambda e, S, m=m, k=k: e.matmul(
                        pC[:, 256 * m:256 * (m + 1)],
                        wselT[:, 512 * k + 128 * m:512 * k + 128 * (m + 1)],
                        xin1fm[:, 256 * k:256 * (k + 1)],
                        start=(k == 0), stop=(k == 7)),
                        sem="pe" if (m == 3 and k == 7) else None)
            q.wait("a", "pe", PE_u[i])
            if i > 0:
                q.wait("a", "dv", V_y[i - 1])
            A_u[i] = q.op("a", lambda e, S: e.activation(
                u_sb[:, :], pB[:, :], AF.Identity), sem="ac")
            q.wait("a", "pe", PE_sel[i])
            if i == 0:
                q.wait("a", "swi", sw_w2)
            if i > 0:
                q.wait("a", "pe", PE_g2[i - 1])
            A_selh[i] = q.op("a", lambda e, S: e.activation(
                selh[:, :], pC[:, :], AF.Silu), sem="ac")
            yield

            q.wait("t", "ac", A_selh[i])
            for gi, m in enumerate((0, 1, 6, 7)):
                for k in range(4):
                    PE_g1[i] = q.op("t", lambda e, S, gi=gi, m=m, k=k: e.matmul(
                        pC[:, 256 * gi:256 * (gi + 1)],
                        wsoT[:, 1024 * k + 128 * m:1024 * k + 128 * (m + 1)],
                        selh[:, 256 * k:256 * (k + 1)],
                        start=(k == 0), stop=(k == 3)),
                        sem="pe" if (gi == 3 and k == 3) else None)
            yield

            q.wait("a", "pe", PE_g1[i])
            if i > 0:
                q.wait("a", "dv", V_y[i - 1])
            A_tad[i] = q.op("a", lambda e, S: e.activation(
                tad[:, :], pC[:, :], AF.Tanh, scale=0.5), sem="ac")
            q.drain("a")
            A_aff[i] = q.op("a", lambda e, S: e.activation(
                tad[:, 0:512], tad[:, 0:512], AF.Identity,
                scale=halfc, bias=halfc),
                sem="ac")
            q.wait("t", "ac", A_tad[i])
            for gi, m in enumerate((2, 3, 4, 5)):
                for k in range(4):
                    PE_g2[i] = q.op("t", lambda e, S, gi=gi, m=m, k=k: e.matmul(
                        pC[:, 256 * gi:256 * (gi + 1)],
                        wsoT[:, 1024 * k + 128 * m:1024 * k + 128 * (m + 1)],
                        selh[:, 256 * k:256 * (k + 1)],
                        start=(k == 0), stop=(k == 3)),
                        sem="pe" if (gi == 3 and k == 3) else None)
            q.wait("a", "pe", PE_g2[i])
            A_tbc[i] = q.op("a", lambda e, S: e.activation(
                tbc[:, :], pC[:, :], AF.Tanh), sem="ac")
            q.wait("v", "ac", A_aff[i])
            q.wait("v", "ac", A_u[i])
            if i > 0:
                q.wait("v", "pe", PE_ao[i - 1][1])
            q.drain("v")
            q.op("v", lambda e, S: e.scalar_tensor_tensor(
                tad[:, 512:1024], tad[:, 512:1024], 1.0, u_sb[:, :], OP.add, OP.mult))
            q.wait("v", "ac", A_tbc[i])
            q.op("v", lambda e, S: e.tensor_tensor(
                tbc[:, 0:512], tbc[:, 0:512], u_sb[:, :], OP.mult))
            q.drain("v")
            hof, hofp = 512 * par, 512 * (1 - par)
            for n in range(2):
                init = 0.0 if i == 0 else hsc[:, hofp + 256 * n + 255:hofp + 256 * n + 256]
                q.op("v", lambda e, S, n=n, init=init, hof=hof: e.tensor_tensor_scan(
                    hsc[:, hof + 256 * n:hof + 256 * (n + 1)],
                    tad[:, 256 * n:256 * (n + 1)],
                    tbc[:, 256 * n:256 * (n + 1)],
                    init, OP.mult, OP.add))
            q.drain("v")
            q.op("v", lambda e, S, hof=hof: e.tensor_tensor(
                tbc[:, 512:1024], tbc[:, 512:1024], hsc[:, hof:hof + 512], OP.mult))
            q.drain("v")
            for n in range(2):
                V_y[i] = q.op("v", lambda e, S, n=n: e.scalar_tensor_tensor(
                    ysb[:, 256 * n:256 * (n + 1)],
                    tad[:, 512 + 256 * n:512 + 256 * (n + 1)],
                    dp2[:, n:n + 1],
                    tbc[:, 512 + 256 * n:512 + 256 * (n + 1)],
                    OP.mult, OP.add), sem="dv")
            yield

            q.wait("t", "dv", V_y[i])
            q.wait("t", "ac", A_tbc[i])
            for s in range(NSUB):
                bank = pC if s == 0 else pD
                if s == 1:
                    if i >= 3:
                        q.wait("t", "ac", A_ffevD[i - 2])
                    elif i >= 1:
                        q.wait("t", "ac", A_aoev[i - 1][1])
                for n in range(2):
                    for jj in range(2):
                        PE_ao[i][s] = q.op("t",
                            lambda e, S, bank=bank, s=s, n=n, jj=jj: e.matmul(
                                bank[:, 512 * jj:512 * (jj + 1)],
                                ysb[:, 256 * n + 128 * s:256 * n + 128 * (s + 1)],
                                woutT[:, 1024 * n + 512 * jj:1024 * n + 512 * (jj + 1)],
                                start=(n == 0), stop=(n == 1)),
                            sem="pe" if (n == 1 and jj == 1) else None)
            q.wait("a", "pe", PE_ao[i][0])
            if i > 0:
                q.wait("a", "dv", P_x2[i - 1][0])
            A_aoev[i][0] = q.op("a", lambda e, S: e.activation(
                stgA[:, :], pC[:, 0:1024], AF.Identity, accum_out=sc("SUM2", 0)),
                sem="ac")
            q.wait("a", "pe", PE_ao[i][1])
            if i > 1 and SO_cnt[i - 2]:
                q.wait("a", "so", SO_cnt[i - 2])
            if i > 0:
                q.wait("a", "dv", P_x2[i - 1][1])
            A_aoev[i][1] = q.op("a", lambda e, S: e.activation(
                stgF[:, 0:1024], pD[:, 0:1024], AF.Identity, accum_out=sc("SUM2", 1)),
                sem="ac")
            q.drain("a")
            q.op("a", lambda e, S: e.activation(
                xin1tm[:, 0:1024], stgA[:, :], AF.Square, accum_out=sc("SSQ2", 0)),
                sem="ac")
            q.op("a", lambda e, S: e.activation(
                xin1tm[:, 1024:2048], stgF[:, 0:1024], AF.Square,
                accum_out=sc("SSQ2", 1)), sem="ac")
            a_sq2 = q.cnt["ac"]
            q.wait("g", "ac", a_sq2)
            g_ch2 = stat_chain("g", "2", 0, 2)
            yield

            yield

            q.wait("a", "gp", g_ch2)
            for s in range(NSUB):
                dst = stgA[:, :] if s == 0 else stgF[:, 0:1024]
                A_t[i][s] = q.op("a", lambda e, S, s=s, dst=dst: e.activation(
                    dst, dst, AF.Identity, bias=sc("B2", s), scale=sc("R2", s)),
                    sem="ac")
            for s in range(NSUB):
                q.wait("v", "ac", A_t[i][s])
                if i > 1:
                    q.wait("v", "gp", P_out[i - 2][1 - s])
                q.drain("v")
                src = stgA[:, :] if s == 0 else stgF[:, 0:1024]
                P_x2[i][s] = q.op("v", lambda e, S, s=s, src=src, i=i:
                                  e.tensor_tensor(x2s(i, s), xinv[:, i % 2, s, :],
                                                  src, OP.add), sem="dv")
            issue_x_dma(i + 2, wait_gp=P_x2[i][1], wait_ac=A_xin1[i])
            q.drain("v")
            for s in range(NSUB):
                q.op("v", lambda e, S, s=s, i=i: e.reduce_max(
                    sc("MX", s), x2s(i, s), AX))
                q.op("v", lambda e, S, s=s, i=i: e.tensor_reduce(
                    sc("MN", s), x2s(i, s), AX, OP.min))
            for s in range(NSUB):
                q.wait("a", "dv", P_x2[i][s])
                q.op("a", lambda e, S, s=s, i=i: e.activation(
                    xin1tm[:, 1024 * s:1024 * (s + 1)], x2s(i, s), AF.Square,
                    accum_out=sc("SSQ3", s)), sem="ac")
            a_sum3 = q.cnt["ac"]
            q.drain("v")
            q.op("v", lambda e, S: e.tensor_tensor(
                sc("QB", 0, 2), sc("R2", 0, 2), sc("SUM2", 0, 2), OP.mult))
            q.drain("v")
            q.op("v", lambda e, S: e.tensor_tensor(
                sc("QB", 0, 2), sc("QB", 0, 2), sc("SUM1", 0, 2), OP.add))
            q.drain("v")
            q.op("v", lambda e, S: e.tensor_scalar(
                sc("QC", 0, 2), sc("B2", 0, 2), float(D), None, OP.mult))
            q.drain("v")
            v_sum3 = q.op("v", lambda e, S: e.tensor_tensor(
                sc("SUM3", 0, 2), sc("QB", 0, 2), sc("QC", 0, 2), OP.add), sem="dv")
            emit_ln1_prep(i + 1, wait_dv=v_sum3)
            q.wait("v", "ac", a_sum3)
            q.drain("v")
            q.op("v", lambda e, S: e.tensor_scalar(
                sc("M3", 0, 2), sc("SUM3", 0, 2), -1.0 / D, None, OP.mult))
            q.op("v", lambda e, S: e.tensor_scalar(
                sc("E3", 0, 2), sc("SSQ3", 0, 2), 1.0 / D, None, OP.mult))
            q.drain("v")
            q.op("v", lambda e, S: e.tensor_tensor(
                sc("QB", 0, 2), sc("M3", 0, 2), sc("M3", 0, 2), OP.mult))
            q.drain("v")
            q.op("v", lambda e, S: e.tensor_tensor(
                sc("E3", 0, 2), sc("E3", 0, 2), sc("QB", 0, 2), OP.subtract))
            q.drain("v")
            q.op("v", lambda e, S: e.tensor_scalar(
                sc("E3", 0, 2), sc("E3", 0, 2), EPS, None, OP.add))
            q.drain("v")
            quake("v", sc("R3", 0, 2), sc("E3", 0, 2), seedi, 2, ("QA", "QB", "QC"))
            q.drain("v")
            q.op("v", lambda e, S: e.tensor_tensor(
                sc("TA", 0, 2), sc("MX", 0, 2), sc("M3", 0, 2), OP.add))
            q.op("v", lambda e, S: e.tensor_tensor(
                sc("TB", 0, 2), sc("MN", 0, 2), sc("M3", 0, 2), OP.add))
            q.drain("v")
            q.op("v", lambda e, S: e.tensor_tensor(
                sc("TA", 0, 2), sc("TA", 0, 2), sc("R3", 0, 2), OP.mult))
            q.op("v", lambda e, S: e.tensor_scalar(
                sc("TB", 0, 2), sc("TB", 0, 2), -1.0, None, OP.mult))
            q.drain("v")
            q.op("v", lambda e, S: e.tensor_tensor(
                sc("TB", 0, 2), sc("TB", 0, 2), sc("R3", 0, 2), OP.mult))
            q.drain("v")
            q.op("v", lambda e, S: e.tensor_tensor(
                sc("MABS", 0, 2), sc("TA", 0, 2), sc("TB", 0, 2), OP.max))
            q.drain("v")
            q.op("v", lambda e, S: e.tensor_scalar(
                sc("MABS", 0, 2), sc("MABS", 0, 2), 1e-5, None, OP.max))
            q.drain("v")
            q.op("v", lambda e, S: e.reciprocal(sc("SC1", 0, 2), sc("MABS", 0, 2)))
            q.drain("v")
            q.op("v", lambda e, S: e.tensor_scalar(
                sc("SC1", 0, 2), sc("SC1", 0, 2), 127.0, None, OP.mult))
            q.drain("v")
            q.op("v", lambda e, S: e.tensor_tensor(
                sc("A1", 0, 2), sc("R3", 0, 2), sc("SC1", 0, 2), OP.mult))
            q.drain("v")
            q.op("v", lambda e, S: e.tensor_tensor(
                sc("B1Q", 0, 2), sc("M3", 0, 2), sc("A1", 0, 2), OP.mult))
            q.drain("v")
            q.op("v", lambda e, S: e.tensor_scalar(
                sc("B1Q", 0, 2), sc("B1Q", 0, 2), RC, None, OP.add))
            if i == 0:
                q.wait("v", "dv", dv_w1)
            q.drain("v")
            V_q1sc[i] = q.op("v", lambda e, S: e.tensor_scalar(
                sc("G1", 0, 2), sc("MABS", 0, 2), wcons[:, 0:1], 1.0 / 127.0,
                OP.mult, OP.mult), sem="dv")
            yield

            q.wait("a", "dv", V_q1sc[i])
            q.wait("g", "dv", V_q1sc[i])
            if i > 0:
                q.wait("g", "pe", PE_T2[i - 1][1])
            if i == 0:
                q.wait("g", "swi", sw_w2)
            for s in range(NSUB):
                for h in range(2):
                    hi = 2 * s + h
                    if hi > 0:
                        q.wait("a", "gp", P_uq1[i][hi - 1])
                    a_an = q.op("a", lambda e, S, s=s, h=h, i=i: e.activation(
                        stgi[:, :], x2sh(i, s, h), AF.Identity,
                        scale=sc("A1", s), bias=sc("B1Q", s)), sem="ac")
                    q.wait("g", "ac", a_an)
                    q.drain("g")
                    bsl = stgb[:, 1024 * s + 512 * h:1024 * s + 512 * (h + 1)]
                    q.op("g", lambda e, S, bsl=bsl: e.tensor_scalar(
                        bsl, stgi[:, :],
                        onec, rccn, OP.mult, OP.add))
                    q.drain("g")
                    P_uq1[i][hi] = q.op("g", lambda e, S, s=s, bsl=bsl: e.tensor_scalar(
                        bsl, bsl,
                        sc("G1", s), zeroc, OP.mult, OP.add), sem="gp")
                q.wait("t", "gp", P_uq1[i][2 * s + 1])
                q.wait("t", "ac", A_aoev[i][0] if s == 0 else A_evT2[i][0])
                for cc in range(8):
                    PE_T2[i][s] = q.op("t", lambda e, S, s=s, cc=cc: e.matmul(
                        pC[:, 128 * cc:128 * (cc + 1)],
                        stgb[:, 1024 * s + 128 * cc:1024 * s + 128 * (cc + 1)],
                        identb[:, :], start=True, stop=True),
                        sem="pe" if cc == 7 else None)
                q.wait("a", "pe", PE_T2[i][s])
                if s == 0 and i > 0:
                    q.wait("a", "pe", PE_w1[i - 1][31])
                if s == 0:
                    dst = xqv[:, par, :, 129:257]
                else:
                    dst = xqv[:, 1 - par, :, 1:129]
                q.op("a", lambda e, S, dst=dst: e.activation(
                    dst, pCv[:, :, :], AF.Identity), sem="ac")
                A_evT2[i][s] = q.cnt["ac"]
            q.wait("v", "ac", A_evT2[i][1])
            if i > 0:
                q.wait("v", "pe", PE_w1[i - 1][31])
            q.drain("v")
            q.op("v", lambda e, S, par=par: e.tensor_copy(
                xqv[:, 1 - par, :, 0], xqv[:, par, :, 256]))
            V_halo[i] = q.op("v", lambda e, S, par=par: e.tensor_copy(
                xqv[:, par, :, 257], xqv[:, 1 - par, :, 1]), sem="dv")
            if i == ntr - 1:
                q.drain("v")
                V_halo[i + 1] = q.op("v", lambda e, S: e.memset(
                    xqv[:, ntr % 2, :, 129:258], 0.0), sem="dv")

        def win_gen(f):
            pf = f % 2
            have_a = f > 0
            have_b = f < ntr
            subs = ([0] if have_a else []) + ([1] if have_b else [])
            emit_z8b(f - 1)

            def issue_w2t(c):
                if c >= 32:
                    return
                if f == 0 and c < 3:
                    q.wait("sy", "swi", sw_w2)
                if c >= 3:
                    q.wait("sy", "pe", PE_w2c[f][c - 3])
                elif f > 0:
                    q.wait("sy", "pe", PE_w2c[f - 1][29 + c])
                swn = "sw%d" % (c % 3)
                SW_w2t[f][c] = (swn, q.op("sy", lambda e, S, c=c: e.dma_start(
                    out=w2sb[:, 1024 * (c % 3):1024 * (c % 3 + 1)],
                    in_=w2s[:, 1024 * c:1024 * (c + 1)]), sem=swn, k=16))

            issue_w2t(0)
            issue_w2t(1)
            issue_w2t(2)

            q.wait("t", "dv", V_halo[f])
            if f == 0:
                q.wait("t", "dv", dv_w1)
                q.wait("t", "dv", dv_pad0)
            if f > 0:
                q.wait("t", "ac", A_ffevE[f - 1])
            for c in range(32):
                off = 512 * (c % 3)
                if c >= 3:
                    q.wait("t", "ac", A_evh[f][c - 3])
                for k in range(8):
                    PE_w1[f][c] = q.op("t",
                        lambda e, S, c=c, k=k, off=off, pf=pf: e.matmul(
                            pE[:, off:off + 258],
                            w1t[:, 4096 * k + 128 * c:4096 * k + 128 * (c + 1)],
                            xqv[:, pf, k, :],
                            start=(k == 0), stop=(k == 7)),
                        sem="pe" if k == 7 else None)
                q.wait("a", "pe", PE_w1[f][c])
                if c >= 4:
                    q.wait("a", "dv", V_cv[f][c - 4])
                q.op("a", lambda e, S, c=c, off=off: e.activation(
                    hpre[:, 258 * (c % 4):258 * (c % 4 + 1)], pE[:, off:off + 258],
                    AF.Identity), sem="ac")
                if c % 2 == 0:
                    if c == 0 and f > 0:
                        q.wait("a", "pe", PE_w2[f - 1])
                    q.op("a", lambda e, S, c=c, off=off: e.activation(
                        h4[:, 256 * c:256 * (c + 1)], pE[:, off + 1:off + 257],
                        AF.Identity, scale=cwt[:, 3 * c + 1:3 * c + 2],
                        bias=cbt[:, c:c + 1]), sem="ac")
                A_evh[f][c] = q.cnt["ac"]
                if c % 4 == 3:
                    if c >= 7:
                        cp = c - 4
                        q.wait("a", "dv", V_cv[f][cp])
                        A_sil[f][cp - 3] = A_sil[f][cp] = q.op(
                            "a", lambda e, S, cp=cp: e.activation(
                                h4[:, 256 * (cp - 3):256 * (cp + 1)],
                                h4[:, 256 * (cp - 3):256 * (cp + 1)], AF.Silu),
                            sem="ac")
                    q.wait("v", "ac", A_evh[f][c])
                    if c == 3 and f > 0:
                        q.wait("v", "pe", PE_w2[f - 1])
                    for cc in range(c - 2, c + 1, 2):
                        hs = hpre[:, 258 * (cc % 4):258 * (cc % 4 + 1)]
                        hd = h4[:, 256 * cc:256 * (cc + 1)]
                        q.op("v", lambda e, S, cc=cc, hs=hs, hd=hd: e.tensor_scalar(
                            hd, hs[:, 1:257], cwt[:, 3 * cc + 1:3 * cc + 2],
                            cbt[:, cc:cc + 1], OP.mult, OP.add))
                    q.drain("v")
                    for cc in range(c - 3, c + 1):
                        hs = hpre[:, 258 * (cc % 4):258 * (cc % 4 + 1)]
                        hd = h4[:, 256 * cc:256 * (cc + 1)]
                        q.op("v", lambda e, S, cc=cc, hs=hs, hd=hd:
                             e.scalar_tensor_tensor(
                                 hd, hs[:, 0:256], cwt[:, 3 * cc:3 * cc + 1], hd,
                                 OP.mult, OP.add))
                    q.drain("v")
                    for cc in range(c - 3, c + 1):
                        hs = hpre[:, 258 * (cc % 4):258 * (cc % 4 + 1)]
                        hd = h4[:, 256 * cc:256 * (cc + 1)]
                        q.op("v", lambda e, S, cc=cc, hs=hs, hd=hd:
                             e.scalar_tensor_tensor(
                                 hd, hs[:, 2:258], cwt[:, 3 * cc + 2:3 * cc + 3],
                                 hd, OP.mult, OP.add),
                             sem="dv" if cc == c else None)
                    for cc in range(c - 3, c + 1):
                        V_cv[f][cc] = q.cnt["dv"]
                    if c >= 7:
                        cp = c - 4
                        q.wait("v", "ac", A_sil[f][cp])
                        q.drain("v")
                        for hh in range(4):
                            gsl = h4[:, 256 * (cp - 3 + hh):256 * (cp - 2 + hh)]
                            if cp == 3 and hh == 0:
                                q.op("v", lambda e, S, gsl=gsl: e.tensor_copy(
                                    accP[:, :], gsl))
                                q.op("v", lambda e, S, gsl=gsl: e.tensor_copy(
                                    accN[:, :], gsl))
                                q.drain("v")
                            else:
                                q.op("v", lambda e, S, gsl=gsl: e.tensor_tensor(
                                    accP[:, :], accP[:, :], gsl, OP.max))
                                q.op("v", lambda e, S, gsl=gsl: e.tensor_tensor(
                                    accN[:, :], accN[:, :], gsl, OP.min))
                                q.drain("v")
                if c % 8 == 7:
                    yield

            q.wait("a", "dv", V_cv[f][31])
            A_sil[f][28] = A_sil[f][31] = q.op(
                "a", lambda e, S: e.activation(
                    h4[:, 256 * 28:256 * 32], h4[:, 256 * 28:256 * 32], AF.Silu),
                sem="ac")
            q.wait("v", "ac", A_sil[f][31])
            q.drain("v")
            for cc in range(28, 32):
                gsl = h4[:, 256 * cc:256 * (cc + 1)]
                q.op("v", lambda e, S, gsl=gsl: e.tensor_tensor(
                    accP[:, :], accP[:, :], gsl, OP.max))
                q.op("v", lambda e, S, gsl=gsl: e.tensor_tensor(
                    accN[:, :], accN[:, :], gsl, OP.min))
                q.drain("v")
            v_mab = q.op("v", lambda e, S: e.scalar_tensor_tensor(
                accP[:, :], accN[:, :], -1.0, accP[:, :], OP.mult, OP.max),
                sem="dv")
            q.wait("t", "dv", v_mab)
            if f + 1 < ntr:
                q.wait("t", "ac", A_u[f + 1])
            else:
                q.wait("t", "dv", V_bc2[f - 1])
            for s in range(2):
                pe_mabT = q.op("t", lambda e, S, s=s: e.matmul(
                    pB[:, 128 * s:128 * (s + 1)],
                    accP[:, 128 * s:128 * (s + 1)],
                    identb[:, :], start=True, stop=True),
                    sem="pe" if s == 1 else None)
            q.wait("v", "pe", pe_mabT)
            for s in range(2):
                q.op("v", lambda e, S, s=s: e.reduce_max(
                    sc("MAB2", s), pB[:, 128 * s:128 * (s + 1)], AX))
            q.drain("v")
            q.op("v", lambda e, S: e.tensor_scalar(
                sc("MAB2", 0, 2), sc("MAB2", 0, 2), 1e-5, None, OP.max))
            q.drain("v")
            q.op("v", lambda e, S: e.reciprocal(sc("SC2", 0, 2), sc("MAB2", 0, 2)))
            q.drain("v")
            q.op("v", lambda e, S: e.tensor_scalar(
                sc("SC2", 0, 2), sc("SC2", 0, 2), 127.0, None, OP.mult))
            if f == 0:
                q.wait("v", "dv", dv_w2)
            q.op("v", lambda e, S: e.tensor_scalar(
                sc("G2", 0, 2), sc("MAB2", 0, 2), wcons[:, 2:3], 1.0 / 127.0,
                OP.mult, OP.mult))
            q.drain("v")
            v_sc2 = q.op("v", lambda e, S: e.tensor_copy(sc("QC"), sc("SC2")),
                         sem="dv")
            q.wait("t", "dv", v_sc2)
            for s in range(2):
                pe_r = q.op("t", lambda e, S, s=s: e.matmul(
                    pB[0:1, 256 + 128 * s:256 + 128 * (s + 1)], sc("SC2", s),
                    identf[:, :], start=True, stop=True),
                    sem="pe" if s == 1 else None)
            q.wait("v", "pe", pe_r)
            v_row = q.op("v", lambda e, S: e.tensor_copy(
                tad[0:1, 0:256], pB[0:1, 256:512]), sem="dv")
            q.wait("t", "dv", v_row)
            pe_bc = q.op("t", lambda e, S: e.matmul(
                pB[:, 256:512], ones[0:1, 0:128], tad[0:1, 0:256],
                start=True, stop=True), sem="pe")
            q.wait("v", "pe", pe_bc)
            V_bc2[f] = q.op("v", lambda e, S: e.tensor_copy(bc2[:, :], pB[:, 256:512]),
                            sem="dv")
            yield

            q.wait("t", "dv", V_bc2[f])
            if f + 1 < ntr:
                q.wait("t", "ac", A_aoev[f + 1][1])
            elif f > 0:
                q.wait("t", "ac", A_ffevD[f - 1])
            q.wait("t", "ac", A_evh[f][31])
            def emit_pair(p2):
                c = 2 * p2
                if p2 == 0:
                    q.wait("v", "dv", V_bc2[f])
                    q.drain("v")
                vtt = 0
                for cc in (c, c + 1):
                    hsl4 = h4[:, 256 * cc:256 * (cc + 1)]
                    vtt = q.op("v", lambda e, S, hsl4=hsl4: e.tensor_tensor(
                        hsl4, hsl4, bc2[:, :], OP.mult),
                        sem="dv" if cc == c + 1 else None)
                q.wait("a", "dv", vtt)
                if p2 >= 2:
                    q.wait("a", "gp", P_uq2[f][p2 - 2])
                elif f > 0:
                    q.wait("a", "gp", P_uq2[f - 1][14 + p2])
                ksl = k2i[:, 512 * (p2 % 2):512 * (p2 % 2 + 1)]
                psl = h4[:, 512 * p2:512 * (p2 + 1)]
                A_anc[f][p2] = q.op("a", lambda e, S, ksl=ksl, psl=psl:
                                    e.activation(ksl, psl, AF.Identity,
                                                 bias=rcc), sem="ac")
                q.wait("g", "ac", A_anc[f][p2])
                P_uq2[f][p2] = q.op("g", lambda e, S, ksl=ksl, psl=psl:
                                    e.tensor_scalar(psl, ksl, onec,
                                                    rccn, OP.mult, OP.add),
                                    sem="gp")

            for p2 in range(6):
                emit_pair(p2)
            for c in range(32):
                if c >= 12 and c % 2 == 0:
                    emit_pair(c // 2)
                q.wait("t", "gp", P_uq2[f][c // 2])
                q.wait("t", *SW_w2t[f][c])
                for s in subs:
                    bank = pD if s == 0 else pE
                    for jj in range(2):
                        coff = 1024 * (c % 3) + 512 * jj
                        mm = q.op("t", lambda e, S, c=c, s=s, jj=jj, bank=bank,
                                  coff=coff: e.matmul(
                                      bank[:, 512 * jj:512 * (jj + 1)],
                                      h4[:, 256 * c + 128 * s:256 * c + 128 * (s + 1)],
                                      w2sb[:, coff:coff + 512],
                                      start=(c == 0), stop=(c == 31)),
                                  sem="pe" if (s == subs[-1] and jj == 1) else None)
                PE_w2c[f][c] = mm
                issue_w2t(c + 3)
                if c == 11:
                    yield
            PE_w2[f] = PE_w2c[f][31]
            yield

            q.wait("a", "pe", PE_w2[f])
            if SO_cnt[f - 1]:
                q.wait("a", "so", SO_cnt[f - 1])
            if have_b:
                q.wait("a", "dv", P_x2[f][1])
                A_ffevE[f] = q.op("a", lambda e, S: e.activation(
                    stgF[:, 1024:2048], pE[:, 0:1024], AF.Identity,
                    accum_out=sc("SUMF", 1)), sem="ac")
            if have_a:
                A_ffevD[f] = q.op("a", lambda e, S: e.activation(
                    stgF[:, 0:1024], pD[:, 0:1024], AF.Identity,
                    accum_out=sc("SUMF", 0)), sem="ac")
            q.drain("a")
            q.wait("a", "dv", P_x2[min(f + 1, ntr - 1)][0])
            for s in subs:
                src_ = stgF[:, 1024 * s:1024 * (s + 1)]
                q.op("a", lambda e, S, s=s, src_=src_: e.activation(
                    stgA[:, :], src_, AF.Square, accum_out=sc("SSQF", s)), sem="ac")
                q.drain("a")
            a_sqf = q.cnt["ac"]
            q.wait("v", "ac", a_sqf)
            V_chF[f] = stat_chain("v", "F", subs[0], len(subs), var_scale_col="G2")

        def emit_z8b(f):
            if f < 0:
                return
            have_a = f > 0
            have_b = f < ntr
            subs = ([0] if have_a else []) + ([1] if have_b else [])
            q.wait("a", "dv", V_chF[f])
            a_apF = 0
            for s in subs:
                dst = stgF[:, 1024 * s:1024 * (s + 1)]
                a_apF = q.op("a", lambda e, S, s=s, dst=dst: e.activation(
                    dst, dst, AF.Identity, bias=sc("BF", s), scale=sc("RF", s)),
                    sem="ac")
            q.wait("g", "ac", a_apF)
            if have_a:
                q.wait("g", "dv", P_x2[f - 1][1])
            if have_b:
                q.wait("g", "dv", P_x2[f][0])
            q.drain("g")
            for s in subs:
                dst = stgF[:, 1024 * s:1024 * (s + 1)]
                x2src = x2s(f - 1, 1) if s == 0 else x2s(f, 0)
                P_out[f][s] = q.op("g", lambda e, S, dst=dst, x2src=x2src:
                                   e.tensor_tensor(dst, x2src, dst, OP.add),
                                   sem="gp")
                rows = ((f - 1) * TT + 128) if s == 0 else f * TT
                q.wait("sy", "gp", P_out[f][s])
                dstH = out_hbm[rows:rows + 128, :]
                SO_cnt[f] = q.op("sy", lambda e, S, dstH=dstH, dst=dst: e.dma_start(
                    out=dstH, in_=dst), sem="so", k=16)

        def run_interleaved(g1, g2):
            gens = [g1, g2]
            while gens:
                for g in list(gens):
                    try:
                        next(g)
                    except StopIteration:
                        gens.remove(g)

        for i in range(ntr):
            if i > 0:
                run_interleaved(attn_gen(i), win_gen(i - 1))
            else:
                for _ in attn_gen(0):
                    pass
        if ntr > 0:
            for _ in win_gen(ntr - 1):
                pass
            for _ in win_gen(ntr):
                pass
            emit_z8b(ntr)
        for _s, _v in q.cnt.items():
            q.wait("sy", _s, _v)

        with nc.Block() as block:
            @block.sync
            def _(eng):
                for fn in q.q["sy"]:
                    fn(eng, sems)

            @block.vector
            def _(eng):
                for fn in q.q["v"]:
                    fn(eng, sems)

            @block.scalar
            def _(eng):
                for fn in q.q["a"]:
                    fn(eng, sems)

            @block.tensor
            def _(eng):
                for fn in q.q["t"]:
                    fn(eng, sems)

            @block.gpsimd
            def _(eng):
                for fn in q.q["g"]:
                    fn(eng, sems)

    return nc


def _host_layout(inputs):
    f32 = np.float32

    def fm(w, nchunk):
        i, o = w.shape
        return np.ascontiguousarray(
            w.reshape(nchunk, 128, o).transpose(1, 0, 2).reshape(128, nchunk * o))

    W_in = np.asarray(inputs["L0_attn_W_in"], f32)
    W_sel = np.asarray(inputs["L0_attn_W_sel_in"], f32)
    W_so = np.asarray(inputs["L0_attn_W_sel_out"], f32)
    W_out = np.asarray(inputs["L0_attn_W_out"], f32)
    w1 = np.asarray(inputs["L0_ffn_w1"], f32)
    w2 = np.asarray(inputs["L0_ffn_w2"], f32)
    d_param = np.asarray(inputs["L0_attn_d_param"], f32)
    cw = np.asarray(inputs["L0_ffn_conv_w"], f32)[:, 0, :]
    cb = np.asarray(inputs["L0_ffn_conv_b"], f32)
    return {
        "winT": fm(np.ascontiguousarray(W_in.T), 8),
        "wselT": fm(np.ascontiguousarray(W_sel.T), 8),
        "wsoT": fm(np.ascontiguousarray(W_so.T), 4),
        "woutT": fm(np.ascontiguousarray(W_out.T), 2),
        "w1T": fm(np.ascontiguousarray(w1.T), 8),
        "w2T": fm(np.ascontiguousarray(w2.T), 32),
        "dp": np.ascontiguousarray(d_param.reshape(2, 128).T),
        "cw": np.ascontiguousarray(cw.reshape(32, 128, 3).transpose(1, 0, 2).reshape(128, 96)),
        "cb": np.ascontiguousarray(cb.reshape(32, 128).T),
    }


def kernel(**inputs):
    x = np.ascontiguousarray(np.asarray(inputs["L0_attn_x"], np.float32))
    wd = _host_layout(inputs)
    nc = build_program()
    in_maps = []
    for c in range(8):
        m = {"x": np.ascontiguousarray(x[c])}
        m.update(wd)
        in_maps.append(m)
    res = run_bass_kernel_spmd(nc, in_maps, core_ids=list(range(8)))
    return np.stack([np.asarray(r["out"]) for r in res.results], axis=0)


# version 26
# speedup vs baseline: 1.1596x; 1.0232x over previous
import numpy as np
import concourse.bass as bass
import concourse.mybir as mybir
from concourse.bass_utils import run_bass_kernel_spmd
from concourse.alu_op_type import AluOpType as OP

F32 = mybir.dt.float32
BF16 = mybir.dt.bfloat16
I32 = mybir.dt.int32
AF = mybir.ActivationFunctionType
AX = mybir.AxisListType.X

D = 1024
T_TOK = 4096
TT = 256
NT = T_TOK // TT
NSUB = 2
HD = 4096
NST = 256
SEL = 512
EPS = 1e-5
QMAGIC = 1597463007.0
RC = 12582912.0

_names = ("SUM1 SSQ1 M1 E1 R1 B1 SUM2 SSQ2 M2 E2 R2 B2 "
          "SUM3 SSQ3 M3 E3 R3 MX MN TA TB MABS SC1 A1 B1Q G1 "
          "SUMF SSQF MF EF RF BF GG MAB2 SC2 G2 QA QB QC PA PB PC").split()
C = {n: 2 * k for k, n in enumerate(_names)}
STT_COLS = 2 * len(_names)


class Q:
    ENGS = ("sy", "v", "a", "t", "g")

    def __init__(self):
        self.q = {e: [] for e in self.ENGS}
        self.cnt = {s: 0 for s in ("sm", "sx0", "sx1", "so", "sw0", "sw1",
                                   "swi", "gd", "gp", "pe", "dv", "ac",
                                   "ad", "vd", "sw2", "sw3")}

    def do(self, qn, f):
        self.q[qn].append(f)

    def op(self, qn, f, sem=None, k=1):
        if sem is None:
            self.q[qn].append(f)
            return None
        self.cnt[sem] += k
        self.q[qn].append(lambda e, S, f=f, sem=sem, k=k: f(e, S).then_inc(S[sem], k))
        return self.cnt[sem]

    def wait(self, qn, sem, v):
        if not v:
            return
        self.q[qn].append(lambda e, S, sem=sem, v=v: e.wait_ge(S[sem], v))

    def drain(self, qn):
        self.q[qn].append(lambda e, S: e.drain())


def build_program(ntr=NT):
    nc = bass.Bass("TRN2", target_bir_lowering=False)

    x_hbm = nc.dram_tensor("x", [T_TOK, D], F32, kind="ExternalInput").ap()
    winT_h = nc.dram_tensor("winT", [128, 8 * NST], F32, kind="ExternalInput").ap()
    wselT_h = nc.dram_tensor("wselT", [128, 8 * SEL], F32, kind="ExternalInput").ap()
    wsoT_h = nc.dram_tensor("wsoT", [128, 4 * D], F32, kind="ExternalInput").ap()
    woutT_h = nc.dram_tensor("woutT", [128, 2 * D], F32, kind="ExternalInput").ap()
    w1T_h = nc.dram_tensor("w1T", [128, 8 * HD], F32, kind="ExternalInput").ap()
    w2T_h = nc.dram_tensor("w2T", [128, 32 * D], F32, kind="ExternalInput").ap()
    dp_h = nc.dram_tensor("dp", [128, 2], F32, kind="ExternalInput").ap()
    cw_h = nc.dram_tensor("cw", [128, 96], F32, kind="ExternalInput").ap()
    cb_h = nc.dram_tensor("cb", [128, 32], F32, kind="ExternalInput").ap()
    out_hbm = nc.dram_tensor("out", [T_TOK, D], F32, kind="ExternalOutput").ap()
    w2s = nc.dram_tensor("w2s", [128, 32 * D], BF16, kind="Internal").ap()

    q = Q()
    from contextlib import ExitStack
    st_ = ExitStack()
    sb = lambda name, shape, dt: st_.enter_context(nc.sbuf_tensor(name, shape, dt))
    ps_ = lambda name, shape, dt: st_.enter_context(nc.psum_tensor(name, shape, dt))

    with st_:
        identf = sb("identf", [128, 128], F32)
        identb = sb("identb", [128, 128], BF16)
        ones = sb("ones", [128, 128], F32)
        winT = sb("winT_b", [128, 8 * NST], BF16)
        wselT = sb("wselT_b", [128, 8 * SEL], BF16)
        wsoT = sb("wsoT_b", [128, 4 * D], BF16)
        woutT = sb("woutT_b", [128, 2 * D], BF16)
        w1t = sb("w1t", [128, 8 * HD], BF16)
        dp2 = sb("dp2", [128, 2], F32)
        cwt = sb("cwt", [128, 96], F32)
        cbt = sb("cbt", [128, 32], F32)
        wcons = sb("wcons", [128, 4], F32)
        stgA = sb("stgA", [128, 1024], F32)
        stgF = sb("stgF", [128, 2048], F32)
        stgi = sb("stgi", [128, 512], I32)
        stgb = sb("stgb", [128, 2048], BF16)
        xin = sb("xin", [128, 2 * 2048], F32)
        x2 = sb("x2", [128, 5 * 1024], F32)
        xin1tm = sb("xin1tm", [128, 2048], BF16)
        xin1fm = sb("xin1fm", [128, 2048], BF16)
        xqw = sb("xqw", [128, 2 * 8 * 258], BF16)
        u_sb = sb("u_sb", [128, 512], BF16)
        tad = sb("tad", [128, 1024], F32)
        tbc = sb("tbc", [128, 1024], F32)
        hsc = sb("hsc", [128, 2 * 512], F32)
        ysb = sb("ysb", [128, 512], BF16)
        selh = sb("selh", [128, 1024], BF16)
        h4 = sb("h4", [128, 32 * 256], BF16)
        hpre = sb("hpre", [128, 4 * 258], BF16)
        k2i = sb("k2i", [128, 2 * 512], I32)
        bc2 = sb("bc2", [128, 256], BF16)
        accP = sb("accP", [128, 256], BF16)
        accN = sb("accN", [128, 256], BF16)
        w2sb = sb("w2sb", [128, 4096], BF16)
        stt = sb("stt", [128, STT_COLS], F32)
        wtot = stt.ap()[0:1, C["MX"]:C["MX"] + 4]
        seedib = sb("seedib", [128, 4], I32)
        seedi = seedib.ap()[:, 0:2]
        seedp = seedib.ap()[:, 2:4]
        consts = sb("consts", [128, 8], F32)
        rcc = consts.ap()[:, 0:1]
        rccn = consts.ap()[:, 1:2]
        onec = consts.ap()[:, 2:3]
        halfc = consts.ap()[:, 3:4]
        zeroc = consts.ap()[:, 4:5]

        pB = ps_("pB", [128, 512], F32)
        pC = ps_("pC", [128, 1024], F32)
        pD = ps_("pD", [128, 1024], F32)
        pE = ps_("pE", [128, 1536], F32)

        sems = {}
        for s in ("sm", "sx0", "sx1", "so", "sw0", "sw1", "swi", "gd", "gp",
                  "pe", "dv", "ac", "ad", "vd", "sw2", "sw3"):
            sems[s] = st_.enter_context(nc.semaphore(s))

        xinv = xin.ap().rearrange("p (q u d) -> p q u d", q=2, u=2)
        fm1v = xin1fm.ap().rearrange("p (c u w) -> p c u w", c=8, u=2)
        pCv = pC.ap().rearrange("p (c w) -> p c w", c=8)
        xqv = xqw.ap().rearrange("p (b c w) -> p b c w", b=2, c=8)

        def sc(col, s=0, n=1):
            return stt[:, C[col] + s:C[col] + s + n]

        def x2slot(i, s):
            return (i % 2) if s == 0 else 2 + (i % 3)

        def x2s(i, s):
            slot = x2slot(i, s)
            return x2[:, 1024 * slot:1024 * (slot + 1)]

        def x2sh(i, s, h):
            slot = x2slot(i, s)
            return x2[:, 1024 * slot + 512 * h:1024 * slot + 512 * (h + 1)]

        q.do("g", lambda e, S: e.memset(ones[:, :], 1.0))
        q.do("g", lambda e, S: e.memset(rcc, RC))
        q.do("g", lambda e, S: e.memset(rccn, -RC))
        q.do("g", lambda e, S: e.memset(onec, 1.0))
        q.do("g", lambda e, S: e.memset(halfc, 0.5))
        q.do("g", lambda e, S: e.memset(zeroc, 0.0))
        q.drain("g")
        q.op("g", lambda e, S: e.affine_select(
            identf[:, :], ones[:, :], [[1, 128]], OP.is_equal, 0.0,
            base=0, channel_multiplier=-1), sem="gp")
        gp_id = q.cnt["gp"]
        q.wait("v", "gp", gp_id)
        dv_id = q.op("v", lambda e, S: e.tensor_copy(identb[:, :], identf[:, :]),
                     sem="dv")
        dv_pad0 = q.op("v", lambda e, S: e.memset(xqv[:, 0, :, 0:129], 0.0),
                       sem="dv")

        for dst, src in ((dp2, dp_h), (cwt, cw_h), (cbt, cb_h)):
            q.op("sy", lambda e, S, dst=dst, src=src: e.dma_start(
                out=dst[:, :], in_=src[:, :]), sem="sm", k=16)
        si_small = q.cnt["sm"]
        q.wait("v", "sm", si_small)
        q.drain("v")
        dv_dp2 = q.op("v", lambda e, S: e.tensor_scalar_mul(dp2[:, :], dp2[:, :], 0.5),
                      sem="dv")

        si_x = [None] * (NT + 2)

        def issue_x_dma(i, wait_gp=0, wait_ac=0):
            if i >= ntr:
                return
            par = i % 2
            src = x_hbm[i * TT:(i + 1) * TT, :].rearrange("(s p) d -> p s d", p=128)
            dst = xinv[:, par, :, :]
            q.wait("sy", "dv", wait_gp)
            q.wait("sy", "ac", wait_ac)
            sxn = "sx%d" % (i % 2)
            si_x[i] = (sxn, q.op("sy", lambda e, S, src=src, dst=dst:
                                 e.dma_start(out=dst, in_=src), sem=sxn, k=16))

        issue_x_dma(0)
        issue_x_dma(1)

        stg_gate = {0: None, 1: None}
        last_dma = {"gd": q.cnt["gd"], "sm": q.cnt["sm"]}

        def piece_dma(p, src_h, gate):
            qn, sem = ("g", "gd") if p % 2 == 0 else ("sy", "sm")
            if gate is None:
                gate = stg_gate[p % 2]
            if gate:
                q.wait(qn, gate[0], gate[1])
            q.wait(qn, sem, last_dma[sem])
            v = q.op(qn, lambda e, S, p=p, src_h=src_h: e.dma_start(
                out=stgF[:, 1024 * (p % 2):1024 * (p % 2 + 1)],
                in_=src_h[:, 1024 * p:1024 * (p + 1)]), sem=sem, k=16)
            last_dma[sem] = v
            return (sem, v)

        def stage_cast(dst, src_h, ncols):
            npc = (ncols + 1023) // 1024
            dmas = [None] * (npc + 2)
            cons = [0] * (npc + 2)
            dmas[0] = piece_dma(0, src_h, None)
            if npc > 1:
                dmas[1] = piece_dma(1, src_h, None)
            for p in range(npc):
                w = min(1024, ncols - 1024 * p)
                q.wait("v", *dmas[p])
                cons[p] = q.op("v", lambda e, S, dst=dst, p=p, w=w: e.tensor_copy(
                    dst[:, 1024 * p:1024 * p + w],
                    stgF[:, 1024 * (p % 2):1024 * (p % 2) + w]), sem="dv")
                stg_gate[p % 2] = ("dv", cons[p])
                if p + 2 < npc:
                    dmas[p + 2] = piece_dma(p + 2, src_h, ("dv", cons[p]))
            return q.cnt["dv"]

        stage_cast(winT, winT_h, 8 * NST)
        stage_cast(wselT, wselT_h, 8 * SEL)
        stage_cast(wsoT, wsoT_h, 4 * D)
        dv_attn_w = stage_cast(woutT, woutT_h, 2 * D)

        last_dma.update({"ad": 0, "vd": 0})

        class WP:
            pass

        c1 = WP()
        c1.src, c1.npc = w1T_h, 32
        c1.queues = (("g", "gd"), ("sy", "sm"))
        c1.slot = lambda j: stgF[:, 1024 * j:1024 * (j + 1)]
        c1.acc, c1.accs, c1.vs = "QA", "QB", "QC"
        c1.junk = xin1tm[:, 0:1024]
        c1.mean_col, c1.scale_col = 0, 1
        c1.gate = [stg_gate[0], stg_gate[1]]
        c2 = WP()
        c2.src, c2.npc = w2T_h, 32
        c2.queues = (("a", "ad"), ("a", "ad"))
        c2.slot = lambda j: (tad if j == 0 else tbc)[:, 0:1024]
        c2.acc, c2.accs, c2.vs = "PA", "PB", "PC"
        c2.junk = xin1fm[:, 0:1024]
        c2.mean_col, c2.scale_col = 2, 3
        c2.gate = [None, None]

        def wp_dma(ctx, p, gate):
            qn, sem = ctx.queues[p % 2]
            if gate is None:
                gate = ctx.gate[p % 2]
            if gate:
                q.wait(qn, gate[0], gate[1])
            q.wait(qn, sem, last_dma[sem])
            v = q.op(qn, lambda e, S, p=p, ctx=ctx: e.dma_start(
                out=ctx.slot(p % 2), in_=ctx.src[:, 1024 * p:1024 * (p + 1)]),
                sem=sem, k=16)
            last_dma[sem] = v
            return (sem, v)

        for ctx in (c1, c2):
            ctx.dmas = [None] * (ctx.npc + 2)
            ctx.addm = [0] * (ctx.npc + 2)
            ctx.acm = [0] * (ctx.npc + 2)
            ctx.dmas[0] = wp_dma(ctx, 0, None)
            ctx.dmas[1] = wp_dma(ctx, 1, None)
        for p in range(32):
            for ctx in (c1, c2):
                q.wait("a", *ctx.dmas[p])
                if p >= 2:
                    q.wait("a", "dv", ctx.addm[p - 2])
                q.drain("a")
                ctx.acm[p] = q.op("a", lambda e, S, p=p, ctx=ctx: e.activation(
                    ctx.junk, ctx.slot(p % 2), AF.Abs,
                    accum_out=sc(ctx.acc, p % 2)), sem="ac")
                ctx.gate[p % 2] = ("ac", ctx.acm[p])
                if p + 2 < ctx.npc:
                    ctx.dmas[p + 2] = wp_dma(ctx, p + 2, ("ac", ctx.acm[p]))
                q.wait("v", "ac", ctx.acm[p])
                if p == 0:
                    ctx.addm[p] = q.op("v", lambda e, S, ctx=ctx: e.tensor_copy(
                        sc(ctx.accs), sc(ctx.acc, 0)), sem="dv")
                else:
                    q.drain("v")
                    ctx.addm[p] = q.op("v", lambda e, S, p=p, ctx=ctx: e.tensor_tensor(
                        sc(ctx.accs), sc(ctx.accs), sc(ctx.acc, p % 2), OP.add),
                        sem="dv")
        for ctx in (c1, c2):
            q.drain("v")
            vsum = q.op("v", lambda e, S, ctx=ctx: e.tensor_copy(
                sc(ctx.vs), sc(ctx.accs)), sem="dv")
            q.wait("t", "dv", vsum)
            mm1 = q.op("t", lambda e, S, ctx=ctx: e.matmul(
                pC[0:1, 0:1], ones[:, 0:1], sc(ctx.vs), start=True, stop=True),
                sem="pe")
            q.wait("v", "pe", mm1)
            q.op("v", lambda e, S, ctx=ctx: e.tensor_scalar(
                wtot[0:1, ctx.mean_col:ctx.mean_col + 1], pC[0:1, 0:1],
                1.0 / (128.0 * 1024 * ctx.npc), 1e-5, OP.mult, OP.max))
            q.drain("v")
            vrec = q.op("v", lambda e, S, ctx=ctx: e.reciprocal(
                wtot[0:1, ctx.scale_col:ctx.scale_col + 1],
                wtot[0:1, ctx.mean_col:ctx.mean_col + 1]), sem="dv")
            q.drain("v")
            q.wait("t", "dv", vrec)
            mm2 = q.op("t", lambda e, S, ctx=ctx: e.matmul(
                pC[:, 0:2], ones[0:1, 0:128],
                wtot[0:1, ctx.mean_col:ctx.mean_col + 2],
                start=True, stop=True), sem="pe")
            q.wait("v", "pe", mm2)
            ctx.vbc = q.op("v", lambda e, S, ctx=ctx: e.tensor_copy(
                wcons[:, ctx.mean_col:ctx.mean_col + 2], pC[:, 0:2]), sem="dv")
            q.drain("v")

        for ctx in (c1, c2):
            ctx.dmas = [None] * (ctx.npc + 2)
            ctx.clip = [0] * (ctx.npc + 2)
            ctx.dmas[0] = wp_dma(ctx, 0, None)
            ctx.dmas[1] = wp_dma(ctx, 1, None)
        last_sw = [q.cnt["swi"], q.cnt["swi"]]
        for p in range(32):
            ctx = c1
            q.wait("a", *ctx.dmas[p])
            if p == 0:
                q.wait("a", "dv", ctx.vbc)
            if p >= 1:
                q.wait("a", "dv", ctx.clip[p - 1])
            a_an = q.op("a", lambda e, S, p=p, ctx=ctx: e.activation(
                k2i[:, 0:1024], ctx.slot(p % 2), AF.Identity,
                scale=wcons[:, ctx.scale_col:ctx.scale_col + 1],
                bias=rcc), sem="ac")
            ctx.gate[p % 2] = ("ac", a_an)
            if p + 2 < ctx.npc:
                ctx.dmas[p + 2] = wp_dma(ctx, p + 2, ("ac", a_an))
            q.wait("v", "ac", a_an)
            dstb = w1t[:, 1024 * p:1024 * (p + 1)]
            q.drain("v")
            q.op("v", lambda e, S, dstb=dstb: e.tensor_scalar(
                dstb, k2i[:, 0:1024], -RC, -1.0, OP.add, OP.max))
            q.drain("v")
            ctx.clip[p] = q.op("v", lambda e, S, dstb=dstb: e.tensor_scalar(
                dstb, dstb, 1.0, None, OP.min), sem="dv")
            ctx = c2
            q.wait("a", *ctx.dmas[p])
            if p == 0:
                q.wait("a", "dv", ctx.vbc)
            dstb = stgb[:, 0:1024] if p % 2 == 0 else selh[:, 0:1024]
            lastc = 0
            for h in range(2):
                if p >= 1 or h == 1:
                    q.wait("a", "dv", ctx.clip[p - 1] if h == 0 else lastc)
                a_an = q.op("a", lambda e, S, p=p, h=h, ctx=ctx: e.activation(
                    stgi[:, :], ctx.slot(p % 2)[:, 512 * h:512 * (h + 1)],
                    AF.Identity,
                    scale=wcons[:, ctx.scale_col:ctx.scale_col + 1],
                    bias=rcc), sem="ac")
                if h == 1:
                    ctx.gate[p % 2] = ("ac", a_an)
                    if p + 2 < ctx.npc:
                        ctx.dmas[p + 2] = wp_dma(ctx, p + 2, ("ac", a_an))
                q.wait("v", "ac", a_an)
                if h == 0 and p >= 2:
                    q.wait("v", "swi", last_sw[p % 2])
                dsl = dstb[:, 512 * h:512 * (h + 1)]
                q.drain("v")
                q.op("v", lambda e, S, dsl=dsl: e.tensor_scalar(
                    dsl, stgi[:, :], -RC, -1.0, OP.add, OP.max))
                q.drain("v")
                lastc = q.op("v", lambda e, S, dsl=dsl: e.tensor_scalar(
                    dsl, dsl, 1.0, None, OP.min), sem="dv")
            ctx.clip[p] = lastc
            q.wait("a", "dv", ctx.clip[p])
            q.wait("a", "swi", max(last_sw))
            last_sw[p % 2] = q.op("a", lambda e, S, p=p, dstb=dstb: e.dma_start(
                out=w2s[:, 1024 * p:1024 * (p + 1)], in_=dstb),
                sem="swi", k=16)
        dv_w1 = c1.clip[31]
        dv_w2 = c2.vbc
        sw_w2 = last_sw[1] if last_sw[1] > last_sw[0] else last_sw[0]

        def quake(qn, dst_ap, src_ap, seed_t, n, pre):
            qa, qb, qc = (sc(pre[0], 0, n), sc(pre[1], 0, n), sc(pre[2], 0, n))
            q.op(qn, lambda e, S: e.tensor_copy(qa, src_ap.bitcast(I32)))
            q.drain(qn)
            q.op(qn, lambda e, S: e.tensor_scalar(
                seed_t[:, 0:n], qa, -0.5, QMAGIC, OP.mult, OP.add))
            q.drain(qn)
            cur = seed_t[:, 0:n].bitcast(F32)
            for it in range(3):
                tgt = dst_ap if it == 2 else qa
                q.op(qn, lambda e, S, cur=cur: e.tensor_tensor(qb, cur, cur, OP.mult))
                q.drain(qn)
                q.op(qn, lambda e, S: e.tensor_tensor(qc, src_ap, qb, OP.mult))
                q.drain(qn)
                q.op(qn, lambda e, S: e.tensor_scalar(qc, qc, -0.5, 1.5, OP.mult, OP.add))
                q.drain(qn)
                q.op(qn, lambda e, S, cur=cur, tgt=tgt: e.tensor_tensor(
                    tgt, cur, qc, OP.mult))
                q.drain(qn)
                cur = qa

        def stat_chain(qn, pre, n0, nn, var_scale_col=None):
            scr = ("PA", "PB", "PC") if qn == "g" else ("QA", "QB", "QC")
            seed_t = seedp if qn == "g" else seedi
            sem = "gp" if qn == "g" else "dv"
            Mc = sc("M" + pre, n0, nn)
            Ec = sc("E" + pre, n0, nn)
            q.drain(qn)
            q.op(qn, lambda e, S: e.tensor_scalar(
                Mc, sc("SUM" + pre, n0, nn), -1.0 / D, None, OP.mult))
            q.op(qn, lambda e, S: e.tensor_scalar(
                Ec, sc("SSQ" + pre, n0, nn), 1.0 / D, None, OP.mult))
            q.drain(qn)
            q.op(qn, lambda e, S: e.tensor_tensor(
                sc(scr[1], n0, nn), Mc, Mc, OP.mult))
            q.drain(qn)
            q.op(qn, lambda e, S: e.tensor_tensor(Ec, Ec, sc(scr[1], n0, nn),
                                                  OP.subtract))
            q.drain(qn)
            if var_scale_col is not None:
                q.op(qn, lambda e, S: e.tensor_tensor(
                    sc("GG", n0, nn), sc(var_scale_col, n0, nn),
                    sc(var_scale_col, n0, nn), OP.mult))
                q.drain(qn)
                q.op(qn, lambda e, S: e.tensor_tensor(
                    Ec, Ec, sc("GG", n0, nn), OP.mult))
                q.drain(qn)
            q.op(qn, lambda e, S: e.tensor_scalar(Ec, Ec, EPS, None, OP.add))
            q.drain(qn)
            quake(qn, sc("R" + pre, n0, nn), Ec, seed_t, nn, scr)
            q.drain(qn)
            if var_scale_col is not None:
                q.op(qn, lambda e, S: e.tensor_tensor(
                    sc("R" + pre, n0, nn), sc("R" + pre, n0, nn),
                    sc(var_scale_col, n0, nn), OP.mult))
                q.drain(qn)
            q.op(qn, lambda e, S: e.tensor_tensor(
                sc("B" + pre, n0, nn), Mc, sc("R" + pre, n0, nn), OP.mult))
            return q.op(qn, lambda e, S: e.tensor_copy(
                sc(scr[2], 0, 1), sc("M" + pre, n0, 1)), sem=sem)

        Z = lambda: [0] * (NT + 2)
        Z2 = lambda: [[0, 0] for _ in range(NT + 2)]
        PE_T1, A_evT1, PE_u, PE_sel = Z2(), Z(), Z(), Z()
        PE_g1, PE_g2, A_u, A_selh, A_tad, A_tbc, A_aff, V_y = (
            Z(), Z(), Z(), Z(), Z(), Z(), Z(), Z())
        PE_ao, A_aoev, A_t, P_x2 = Z2(), Z2(), Z2(), Z2()
        PE_T2, A_evT2 = Z2(), Z2()
        A_xin1, V_q1sc, V_halo = Z(), Z(), Z()
        P_uq1 = [[0] * 4 for _ in range(NT + 2)]
        PE_w1 = [[0] * 32 for _ in range(NT + 2)]
        A_evh = [[0] * 32 for _ in range(NT + 2)]
        V_cv = [[0] * 32 for _ in range(NT + 2)]
        A_sil = [[0] * 32 for _ in range(NT + 2)]
        A_anc = [[0] * 16 for _ in range(NT + 2)]
        P_uq2 = [[0] * 16 for _ in range(NT + 2)]
        PE_w2c = [[0] * 32 for _ in range(NT + 2)]
        SW_w2t = [[None] * 32 for _ in range(NT + 2)]
        PE_w2, V_bc2, A_ffevD, A_ffevE, V_chF = Z(), Z(), Z(), Z(), Z()
        P_out, SO_cnt = Z2(), Z()

        def emit_ln1_prep(i, wait_dv=0):
            if i >= ntr:
                return
            q.wait("a", *si_x[i])
            q.wait("a", "dv", wait_dv)
            jw = PE_T1[i - 1][1] if i > 0 else 0
            if jw:
                q.wait("a", "pe", jw)
            q.drain("a")
            for s in range(NSUB):
                q.op("a", lambda e, S, s=s, i=i: e.activation(
                    xin1tm[:, 1024 * s:1024 * (s + 1)], xinv[:, i % 2, s, :],
                    AF.Identity, accum_out=sc("SUM1", s)), sem="ac")
            q.drain("a")
            for s in range(NSUB):
                q.op("a", lambda e, S, s=s, i=i: e.activation(
                    xin1tm[:, 1024 * s:1024 * (s + 1)], xinv[:, i % 2, s, :],
                    AF.Square, accum_out=sc("SSQ1", s)), sem="ac")
            a_sums = q.cnt["ac"]
            q.wait("v", "ac", a_sums)
            if i == 0:
                q.wait("v", "dv", dv_dp2)
            g_ch = stat_chain("v", "1", 0, 2)
            q.wait("a", "dv", g_ch)
            for s in range(NSUB):
                q.op("a", lambda e, S, s=s, i=i: e.activation(
                    xin1tm[:, 1024 * s:1024 * (s + 1)], xinv[:, i % 2, s, :],
                    AF.Identity, bias=sc("B1", s), scale=sc("R1", s)), sem="ac")
            A_xin1[i] = q.cnt["ac"]

        def attn_gen(i):
            par = i % 2

            if i == 0:
                emit_ln1_prep(0)
                q.wait("t", "dv", dv_id)
                q.wait("t", "dv", dv_attn_w)
            q.wait("t", "ac", A_xin1[i])
            if i > 0:
                q.wait("t", "ac", A_evT2[i - 1][1])
            ev0 = [0]
            for s in range(NSUB):
                if s == 1:
                    q.wait("t", "ac", ev0[0])
                for cc in range(8):
                    PE_T1[i][s] = q.op("t", lambda e, S, s=s, cc=cc: e.matmul(
                        pC[:, 128 * cc:128 * (cc + 1)],
                        xin1tm[:, 1024 * s + 128 * cc:1024 * s + 128 * (cc + 1)],
                        identb[:, :], start=True, stop=True),
                        sem="pe" if cc == 7 else None)
                q.wait("a", "pe", PE_T1[i][s])
                if i > 0 and s == 0:
                    q.wait("a", "pe", PE_sel[i - 1])
                ev = q.op("a", lambda e, S, s=s: e.activation(
                    fm1v[:, :, s, :], pCv[:, :, :], AF.Identity), sem="ac")
                if s == 0:
                    ev0[0] = ev
            A_evT1[i] = q.cnt["ac"]
            yield

            q.wait("t", "ac", A_evT1[i])
            if i > 1:
                q.wait("t", "dv", V_bc2[i - 2])
            for n in range(2):
                for k in range(8):
                    PE_u[i] = q.op("t", lambda e, S, n=n, k=k: e.matmul(
                        pB[:, 256 * n:256 * (n + 1)],
                        winT[:, 256 * k + 128 * n:256 * k + 128 * (n + 1)],
                        xin1fm[:, 256 * k:256 * (k + 1)],
                        start=(k == 0), stop=(k == 7)),
                        sem="pe" if (n == 1 and k == 7) else None)
            for m in range(4):
                for k in range(8):
                    PE_sel[i] = q.op("t", lambda e, S, m=m, k=k: e.matmul(
                        pC[:, 256 * m:256 * (m + 1)],
                        wselT[:, 512 * k + 128 * m:512 * k + 128 * (m + 1)],
                        xin1fm[:, 256 * k:256 * (k + 1)],
                        start=(k == 0), stop=(k == 7)),
                        sem="pe" if (m == 3 and k == 7) else None)
            q.wait("a", "pe", PE_u[i])
            if i > 0:
                q.wait("a", "dv", V_y[i - 1])
            A_u[i] = q.op("a", lambda e, S: e.activation(
                u_sb[:, :], pB[:, :], AF.Identity), sem="ac")
            q.wait("a", "pe", PE_sel[i])
            if i == 0:
                q.wait("a", "swi", sw_w2)
            if i > 0:
                q.wait("a", "pe", PE_g2[i - 1])
            A_selh[i] = q.op("a", lambda e, S: e.activation(
                selh[:, :], pC[:, :], AF.Silu), sem="ac")
            yield

            q.wait("t", "ac", A_selh[i])
            for gi, m in enumerate((0, 1, 6, 7)):
                for k in range(4):
                    PE_g1[i] = q.op("t", lambda e, S, gi=gi, m=m, k=k: e.matmul(
                        pC[:, 256 * gi:256 * (gi + 1)],
                        wsoT[:, 1024 * k + 128 * m:1024 * k + 128 * (m + 1)],
                        selh[:, 256 * k:256 * (k + 1)],
                        start=(k == 0), stop=(k == 3)),
                        sem="pe" if (gi == 3 and k == 3) else None)
            yield

            q.wait("a", "pe", PE_g1[i])
            if i > 0:
                q.wait("a", "dv", V_y[i - 1])
            A_tad[i] = q.op("a", lambda e, S: e.activation(
                tad[:, :], pC[:, :], AF.Tanh, scale=0.5), sem="ac")
            q.drain("a")
            A_aff[i] = q.op("a", lambda e, S: e.activation(
                tad[:, 0:512], tad[:, 0:512], AF.Identity,
                scale=halfc, bias=halfc),
                sem="ac")
            q.wait("t", "ac", A_tad[i])
            for gi, m in enumerate((2, 3, 4, 5)):
                for k in range(4):
                    PE_g2[i] = q.op("t", lambda e, S, gi=gi, m=m, k=k: e.matmul(
                        pC[:, 256 * gi:256 * (gi + 1)],
                        wsoT[:, 1024 * k + 128 * m:1024 * k + 128 * (m + 1)],
                        selh[:, 256 * k:256 * (k + 1)],
                        start=(k == 0), stop=(k == 3)),
                        sem="pe" if (gi == 3 and k == 3) else None)
            q.wait("a", "pe", PE_g2[i])
            A_tbc[i] = q.op("a", lambda e, S: e.activation(
                tbc[:, :], pC[:, :], AF.Tanh), sem="ac")
            q.wait("v", "ac", A_aff[i])
            q.wait("v", "ac", A_u[i])
            if i > 0:
                q.wait("v", "pe", PE_ao[i - 1][1])
            q.drain("v")
            q.op("v", lambda e, S: e.scalar_tensor_tensor(
                tad[:, 512:1024], tad[:, 512:1024], 1.0, u_sb[:, :], OP.add, OP.mult))
            q.wait("v", "ac", A_tbc[i])
            q.op("v", lambda e, S: e.tensor_tensor(
                tbc[:, 0:512], tbc[:, 0:512], u_sb[:, :], OP.mult))
            q.drain("v")
            hof, hofp = 512 * par, 512 * (1 - par)
            for n in range(2):
                init = 0.0 if i == 0 else hsc[:, hofp + 256 * n + 255:hofp + 256 * n + 256]
                q.op("v", lambda e, S, n=n, init=init, hof=hof: e.tensor_tensor_scan(
                    hsc[:, hof + 256 * n:hof + 256 * (n + 1)],
                    tad[:, 256 * n:256 * (n + 1)],
                    tbc[:, 256 * n:256 * (n + 1)],
                    init, OP.mult, OP.add))
            q.drain("v")
            q.op("v", lambda e, S, hof=hof: e.tensor_tensor(
                tbc[:, 512:1024], tbc[:, 512:1024], hsc[:, hof:hof + 512], OP.mult))
            q.drain("v")
            for n in range(2):
                V_y[i] = q.op("v", lambda e, S, n=n: e.scalar_tensor_tensor(
                    ysb[:, 256 * n:256 * (n + 1)],
                    tad[:, 512 + 256 * n:512 + 256 * (n + 1)],
                    dp2[:, n:n + 1],
                    tbc[:, 512 + 256 * n:512 + 256 * (n + 1)],
                    OP.mult, OP.add), sem="dv")
            yield

            q.wait("t", "dv", V_y[i])
            q.wait("t", "ac", A_tbc[i])
            for s in range(NSUB):
                bank = pC if s == 0 else pD
                if s == 1:
                    if i >= 3:
                        q.wait("t", "ac", A_ffevD[i - 2])
                    elif i >= 1:
                        q.wait("t", "ac", A_aoev[i - 1][1])
                for n in range(2):
                    for jj in range(2):
                        PE_ao[i][s] = q.op("t",
                            lambda e, S, bank=bank, s=s, n=n, jj=jj: e.matmul(
                                bank[:, 512 * jj:512 * (jj + 1)],
                                ysb[:, 256 * n + 128 * s:256 * n + 128 * (s + 1)],
                                woutT[:, 1024 * n + 512 * jj:1024 * n + 512 * (jj + 1)],
                                start=(n == 0), stop=(n == 1)),
                            sem="pe" if (n == 1 and jj == 1) else None)
            q.wait("a", "pe", PE_ao[i][0])
            if i > 0:
                q.wait("a", "dv", P_x2[i - 1][0])
            A_aoev[i][0] = q.op("a", lambda e, S: e.activation(
                stgA[:, :], pC[:, 0:1024], AF.Identity, accum_out=sc("SUM2", 0)),
                sem="ac")
            q.wait("a", "pe", PE_ao[i][1])
            if i > 1 and SO_cnt[i - 2]:
                q.wait("a", "so", SO_cnt[i - 2])
            if i > 0:
                q.wait("a", "dv", P_x2[i - 1][1])
            A_aoev[i][1] = q.op("a", lambda e, S: e.activation(
                stgF[:, 0:1024], pD[:, 0:1024], AF.Identity, accum_out=sc("SUM2", 1)),
                sem="ac")
            q.drain("a")
            q.op("a", lambda e, S: e.activation(
                xin1tm[:, 0:1024], stgA[:, :], AF.Square, accum_out=sc("SSQ2", 0)),
                sem="ac")
            q.op("a", lambda e, S: e.activation(
                xin1tm[:, 1024:2048], stgF[:, 0:1024], AF.Square,
                accum_out=sc("SSQ2", 1)), sem="ac")
            a_sq2 = q.cnt["ac"]
            q.wait("g", "ac", a_sq2)
            g_ch2 = stat_chain("g", "2", 0, 2)
            yield

            yield

            q.wait("a", "gp", g_ch2)
            for s in range(NSUB):
                dst = stgA[:, :] if s == 0 else stgF[:, 0:1024]
                A_t[i][s] = q.op("a", lambda e, S, s=s, dst=dst: e.activation(
                    dst, dst, AF.Identity, bias=sc("B2", s), scale=sc("R2", s)),
                    sem="ac")
            for s in range(NSUB):
                q.wait("v", "ac", A_t[i][s])
                if i > 1:
                    q.wait("v", "gp", P_out[i - 2][1 - s])
                q.drain("v")
                src = stgA[:, :] if s == 0 else stgF[:, 0:1024]
                P_x2[i][s] = q.op("v", lambda e, S, s=s, src=src, i=i:
                                  e.tensor_tensor(x2s(i, s), xinv[:, i % 2, s, :],
                                                  src, OP.add), sem="dv")
            issue_x_dma(i + 2, wait_gp=P_x2[i][1], wait_ac=A_xin1[i])
            q.drain("v")
            for s in range(NSUB):
                q.op("v", lambda e, S, s=s, i=i: e.reduce_max(
                    sc("MX", s), x2s(i, s), AX))
                q.op("v", lambda e, S, s=s, i=i: e.tensor_reduce(
                    sc("MN", s), x2s(i, s), AX, OP.min))
            for s in range(NSUB):
                q.wait("a", "dv", P_x2[i][s])
                q.op("a", lambda e, S, s=s, i=i: e.activation(
                    xin1tm[:, 1024 * s:1024 * (s + 1)], x2s(i, s), AF.Square,
                    accum_out=sc("SSQ3", s)), sem="ac")
            a_sum3 = q.cnt["ac"]
            q.drain("v")
            q.op("v", lambda e, S: e.tensor_tensor(
                sc("QB", 0, 2), sc("R2", 0, 2), sc("SUM2", 0, 2), OP.mult))
            q.drain("v")
            q.op("v", lambda e, S: e.tensor_tensor(
                sc("QB", 0, 2), sc("QB", 0, 2), sc("SUM1", 0, 2), OP.add))
            q.drain("v")
            q.op("v", lambda e, S: e.tensor_scalar(
                sc("QC", 0, 2), sc("B2", 0, 2), float(D), None, OP.mult))
            q.drain("v")
            v_sum3 = q.op("v", lambda e, S: e.tensor_tensor(
                sc("SUM3", 0, 2), sc("QB", 0, 2), sc("QC", 0, 2), OP.add), sem="dv")
            emit_ln1_prep(i + 1, wait_dv=v_sum3)
            q.wait("v", "ac", a_sum3)
            q.drain("v")
            q.op("v", lambda e, S: e.tensor_scalar(
                sc("M3", 0, 2), sc("SUM3", 0, 2), -1.0 / D, None, OP.mult))
            q.op("v", lambda e, S: e.tensor_scalar(
                sc("E3", 0, 2), sc("SSQ3", 0, 2), 1.0 / D, None, OP.mult))
            q.drain("v")
            q.op("v", lambda e, S: e.tensor_tensor(
                sc("QB", 0, 2), sc("M3", 0, 2), sc("M3", 0, 2), OP.mult))
            q.drain("v")
            q.op("v", lambda e, S: e.tensor_tensor(
                sc("E3", 0, 2), sc("E3", 0, 2), sc("QB", 0, 2), OP.subtract))
            q.drain("v")
            q.op("v", lambda e, S: e.tensor_scalar(
                sc("E3", 0, 2), sc("E3", 0, 2), EPS, None, OP.add))
            q.drain("v")
            quake("v", sc("R3", 0, 2), sc("E3", 0, 2), seedi, 2, ("QA", "QB", "QC"))
            q.drain("v")
            q.op("v", lambda e, S: e.tensor_tensor(
                sc("TA", 0, 2), sc("MX", 0, 2), sc("M3", 0, 2), OP.add))
            q.op("v", lambda e, S: e.tensor_tensor(
                sc("TB", 0, 2), sc("MN", 0, 2), sc("M3", 0, 2), OP.add))
            q.drain("v")
            q.op("v", lambda e, S: e.tensor_tensor(
                sc("TA", 0, 2), sc("TA", 0, 2), sc("R3", 0, 2), OP.mult))
            q.op("v", lambda e, S: e.tensor_scalar(
                sc("TB", 0, 2), sc("TB", 0, 2), -1.0, None, OP.mult))
            q.drain("v")
            q.op("v", lambda e, S: e.tensor_tensor(
                sc("TB", 0, 2), sc("TB", 0, 2), sc("R3", 0, 2), OP.mult))
            q.drain("v")
            q.op("v", lambda e, S: e.tensor_tensor(
                sc("MABS", 0, 2), sc("TA", 0, 2), sc("TB", 0, 2), OP.max))
            q.drain("v")
            q.op("v", lambda e, S: e.tensor_scalar(
                sc("MABS", 0, 2), sc("MABS", 0, 2), 1e-5, None, OP.max))
            q.drain("v")
            q.op("v", lambda e, S: e.reciprocal(sc("SC1", 0, 2), sc("MABS", 0, 2)))
            q.drain("v")
            q.op("v", lambda e, S: e.tensor_scalar(
                sc("SC1", 0, 2), sc("SC1", 0, 2), 127.0, None, OP.mult))
            q.drain("v")
            q.op("v", lambda e, S: e.tensor_tensor(
                sc("A1", 0, 2), sc("R3", 0, 2), sc("SC1", 0, 2), OP.mult))
            q.drain("v")
            q.op("v", lambda e, S: e.tensor_tensor(
                sc("B1Q", 0, 2), sc("M3", 0, 2), sc("A1", 0, 2), OP.mult))
            q.drain("v")
            q.op("v", lambda e, S: e.tensor_scalar(
                sc("B1Q", 0, 2), sc("B1Q", 0, 2), RC, None, OP.add))
            if i == 0:
                q.wait("v", "dv", dv_w1)
            q.drain("v")
            V_q1sc[i] = q.op("v", lambda e, S: e.tensor_scalar(
                sc("G1", 0, 2), sc("MABS", 0, 2), wcons[:, 0:1], 1.0 / 127.0,
                OP.mult, OP.mult), sem="dv")
            yield

            q.wait("a", "dv", V_q1sc[i])
            q.wait("g", "dv", V_q1sc[i])
            if i > 0:
                q.wait("g", "pe", PE_T2[i - 1][1])
            if i == 0:
                q.wait("g", "swi", sw_w2)
            for s in range(NSUB):
                for h in range(2):
                    hi = 2 * s + h
                    if hi > 0:
                        q.wait("a", "gp", P_uq1[i][hi - 1])
                    a_an = q.op("a", lambda e, S, s=s, h=h, i=i: e.activation(
                        stgi[:, :], x2sh(i, s, h), AF.Identity,
                        scale=sc("A1", s), bias=sc("B1Q", s)), sem="ac")
                    q.wait("g", "ac", a_an)
                    q.drain("g")
                    bsl = stgb[:, 1024 * s + 512 * h:1024 * s + 512 * (h + 1)]
                    q.op("g", lambda e, S, bsl=bsl: e.tensor_scalar(
                        bsl, stgi[:, :],
                        onec, rccn, OP.mult, OP.add))
                    q.drain("g")
                    P_uq1[i][hi] = q.op("g", lambda e, S, s=s, bsl=bsl: e.tensor_scalar(
                        bsl, bsl,
                        sc("G1", s), zeroc, OP.mult, OP.add), sem="gp")
                q.wait("t", "gp", P_uq1[i][2 * s + 1])
                q.wait("t", "ac", A_aoev[i][0] if s == 0 else A_evT2[i][0])
                for cc in range(8):
                    PE_T2[i][s] = q.op("t", lambda e, S, s=s, cc=cc: e.matmul(
                        pC[:, 128 * cc:128 * (cc + 1)],
                        stgb[:, 1024 * s + 128 * cc:1024 * s + 128 * (cc + 1)],
                        identb[:, :], start=True, stop=True),
                        sem="pe" if cc == 7 else None)
                q.wait("a", "pe", PE_T2[i][s])
                if s == 0 and i > 0:
                    q.wait("a", "pe", PE_w1[i - 1][31])
                if s == 0:
                    dst = xqv[:, par, :, 129:257]
                else:
                    dst = xqv[:, 1 - par, :, 1:129]
                q.op("a", lambda e, S, dst=dst: e.activation(
                    dst, pCv[:, :, :], AF.Identity), sem="ac")
                A_evT2[i][s] = q.cnt["ac"]
            q.wait("v", "ac", A_evT2[i][1])
            if i > 0:
                q.wait("v", "pe", PE_w1[i - 1][31])
            q.drain("v")
            q.op("v", lambda e, S, par=par: e.tensor_copy(
                xqv[:, 1 - par, :, 0], xqv[:, par, :, 256]))
            V_halo[i] = q.op("v", lambda e, S, par=par: e.tensor_copy(
                xqv[:, par, :, 257], xqv[:, 1 - par, :, 1]), sem="dv")
            if i == ntr - 1:
                q.drain("v")
                V_halo[i + 1] = q.op("v", lambda e, S: e.memset(
                    xqv[:, ntr % 2, :, 129:258], 0.0), sem="dv")

        def win_gen(f):
            pf = f % 2
            have_a = f > 0
            have_b = f < ntr
            subs = ([0] if have_a else []) + ([1] if have_b else [])
            emit_z8b(f - 1)

            def issue_w2t(c):
                if c >= 32:
                    return
                if f == 0 and c < 4:
                    q.wait("sy", "swi", sw_w2)
                if c >= 4:
                    q.wait("sy", "pe", PE_w2c[f][c - 4])
                elif f > 0:
                    q.wait("sy", "pe", PE_w2c[f - 1][28 + c])
                swn = "sw%d" % (c % 4)
                SW_w2t[f][c] = (swn, q.op("sy", lambda e, S, c=c: e.dma_start(
                    out=w2sb[:, 1024 * (c % 4):1024 * (c % 4 + 1)],
                    in_=w2s[:, 1024 * c:1024 * (c + 1)]), sem=swn, k=16))

            issue_w2t(0)
            issue_w2t(1)
            issue_w2t(2)
            issue_w2t(3)

            q.wait("t", "dv", V_halo[f])
            if f == 0:
                q.wait("t", "dv", dv_w1)
                q.wait("t", "dv", dv_pad0)
            if f > 0:
                q.wait("t", "ac", A_ffevE[f - 1])
            for c in range(32):
                off = 512 * (c % 3)
                if c >= 3:
                    q.wait("t", "ac", A_evh[f][c - 3])
                for k in range(8):
                    PE_w1[f][c] = q.op("t",
                        lambda e, S, c=c, k=k, off=off, pf=pf: e.matmul(
                            pE[:, off:off + 258],
                            w1t[:, 4096 * k + 128 * c:4096 * k + 128 * (c + 1)],
                            xqv[:, pf, k, :],
                            start=(k == 0), stop=(k == 7)),
                        sem="pe" if k == 7 else None)
                q.wait("a", "pe", PE_w1[f][c])
                if c >= 4:
                    q.wait("a", "dv", V_cv[f][c - 4])
                q.op("a", lambda e, S, c=c, off=off: e.activation(
                    hpre[:, 258 * (c % 4):258 * (c % 4 + 1)], pE[:, off:off + 258],
                    AF.Identity), sem="ac")
                if c % 2 == 0:
                    if c == 0 and f > 0:
                        q.wait("a", "pe", PE_w2[f - 1])
                    q.op("a", lambda e, S, c=c, off=off: e.activation(
                        h4[:, 256 * c:256 * (c + 1)], pE[:, off + 1:off + 257],
                        AF.Identity, scale=cwt[:, 3 * c + 1:3 * c + 2],
                        bias=cbt[:, c:c + 1]), sem="ac")
                A_evh[f][c] = q.cnt["ac"]
                if c % 4 == 3:
                    if c >= 7:
                        cp = c - 4
                        q.wait("a", "dv", V_cv[f][cp])
                        A_sil[f][cp - 3] = A_sil[f][cp] = q.op(
                            "a", lambda e, S, cp=cp: e.activation(
                                h4[:, 256 * (cp - 3):256 * (cp + 1)],
                                h4[:, 256 * (cp - 3):256 * (cp + 1)], AF.Silu),
                            sem="ac")
                    q.wait("v", "ac", A_evh[f][c])
                    if c == 3 and f > 0:
                        q.wait("v", "pe", PE_w2[f - 1])
                    for cc in range(c - 2, c + 1, 2):
                        hs = hpre[:, 258 * (cc % 4):258 * (cc % 4 + 1)]
                        hd = h4[:, 256 * cc:256 * (cc + 1)]
                        q.op("v", lambda e, S, cc=cc, hs=hs, hd=hd: e.tensor_scalar(
                            hd, hs[:, 1:257], cwt[:, 3 * cc + 1:3 * cc + 2],
                            cbt[:, cc:cc + 1], OP.mult, OP.add))
                    q.drain("v")
                    for cc in range(c - 3, c + 1):
                        hs = hpre[:, 258 * (cc % 4):258 * (cc % 4 + 1)]
                        hd = h4[:, 256 * cc:256 * (cc + 1)]
                        q.op("v", lambda e, S, cc=cc, hs=hs, hd=hd:
                             e.scalar_tensor_tensor(
                                 hd, hs[:, 0:256], cwt[:, 3 * cc:3 * cc + 1], hd,
                                 OP.mult, OP.add))
                    q.drain("v")
                    for cc in range(c - 3, c + 1):
                        hs = hpre[:, 258 * (cc % 4):258 * (cc % 4 + 1)]
                        hd = h4[:, 256 * cc:256 * (cc + 1)]
                        q.op("v", lambda e, S, cc=cc, hs=hs, hd=hd:
                             e.scalar_tensor_tensor(
                                 hd, hs[:, 2:258], cwt[:, 3 * cc + 2:3 * cc + 3],
                                 hd, OP.mult, OP.add),
                             sem="dv" if cc == c else None)
                    for cc in range(c - 3, c + 1):
                        V_cv[f][cc] = q.cnt["dv"]
                    if c >= 7:
                        cp = c - 4
                        q.wait("v", "ac", A_sil[f][cp])
                        q.drain("v")
                        for hh in range(4):
                            gsl = h4[:, 256 * (cp - 3 + hh):256 * (cp - 2 + hh)]
                            if cp == 3 and hh == 0:
                                q.op("v", lambda e, S, gsl=gsl: e.tensor_copy(
                                    accP[:, :], gsl))
                                q.op("v", lambda e, S, gsl=gsl: e.tensor_copy(
                                    accN[:, :], gsl))
                                q.drain("v")
                            else:
                                q.op("v", lambda e, S, gsl=gsl: e.tensor_tensor(
                                    accP[:, :], accP[:, :], gsl, OP.max))
                                q.op("v", lambda e, S, gsl=gsl: e.tensor_tensor(
                                    accN[:, :], accN[:, :], gsl, OP.min))
                                q.drain("v")
                if c % 8 == 7:
                    yield

            q.wait("a", "dv", V_cv[f][31])
            A_sil[f][28] = A_sil[f][31] = q.op(
                "a", lambda e, S: e.activation(
                    h4[:, 256 * 28:256 * 32], h4[:, 256 * 28:256 * 32], AF.Silu),
                sem="ac")
            q.wait("v", "ac", A_sil[f][31])
            q.drain("v")
            for cc in range(28, 32):
                gsl = h4[:, 256 * cc:256 * (cc + 1)]
                q.op("v", lambda e, S, gsl=gsl: e.tensor_tensor(
                    accP[:, :], accP[:, :], gsl, OP.max))
                q.op("v", lambda e, S, gsl=gsl: e.tensor_tensor(
                    accN[:, :], accN[:, :], gsl, OP.min))
                q.drain("v")
            v_mab = q.op("v", lambda e, S: e.scalar_tensor_tensor(
                accP[:, :], accN[:, :], -1.0, accP[:, :], OP.mult, OP.max),
                sem="dv")
            q.wait("t", "dv", v_mab)
            if f + 1 < ntr:
                q.wait("t", "ac", A_u[f + 1])
            else:
                q.wait("t", "dv", V_bc2[f - 1])
            for s in range(2):
                pe_mabT = q.op("t", lambda e, S, s=s: e.matmul(
                    pB[:, 128 * s:128 * (s + 1)],
                    accP[:, 128 * s:128 * (s + 1)],
                    identb[:, :], start=True, stop=True),
                    sem="pe" if s == 1 else None)
            q.wait("v", "pe", pe_mabT)
            for s in range(2):
                q.op("v", lambda e, S, s=s: e.reduce_max(
                    sc("MAB2", s), pB[:, 128 * s:128 * (s + 1)], AX))
            q.drain("v")
            q.op("v", lambda e, S: e.tensor_scalar(
                sc("MAB2", 0, 2), sc("MAB2", 0, 2), 1e-5, None, OP.max))
            q.drain("v")
            q.op("v", lambda e, S: e.reciprocal(sc("SC2", 0, 2), sc("MAB2", 0, 2)))
            q.drain("v")
            q.op("v", lambda e, S: e.tensor_scalar(
                sc("SC2", 0, 2), sc("SC2", 0, 2), 127.0, None, OP.mult))
            if f == 0:
                q.wait("v", "dv", dv_w2)
            q.op("v", lambda e, S: e.tensor_scalar(
                sc("G2", 0, 2), sc("MAB2", 0, 2), wcons[:, 2:3], 1.0 / 127.0,
                OP.mult, OP.mult))
            q.drain("v")
            v_sc2 = q.op("v", lambda e, S: e.tensor_copy(sc("QC"), sc("SC2")),
                         sem="dv")
            q.wait("t", "dv", v_sc2)
            for s in range(2):
                pe_r = q.op("t", lambda e, S, s=s: e.matmul(
                    pB[0:1, 256 + 128 * s:256 + 128 * (s + 1)], sc("SC2", s),
                    identf[:, :], start=True, stop=True),
                    sem="pe" if s == 1 else None)
            q.wait("v", "pe", pe_r)
            v_row = q.op("v", lambda e, S: e.tensor_copy(
                tad[0:1, 0:256], pB[0:1, 256:512]), sem="dv")
            q.wait("t", "dv", v_row)
            pe_bc = q.op("t", lambda e, S: e.matmul(
                pB[:, 256:512], ones[0:1, 0:128], tad[0:1, 0:256],
                start=True, stop=True), sem="pe")
            q.wait("v", "pe", pe_bc)
            V_bc2[f] = q.op("v", lambda e, S: e.tensor_copy(bc2[:, :], pB[:, 256:512]),
                            sem="dv")
            yield

            q.wait("t", "dv", V_bc2[f])
            if f + 1 < ntr:
                q.wait("t", "ac", A_aoev[f + 1][1])
            elif f > 0:
                q.wait("t", "ac", A_ffevD[f - 1])
            q.wait("t", "ac", A_evh[f][31])
            def emit_pair(p2):
                c = 2 * p2
                if p2 == 0:
                    q.wait("v", "dv", V_bc2[f])
                    q.drain("v")
                vtt = 0
                for cc in (c, c + 1):
                    hsl4 = h4[:, 256 * cc:256 * (cc + 1)]
                    vtt = q.op("v", lambda e, S, hsl4=hsl4: e.tensor_tensor(
                        hsl4, hsl4, bc2[:, :], OP.mult),
                        sem="dv" if cc == c + 1 else None)
                q.wait("a", "dv", vtt)
                if p2 >= 2:
                    q.wait("a", "gp", P_uq2[f][p2 - 2])
                elif f > 0:
                    q.wait("a", "gp", P_uq2[f - 1][14 + p2])
                ksl = k2i[:, 512 * (p2 % 2):512 * (p2 % 2 + 1)]
                psl = h4[:, 512 * p2:512 * (p2 + 1)]
                A_anc[f][p2] = q.op("a", lambda e, S, ksl=ksl, psl=psl:
                                    e.activation(ksl, psl, AF.Identity,
                                                 bias=rcc), sem="ac")
                q.wait("g", "ac", A_anc[f][p2])
                P_uq2[f][p2] = q.op("g", lambda e, S, ksl=ksl, psl=psl:
                                    e.tensor_scalar(psl, ksl, onec,
                                                    rccn, OP.mult, OP.add),
                                    sem="gp")

            for p2 in range(6):
                emit_pair(p2)
            for c in range(32):
                if c >= 12 and c % 2 == 0:
                    emit_pair(c // 2)
                q.wait("t", "gp", P_uq2[f][c // 2])
                q.wait("t", *SW_w2t[f][c])
                for s in subs:
                    bank = pD if s == 0 else pE
                    for jj in range(2):
                        coff = 1024 * (c % 4) + 512 * jj
                        mm = q.op("t", lambda e, S, c=c, s=s, jj=jj, bank=bank,
                                  coff=coff: e.matmul(
                                      bank[:, 512 * jj:512 * (jj + 1)],
                                      h4[:, 256 * c + 128 * s:256 * c + 128 * (s + 1)],
                                      w2sb[:, coff:coff + 512],
                                      start=(c == 0), stop=(c == 31)),
                                  sem="pe" if (s == subs[-1] and jj == 1) else None)
                PE_w2c[f][c] = mm
                issue_w2t(c + 4)
                if c == 11:
                    yield
            PE_w2[f] = PE_w2c[f][31]
            yield

            q.wait("a", "pe", PE_w2[f])
            if SO_cnt[f - 1]:
                q.wait("a", "so", SO_cnt[f - 1])
            if have_b:
                q.wait("a", "dv", P_x2[f][1])
                A_ffevE[f] = q.op("a", lambda e, S: e.activation(
                    stgF[:, 1024:2048], pE[:, 0:1024], AF.Identity,
                    accum_out=sc("SUMF", 1)), sem="ac")
            if have_a:
                A_ffevD[f] = q.op("a", lambda e, S: e.activation(
                    stgF[:, 0:1024], pD[:, 0:1024], AF.Identity,
                    accum_out=sc("SUMF", 0)), sem="ac")
            q.drain("a")
            q.wait("a", "dv", P_x2[min(f + 1, ntr - 1)][0])
            for s in subs:
                src_ = stgF[:, 1024 * s:1024 * (s + 1)]
                q.op("a", lambda e, S, s=s, src_=src_: e.activation(
                    stgA[:, :], src_, AF.Square, accum_out=sc("SSQF", s)), sem="ac")
                q.drain("a")
            a_sqf = q.cnt["ac"]
            q.wait("v", "ac", a_sqf)
            V_chF[f] = stat_chain("v", "F", subs[0], len(subs), var_scale_col="G2")

        def emit_z8b(f):
            if f < 0:
                return
            have_a = f > 0
            have_b = f < ntr
            subs = ([0] if have_a else []) + ([1] if have_b else [])
            q.wait("a", "dv", V_chF[f])
            a_apF = 0
            for s in subs:
                dst = stgF[:, 1024 * s:1024 * (s + 1)]
                a_apF = q.op("a", lambda e, S, s=s, dst=dst: e.activation(
                    dst, dst, AF.Identity, bias=sc("BF", s), scale=sc("RF", s)),
                    sem="ac")
            q.wait("g", "ac", a_apF)
            if have_a:
                q.wait("g", "dv", P_x2[f - 1][1])
            if have_b:
                q.wait("g", "dv", P_x2[f][0])
            q.drain("g")
            for s in subs:
                dst = stgF[:, 1024 * s:1024 * (s + 1)]
                x2src = x2s(f - 1, 1) if s == 0 else x2s(f, 0)
                P_out[f][s] = q.op("g", lambda e, S, dst=dst, x2src=x2src:
                                   e.tensor_tensor(dst, x2src, dst, OP.add),
                                   sem="gp")
                rows = ((f - 1) * TT + 128) if s == 0 else f * TT
                q.wait("sy", "gp", P_out[f][s])
                dstH = out_hbm[rows:rows + 128, :]
                SO_cnt[f] = q.op("sy", lambda e, S, dstH=dstH, dst=dst: e.dma_start(
                    out=dstH, in_=dst), sem="so", k=16)

        def run_interleaved(g1, g2):
            gens = [g1, g2]
            while gens:
                for g in list(gens):
                    try:
                        next(g)
                    except StopIteration:
                        gens.remove(g)

        for i in range(ntr):
            if i > 0:
                run_interleaved(attn_gen(i), win_gen(i - 1))
            else:
                for _ in attn_gen(0):
                    pass
        if ntr > 0:
            for _ in win_gen(ntr - 1):
                pass
            for _ in win_gen(ntr):
                pass
            emit_z8b(ntr)
        for _s, _v in q.cnt.items():
            q.wait("sy", _s, _v)

        with nc.Block() as block:
            @block.sync
            def _(eng):
                for fn in q.q["sy"]:
                    fn(eng, sems)

            @block.vector
            def _(eng):
                for fn in q.q["v"]:
                    fn(eng, sems)

            @block.scalar
            def _(eng):
                for fn in q.q["a"]:
                    fn(eng, sems)

            @block.tensor
            def _(eng):
                for fn in q.q["t"]:
                    fn(eng, sems)

            @block.gpsimd
            def _(eng):
                for fn in q.q["g"]:
                    fn(eng, sems)

    return nc


def _host_layout(inputs):
    f32 = np.float32

    def fm(w, nchunk):
        i, o = w.shape
        return np.ascontiguousarray(
            w.reshape(nchunk, 128, o).transpose(1, 0, 2).reshape(128, nchunk * o))

    W_in = np.asarray(inputs["L0_attn_W_in"], f32)
    W_sel = np.asarray(inputs["L0_attn_W_sel_in"], f32)
    W_so = np.asarray(inputs["L0_attn_W_sel_out"], f32)
    W_out = np.asarray(inputs["L0_attn_W_out"], f32)
    w1 = np.asarray(inputs["L0_ffn_w1"], f32)
    w2 = np.asarray(inputs["L0_ffn_w2"], f32)
    d_param = np.asarray(inputs["L0_attn_d_param"], f32)
    cw = np.asarray(inputs["L0_ffn_conv_w"], f32)[:, 0, :]
    cb = np.asarray(inputs["L0_ffn_conv_b"], f32)
    return {
        "winT": fm(np.ascontiguousarray(W_in.T), 8),
        "wselT": fm(np.ascontiguousarray(W_sel.T), 8),
        "wsoT": fm(np.ascontiguousarray(W_so.T), 4),
        "woutT": fm(np.ascontiguousarray(W_out.T), 2),
        "w1T": fm(np.ascontiguousarray(w1.T), 8),
        "w2T": fm(np.ascontiguousarray(w2.T), 32),
        "dp": np.ascontiguousarray(d_param.reshape(2, 128).T),
        "cw": np.ascontiguousarray(cw.reshape(32, 128, 3).transpose(1, 0, 2).reshape(128, 96)),
        "cb": np.ascontiguousarray(cb.reshape(32, 128).T),
    }


def kernel(**inputs):
    x = np.ascontiguousarray(np.asarray(inputs["L0_attn_x"], np.float32))
    wd = _host_layout(inputs)
    nc = build_program()
    in_maps = []
    for c in range(8):
        m = {"x": np.ascontiguousarray(x[c])}
        m.update(wd)
        in_maps.append(m)
    res = run_bass_kernel_spmd(nc, in_maps, core_ids=list(range(8)))
    return np.stack([np.asarray(r["out"]) for r in res.results], axis=0)


# version 27 (final)
# speedup vs baseline: 1.1976x; 1.0328x over previous
import numpy as np
import concourse.bass as bass
import concourse.mybir as mybir
from concourse.bass_utils import run_bass_kernel_spmd
from concourse.alu_op_type import AluOpType as OP

F32 = mybir.dt.float32
BF16 = mybir.dt.bfloat16
I32 = mybir.dt.int32
AF = mybir.ActivationFunctionType
AX = mybir.AxisListType.X

D = 1024
T_TOK = 4096
TT = 256
NT = T_TOK // TT
NSUB = 2
HD = 4096
NST = 256
SEL = 512
EPS = 1e-5
QMAGIC = 1597463007.0
RC = 12582912.0

_names = ("SUM1 SSQ1 M1 E1 R1 B1 SUM2 SSQ2 M2 E2 R2 B2 "
          "SUM3 SSQ3 M3 E3 R3 MX MN TA TB MABS SC1 A1 B1Q G1 "
          "SUMF SSQF MF EF RF BF GG MAB2 SC2 G2 QA QB QC PA PB PC").split()
C = {n: 2 * k for k, n in enumerate(_names)}
STT_COLS = 2 * len(_names)


class Q:
    ENGS = ("sy", "v", "a", "t", "g")

    def __init__(self):
        self.q = {e: [] for e in self.ENGS}
        self.cnt = {s: 0 for s in ("sm", "sx0", "sx1", "so", "sw0", "sw1",
                                   "swi", "gd", "gp", "pe", "dv", "ac",
                                   "ad", "vd", "sw2", "sw3", "sw4")}

    def do(self, qn, f):
        self.q[qn].append(f)

    def op(self, qn, f, sem=None, k=1):
        if sem is None:
            self.q[qn].append(f)
            return None
        self.cnt[sem] += k
        self.q[qn].append(lambda e, S, f=f, sem=sem, k=k: f(e, S).then_inc(S[sem], k))
        return self.cnt[sem]

    def wait(self, qn, sem, v):
        if not v:
            return
        self.q[qn].append(lambda e, S, sem=sem, v=v: e.wait_ge(S[sem], v))

    def drain(self, qn):
        self.q[qn].append(lambda e, S: e.drain())


def build_program(ntr=NT):
    nc = bass.Bass("TRN2", target_bir_lowering=False)

    x_hbm = nc.dram_tensor("x", [T_TOK, D], F32, kind="ExternalInput").ap()
    winT_h = nc.dram_tensor("winT", [128, 8 * NST], F32, kind="ExternalInput").ap()
    wselT_h = nc.dram_tensor("wselT", [128, 8 * SEL], F32, kind="ExternalInput").ap()
    wsoT_h = nc.dram_tensor("wsoT", [128, 4 * D], F32, kind="ExternalInput").ap()
    woutT_h = nc.dram_tensor("woutT", [128, 2 * D], F32, kind="ExternalInput").ap()
    w1T_h = nc.dram_tensor("w1T", [128, 8 * HD], F32, kind="ExternalInput").ap()
    w2T_h = nc.dram_tensor("w2T", [128, 32 * D], F32, kind="ExternalInput").ap()
    dp_h = nc.dram_tensor("dp", [128, 2], F32, kind="ExternalInput").ap()
    cw_h = nc.dram_tensor("cw", [128, 96], F32, kind="ExternalInput").ap()
    cb_h = nc.dram_tensor("cb", [128, 32], F32, kind="ExternalInput").ap()
    out_hbm = nc.dram_tensor("out", [T_TOK, D], F32, kind="ExternalOutput").ap()
    w2s = nc.dram_tensor("w2s", [128, 32 * D], BF16, kind="Internal").ap()

    q = Q()
    from contextlib import ExitStack
    st_ = ExitStack()
    sb = lambda name, shape, dt: st_.enter_context(nc.sbuf_tensor(name, shape, dt))
    ps_ = lambda name, shape, dt: st_.enter_context(nc.psum_tensor(name, shape, dt))

    with st_:
        identf = sb("identf", [128, 128], F32)
        identb = sb("identb", [128, 128], BF16)
        ones = sb("ones", [128, 128], F32)
        winT = sb("winT_b", [128, 8 * NST], BF16)
        wselT = sb("wselT_b", [128, 8 * SEL], BF16)
        wsoT = sb("wsoT_b", [128, 4 * D], BF16)
        woutT = sb("woutT_b", [128, 2 * D], BF16)
        w1t = sb("w1t", [128, 8 * HD], BF16)
        dp2 = sb("dp2", [128, 2], F32)
        cwt = sb("cwt", [128, 96], F32)
        cbt = sb("cbt", [128, 32], F32)
        wcons = sb("wcons", [128, 4], F32)
        stgA = sb("stgA", [128, 1024], F32)
        stgF = sb("stgF", [128, 2048], F32)
        stgi = sb("stgi", [128, 512], I32)
        stgb = sb("stgb", [128, 1024], BF16)
        xin = sb("xin", [128, 2 * 2048], F32)
        x2 = sb("x2", [128, 5 * 1024], F32)
        xin1tm = sb("xin1tm", [128, 2048], BF16)
        xin1fm = sb("xin1fm", [128, 2048], BF16)
        xqw = sb("xqw", [128, 2 * 8 * 258], BF16)
        u_sb = sb("u_sb", [128, 512], BF16)
        tad = sb("tad", [128, 1024], F32)
        tbc = sb("tbc", [128, 1024], F32)
        hsc = sb("hsc", [128, 2 * 512], F32)
        ysb = sb("ysb", [128, 512], BF16)
        selh = sb("selh", [128, 1024], BF16)
        h4 = sb("h4", [128, 32 * 256], BF16)
        hpre = sb("hpre", [128, 4 * 258], BF16)
        k2i = sb("k2i", [128, 2 * 512], I32)
        bc2 = sb("bc2", [128, 256], BF16)
        accP = sb("accP", [128, 256], BF16)
        accN = sb("accN", [128, 256], BF16)
        w2sb = sb("w2sb", [128, 5120], BF16)
        stt = sb("stt", [128, STT_COLS], F32)
        wtot = stt.ap()[0:1, C["MX"]:C["MX"] + 4]
        seedib = sb("seedib", [128, 4], I32)
        seedi = seedib.ap()[:, 0:2]
        seedp = seedib.ap()[:, 2:4]
        consts = sb("consts", [128, 8], F32)
        rcc = consts.ap()[:, 0:1]
        rccn = consts.ap()[:, 1:2]
        onec = consts.ap()[:, 2:3]
        halfc = consts.ap()[:, 3:4]
        zeroc = consts.ap()[:, 4:5]

        pB = ps_("pB", [128, 512], F32)
        pC = ps_("pC", [128, 1024], F32)
        pD = ps_("pD", [128, 1024], F32)
        pE = ps_("pE", [128, 1536], F32)

        sems = {}
        for s in ("sm", "sx0", "sx1", "so", "sw0", "sw1", "swi", "gd", "gp",
                  "pe", "dv", "ac", "ad", "vd", "sw2", "sw3", "sw4"):
            sems[s] = st_.enter_context(nc.semaphore(s))

        xinv = xin.ap().rearrange("p (q u d) -> p q u d", q=2, u=2)
        fm1v = xin1fm.ap().rearrange("p (c u w) -> p c u w", c=8, u=2)
        pCv = pC.ap().rearrange("p (c w) -> p c w", c=8)
        xqv = xqw.ap().rearrange("p (b c w) -> p b c w", b=2, c=8)

        def sc(col, s=0, n=1):
            return stt[:, C[col] + s:C[col] + s + n]

        def x2slot(i, s):
            return (i % 2) if s == 0 else 2 + (i % 3)

        def x2s(i, s):
            slot = x2slot(i, s)
            return x2[:, 1024 * slot:1024 * (slot + 1)]

        def x2sh(i, s, h):
            slot = x2slot(i, s)
            return x2[:, 1024 * slot + 512 * h:1024 * slot + 512 * (h + 1)]

        q.do("g", lambda e, S: e.memset(ones[:, :], 1.0))
        q.do("g", lambda e, S: e.memset(rcc, RC))
        q.do("g", lambda e, S: e.memset(rccn, -RC))
        q.do("g", lambda e, S: e.memset(onec, 1.0))
        q.do("g", lambda e, S: e.memset(halfc, 0.5))
        q.do("g", lambda e, S: e.memset(zeroc, 0.0))
        q.drain("g")
        q.op("g", lambda e, S: e.affine_select(
            identf[:, :], ones[:, :], [[1, 128]], OP.is_equal, 0.0,
            base=0, channel_multiplier=-1), sem="gp")
        gp_id = q.cnt["gp"]
        q.wait("v", "gp", gp_id)
        dv_id = q.op("v", lambda e, S: e.tensor_copy(identb[:, :], identf[:, :]),
                     sem="dv")
        dv_pad0 = q.op("v", lambda e, S: e.memset(xqv[:, 0, :, 0:129], 0.0),
                       sem="dv")

        for dst, src in ((dp2, dp_h), (cwt, cw_h), (cbt, cb_h)):
            q.op("sy", lambda e, S, dst=dst, src=src: e.dma_start(
                out=dst[:, :], in_=src[:, :]), sem="sm", k=16)
        si_small = q.cnt["sm"]
        q.wait("v", "sm", si_small)
        q.drain("v")
        dv_dp2 = q.op("v", lambda e, S: e.tensor_scalar_mul(dp2[:, :], dp2[:, :], 0.5),
                      sem="dv")

        si_x = [None] * (NT + 2)

        def issue_x_dma(i, wait_gp=0, wait_ac=0):
            if i >= ntr:
                return
            par = i % 2
            src = x_hbm[i * TT:(i + 1) * TT, :].rearrange("(s p) d -> p s d", p=128)
            dst = xinv[:, par, :, :]
            q.wait("sy", "dv", wait_gp)
            q.wait("sy", "ac", wait_ac)
            sxn = "sx%d" % (i % 2)
            si_x[i] = (sxn, q.op("sy", lambda e, S, src=src, dst=dst:
                                 e.dma_start(out=dst, in_=src), sem=sxn, k=16))

        issue_x_dma(0)
        issue_x_dma(1)

        stg_gate = {0: None, 1: None}
        last_dma = {"gd": q.cnt["gd"], "sm": q.cnt["sm"]}

        def piece_dma(p, src_h, gate):
            qn, sem = ("g", "gd") if p % 2 == 0 else ("sy", "sm")
            if gate is None:
                gate = stg_gate[p % 2]
            if gate:
                q.wait(qn, gate[0], gate[1])
            q.wait(qn, sem, last_dma[sem])
            v = q.op(qn, lambda e, S, p=p, src_h=src_h: e.dma_start(
                out=stgF[:, 1024 * (p % 2):1024 * (p % 2 + 1)],
                in_=src_h[:, 1024 * p:1024 * (p + 1)]), sem=sem, k=16)
            last_dma[sem] = v
            return (sem, v)

        def stage_cast(dst, src_h, ncols):
            npc = (ncols + 1023) // 1024
            dmas = [None] * (npc + 2)
            cons = [0] * (npc + 2)
            dmas[0] = piece_dma(0, src_h, None)
            if npc > 1:
                dmas[1] = piece_dma(1, src_h, None)
            for p in range(npc):
                w = min(1024, ncols - 1024 * p)
                q.wait("v", *dmas[p])
                cons[p] = q.op("v", lambda e, S, dst=dst, p=p, w=w: e.tensor_copy(
                    dst[:, 1024 * p:1024 * p + w],
                    stgF[:, 1024 * (p % 2):1024 * (p % 2) + w]), sem="dv")
                stg_gate[p % 2] = ("dv", cons[p])
                if p + 2 < npc:
                    dmas[p + 2] = piece_dma(p + 2, src_h, ("dv", cons[p]))
            return q.cnt["dv"]

        stage_cast(winT, winT_h, 8 * NST)
        stage_cast(wselT, wselT_h, 8 * SEL)
        stage_cast(wsoT, wsoT_h, 4 * D)
        dv_attn_w = stage_cast(woutT, woutT_h, 2 * D)

        last_dma.update({"ad": 0, "vd": 0})

        class WP:
            pass

        c1 = WP()
        c1.src, c1.npc = w1T_h, 32
        c1.queues = (("g", "gd"), ("sy", "sm"))
        c1.slot = lambda j: stgF[:, 1024 * j:1024 * (j + 1)]
        c1.acc, c1.accs, c1.vs = "QA", "QB", "QC"
        c1.junk = xin1tm[:, 0:1024]
        c1.mean_col, c1.scale_col = 0, 1
        c1.gate = [stg_gate[0], stg_gate[1]]
        c2 = WP()
        c2.src, c2.npc = w2T_h, 32
        c2.queues = (("a", "ad"), ("a", "ad"))
        c2.slot = lambda j: (tad if j == 0 else tbc)[:, 0:1024]
        c2.acc, c2.accs, c2.vs = "PA", "PB", "PC"
        c2.junk = xin1fm[:, 0:1024]
        c2.mean_col, c2.scale_col = 2, 3
        c2.gate = [None, None]

        def wp_dma(ctx, p, gate):
            qn, sem = ctx.queues[p % 2]
            if gate is None:
                gate = ctx.gate[p % 2]
            if gate:
                q.wait(qn, gate[0], gate[1])
            q.wait(qn, sem, last_dma[sem])
            v = q.op(qn, lambda e, S, p=p, ctx=ctx: e.dma_start(
                out=ctx.slot(p % 2), in_=ctx.src[:, 1024 * p:1024 * (p + 1)]),
                sem=sem, k=16)
            last_dma[sem] = v
            return (sem, v)

        for ctx in (c1, c2):
            ctx.dmas = [None] * (ctx.npc + 2)
            ctx.addm = [0] * (ctx.npc + 2)
            ctx.acm = [0] * (ctx.npc + 2)
            ctx.dmas[0] = wp_dma(ctx, 0, None)
            ctx.dmas[1] = wp_dma(ctx, 1, None)
        for p in range(32):
            for ctx in (c1, c2):
                q.wait("a", *ctx.dmas[p])
                if p >= 2:
                    q.wait("a", "dv", ctx.addm[p - 2])
                q.drain("a")
                ctx.acm[p] = q.op("a", lambda e, S, p=p, ctx=ctx: e.activation(
                    ctx.junk, ctx.slot(p % 2), AF.Abs,
                    accum_out=sc(ctx.acc, p % 2)), sem="ac")
                ctx.gate[p % 2] = ("ac", ctx.acm[p])
                if p + 2 < ctx.npc:
                    ctx.dmas[p + 2] = wp_dma(ctx, p + 2, ("ac", ctx.acm[p]))
                q.wait("v", "ac", ctx.acm[p])
                if p == 0:
                    ctx.addm[p] = q.op("v", lambda e, S, ctx=ctx: e.tensor_copy(
                        sc(ctx.accs), sc(ctx.acc, 0)), sem="dv")
                else:
                    q.drain("v")
                    ctx.addm[p] = q.op("v", lambda e, S, p=p, ctx=ctx: e.tensor_tensor(
                        sc(ctx.accs), sc(ctx.accs), sc(ctx.acc, p % 2), OP.add),
                        sem="dv")
        for ctx in (c1, c2):
            q.drain("v")
            vsum = q.op("v", lambda e, S, ctx=ctx: e.tensor_copy(
                sc(ctx.vs), sc(ctx.accs)), sem="dv")
            q.wait("t", "dv", vsum)
            mm1 = q.op("t", lambda e, S, ctx=ctx: e.matmul(
                pC[0:1, 0:1], ones[:, 0:1], sc(ctx.vs), start=True, stop=True),
                sem="pe")
            q.wait("v", "pe", mm1)
            q.op("v", lambda e, S, ctx=ctx: e.tensor_scalar(
                wtot[0:1, ctx.mean_col:ctx.mean_col + 1], pC[0:1, 0:1],
                1.0 / (128.0 * 1024 * ctx.npc), 1e-5, OP.mult, OP.max))
            q.drain("v")
            vrec = q.op("v", lambda e, S, ctx=ctx: e.reciprocal(
                wtot[0:1, ctx.scale_col:ctx.scale_col + 1],
                wtot[0:1, ctx.mean_col:ctx.mean_col + 1]), sem="dv")
            q.drain("v")
            q.wait("t", "dv", vrec)
            mm2 = q.op("t", lambda e, S, ctx=ctx: e.matmul(
                pC[:, 0:2], ones[0:1, 0:128],
                wtot[0:1, ctx.mean_col:ctx.mean_col + 2],
                start=True, stop=True), sem="pe")
            q.wait("v", "pe", mm2)
            ctx.vbc = q.op("v", lambda e, S, ctx=ctx: e.tensor_copy(
                wcons[:, ctx.mean_col:ctx.mean_col + 2], pC[:, 0:2]), sem="dv")
            q.drain("v")

        for ctx in (c1, c2):
            ctx.dmas = [None] * (ctx.npc + 2)
            ctx.clip = [0] * (ctx.npc + 2)
            ctx.dmas[0] = wp_dma(ctx, 0, None)
            ctx.dmas[1] = wp_dma(ctx, 1, None)
        last_sw = [q.cnt["swi"], q.cnt["swi"]]
        for p in range(32):
            ctx = c1
            q.wait("a", *ctx.dmas[p])
            if p == 0:
                q.wait("a", "dv", ctx.vbc)
            if p >= 1:
                q.wait("a", "dv", ctx.clip[p - 1])
            a_an = q.op("a", lambda e, S, p=p, ctx=ctx: e.activation(
                k2i[:, 0:1024], ctx.slot(p % 2), AF.Identity,
                scale=wcons[:, ctx.scale_col:ctx.scale_col + 1],
                bias=rcc), sem="ac")
            ctx.gate[p % 2] = ("ac", a_an)
            if p + 2 < ctx.npc:
                ctx.dmas[p + 2] = wp_dma(ctx, p + 2, ("ac", a_an))
            q.wait("v", "ac", a_an)
            dstb = w1t[:, 1024 * p:1024 * (p + 1)]
            q.drain("v")
            q.op("v", lambda e, S, dstb=dstb: e.tensor_scalar(
                dstb, k2i[:, 0:1024], -RC, -1.0, OP.add, OP.max))
            q.drain("v")
            ctx.clip[p] = q.op("v", lambda e, S, dstb=dstb: e.tensor_scalar(
                dstb, dstb, 1.0, None, OP.min), sem="dv")
            ctx = c2
            q.wait("a", *ctx.dmas[p])
            if p == 0:
                q.wait("a", "dv", ctx.vbc)
            dstb = stgb[:, 0:1024] if p % 2 == 0 else selh[:, 0:1024]
            lastc = 0
            for h in range(2):
                if p >= 1 or h == 1:
                    q.wait("a", "dv", ctx.clip[p - 1] if h == 0 else lastc)
                a_an = q.op("a", lambda e, S, p=p, h=h, ctx=ctx: e.activation(
                    stgi[:, :], ctx.slot(p % 2)[:, 512 * h:512 * (h + 1)],
                    AF.Identity,
                    scale=wcons[:, ctx.scale_col:ctx.scale_col + 1],
                    bias=rcc), sem="ac")
                if h == 1:
                    ctx.gate[p % 2] = ("ac", a_an)
                    if p + 2 < ctx.npc:
                        ctx.dmas[p + 2] = wp_dma(ctx, p + 2, ("ac", a_an))
                q.wait("v", "ac", a_an)
                if h == 0 and p >= 2:
                    q.wait("v", "swi", last_sw[p % 2])
                dsl = dstb[:, 512 * h:512 * (h + 1)]
                q.drain("v")
                q.op("v", lambda e, S, dsl=dsl: e.tensor_scalar(
                    dsl, stgi[:, :], -RC, -1.0, OP.add, OP.max))
                q.drain("v")
                lastc = q.op("v", lambda e, S, dsl=dsl: e.tensor_scalar(
                    dsl, dsl, 1.0, None, OP.min), sem="dv")
            ctx.clip[p] = lastc
            q.wait("a", "dv", ctx.clip[p])
            q.wait("a", "swi", max(last_sw))
            last_sw[p % 2] = q.op("a", lambda e, S, p=p, dstb=dstb: e.dma_start(
                out=w2s[:, 1024 * p:1024 * (p + 1)], in_=dstb),
                sem="swi", k=16)
        dv_w1 = c1.clip[31]
        dv_w2 = c2.vbc
        sw_w2 = last_sw[1] if last_sw[1] > last_sw[0] else last_sw[0]

        def quake(qn, dst_ap, src_ap, seed_t, n, pre):
            qa, qb, qc = (sc(pre[0], 0, n), sc(pre[1], 0, n), sc(pre[2], 0, n))
            q.op(qn, lambda e, S: e.tensor_copy(qa, src_ap.bitcast(I32)))
            q.drain(qn)
            q.op(qn, lambda e, S: e.tensor_scalar(
                seed_t[:, 0:n], qa, -0.5, QMAGIC, OP.mult, OP.add))
            q.drain(qn)
            cur = seed_t[:, 0:n].bitcast(F32)
            for it in range(3):
                tgt = dst_ap if it == 2 else qa
                q.op(qn, lambda e, S, cur=cur: e.tensor_tensor(qb, cur, cur, OP.mult))
                q.drain(qn)
                q.op(qn, lambda e, S: e.tensor_tensor(qc, src_ap, qb, OP.mult))
                q.drain(qn)
                q.op(qn, lambda e, S: e.tensor_scalar(qc, qc, -0.5, 1.5, OP.mult, OP.add))
                q.drain(qn)
                q.op(qn, lambda e, S, cur=cur, tgt=tgt: e.tensor_tensor(
                    tgt, cur, qc, OP.mult))
                q.drain(qn)
                cur = qa

        def stat_chain(qn, pre, n0, nn, var_scale_col=None):
            scr = ("PA", "PB", "PC") if qn == "g" else ("QA", "QB", "QC")
            seed_t = seedp if qn == "g" else seedi
            sem = "gp" if qn == "g" else "dv"
            Mc = sc("M" + pre, n0, nn)
            Ec = sc("E" + pre, n0, nn)
            q.drain(qn)
            q.op(qn, lambda e, S: e.tensor_scalar(
                Mc, sc("SUM" + pre, n0, nn), -1.0 / D, None, OP.mult))
            q.op(qn, lambda e, S: e.tensor_scalar(
                Ec, sc("SSQ" + pre, n0, nn), 1.0 / D, None, OP.mult))
            q.drain(qn)
            q.op(qn, lambda e, S: e.tensor_tensor(
                sc(scr[1], n0, nn), Mc, Mc, OP.mult))
            q.drain(qn)
            q.op(qn, lambda e, S: e.tensor_tensor(Ec, Ec, sc(scr[1], n0, nn),
                                                  OP.subtract))
            q.drain(qn)
            if var_scale_col is not None:
                q.op(qn, lambda e, S: e.tensor_tensor(
                    sc("GG", n0, nn), sc(var_scale_col, n0, nn),
                    sc(var_scale_col, n0, nn), OP.mult))
                q.drain(qn)
                q.op(qn, lambda e, S: e.tensor_tensor(
                    Ec, Ec, sc("GG", n0, nn), OP.mult))
                q.drain(qn)
            q.op(qn, lambda e, S: e.tensor_scalar(Ec, Ec, EPS, None, OP.add))
            q.drain(qn)
            quake(qn, sc("R" + pre, n0, nn), Ec, seed_t, nn, scr)
            q.drain(qn)
            if var_scale_col is not None:
                q.op(qn, lambda e, S: e.tensor_tensor(
                    sc("R" + pre, n0, nn), sc("R" + pre, n0, nn),
                    sc(var_scale_col, n0, nn), OP.mult))
                q.drain(qn)
            q.op(qn, lambda e, S: e.tensor_tensor(
                sc("B" + pre, n0, nn), Mc, sc("R" + pre, n0, nn), OP.mult))
            return q.op(qn, lambda e, S: e.tensor_copy(
                sc(scr[2], 0, 1), sc("M" + pre, n0, 1)), sem=sem)

        Z = lambda: [0] * (NT + 2)
        Z2 = lambda: [[0, 0] for _ in range(NT + 2)]
        PE_T1, A_evT1, PE_u, PE_sel = Z2(), Z(), Z(), Z()
        PE_g1, PE_g2, A_u, A_selh, A_tad, A_tbc, A_aff, V_y = (
            Z(), Z(), Z(), Z(), Z(), Z(), Z(), Z())
        PE_ao, A_aoev, A_t, P_x2 = Z2(), Z2(), Z2(), Z2()
        PE_T2, A_evT2 = Z2(), Z2()
        A_xin1, V_q1sc, V_halo = Z(), Z(), Z()
        P_uq1 = [[0] * 4 for _ in range(NT + 2)]
        PE_w1 = [[0] * 32 for _ in range(NT + 2)]
        A_evh = [[0] * 32 for _ in range(NT + 2)]
        V_cv = [[0] * 32 for _ in range(NT + 2)]
        A_sil = [[0] * 32 for _ in range(NT + 2)]
        A_anc = [[0] * 16 for _ in range(NT + 2)]
        P_uq2 = [[0] * 16 for _ in range(NT + 2)]
        PE_w2c = [[0] * 32 for _ in range(NT + 2)]
        SW_w2t = [[None] * 32 for _ in range(NT + 2)]
        PE_w2, V_bc2, A_ffevD, A_ffevE, V_chF = Z(), Z(), Z(), Z(), Z()
        P_out, SO_cnt = Z2(), Z()

        def emit_ln1_prep(i, wait_dv=0):
            if i >= ntr:
                return
            q.wait("a", *si_x[i])
            q.wait("a", "dv", wait_dv)
            jw = PE_T1[i - 1][1] if i > 0 else 0
            if jw:
                q.wait("a", "pe", jw)
            q.drain("a")
            for s in range(NSUB):
                q.op("a", lambda e, S, s=s, i=i: e.activation(
                    xin1tm[:, 1024 * s:1024 * (s + 1)], xinv[:, i % 2, s, :],
                    AF.Identity, accum_out=sc("SUM1", s)), sem="ac")
            q.drain("a")
            for s in range(NSUB):
                q.op("a", lambda e, S, s=s, i=i: e.activation(
                    xin1tm[:, 1024 * s:1024 * (s + 1)], xinv[:, i % 2, s, :],
                    AF.Square, accum_out=sc("SSQ1", s)), sem="ac")
            a_sums = q.cnt["ac"]
            q.wait("v", "ac", a_sums)
            if i == 0:
                q.wait("v", "dv", dv_dp2)
            g_ch = stat_chain("v", "1", 0, 2)
            q.wait("a", "dv", g_ch)
            for s in range(NSUB):
                q.op("a", lambda e, S, s=s, i=i: e.activation(
                    xin1tm[:, 1024 * s:1024 * (s + 1)], xinv[:, i % 2, s, :],
                    AF.Identity, bias=sc("B1", s), scale=sc("R1", s)), sem="ac")
            A_xin1[i] = q.cnt["ac"]

        def attn_gen(i):
            par = i % 2

            if i == 0:
                emit_ln1_prep(0)
                q.wait("t", "dv", dv_id)
                q.wait("t", "dv", dv_attn_w)
            q.wait("t", "ac", A_xin1[i])
            if i > 0:
                q.wait("t", "ac", A_evT2[i - 1][1])
            ev0 = [0]
            for s in range(NSUB):
                if s == 1:
                    q.wait("t", "ac", ev0[0])
                for cc in range(8):
                    PE_T1[i][s] = q.op("t", lambda e, S, s=s, cc=cc: e.matmul(
                        pC[:, 128 * cc:128 * (cc + 1)],
                        xin1tm[:, 1024 * s + 128 * cc:1024 * s + 128 * (cc + 1)],
                        identb[:, :], start=True, stop=True),
                        sem="pe" if cc == 7 else None)
                q.wait("a", "pe", PE_T1[i][s])
                if i > 0 and s == 0:
                    q.wait("a", "pe", PE_sel[i - 1])
                ev = q.op("a", lambda e, S, s=s: e.activation(
                    fm1v[:, :, s, :], pCv[:, :, :], AF.Identity), sem="ac")
                if s == 0:
                    ev0[0] = ev
            A_evT1[i] = q.cnt["ac"]
            yield

            q.wait("t", "ac", A_evT1[i])
            if i > 1:
                q.wait("t", "dv", V_bc2[i - 2])
            for n in range(2):
                for k in range(8):
                    PE_u[i] = q.op("t", lambda e, S, n=n, k=k: e.matmul(
                        pB[:, 256 * n:256 * (n + 1)],
                        winT[:, 256 * k + 128 * n:256 * k + 128 * (n + 1)],
                        xin1fm[:, 256 * k:256 * (k + 1)],
                        start=(k == 0), stop=(k == 7)),
                        sem="pe" if (n == 1 and k == 7) else None)
            for m in range(4):
                for k in range(8):
                    PE_sel[i] = q.op("t", lambda e, S, m=m, k=k: e.matmul(
                        pC[:, 256 * m:256 * (m + 1)],
                        wselT[:, 512 * k + 128 * m:512 * k + 128 * (m + 1)],
                        xin1fm[:, 256 * k:256 * (k + 1)],
                        start=(k == 0), stop=(k == 7)),
                        sem="pe" if (m == 3 and k == 7) else None)
            q.wait("a", "pe", PE_u[i])
            if i > 0:
                q.wait("a", "dv", V_y[i - 1])
            A_u[i] = q.op("a", lambda e, S: e.activation(
                u_sb[:, :], pB[:, :], AF.Identity), sem="ac")
            q.wait("a", "pe", PE_sel[i])
            if i == 0:
                q.wait("a", "swi", sw_w2)
            if i > 0:
                q.wait("a", "pe", PE_g2[i - 1])
            A_selh[i] = q.op("a", lambda e, S: e.activation(
                selh[:, :], pC[:, :], AF.Silu), sem="ac")
            yield

            q.wait("t", "ac", A_selh[i])
            for gi, m in enumerate((0, 1, 6, 7)):
                for k in range(4):
                    PE_g1[i] = q.op("t", lambda e, S, gi=gi, m=m, k=k: e.matmul(
                        pC[:, 256 * gi:256 * (gi + 1)],
                        wsoT[:, 1024 * k + 128 * m:1024 * k + 128 * (m + 1)],
                        selh[:, 256 * k:256 * (k + 1)],
                        start=(k == 0), stop=(k == 3)),
                        sem="pe" if (gi == 3 and k == 3) else None)
            yield

            q.wait("a", "pe", PE_g1[i])
            if i > 0:
                q.wait("a", "dv", V_y[i - 1])
            A_tad[i] = q.op("a", lambda e, S: e.activation(
                tad[:, :], pC[:, :], AF.Tanh, scale=0.5), sem="ac")
            q.drain("a")
            A_aff[i] = q.op("a", lambda e, S: e.activation(
                tad[:, 0:512], tad[:, 0:512], AF.Identity,
                scale=halfc, bias=halfc),
                sem="ac")
            q.wait("t", "ac", A_tad[i])
            for gi, m in enumerate((2, 3, 4, 5)):
                for k in range(4):
                    PE_g2[i] = q.op("t", lambda e, S, gi=gi, m=m, k=k: e.matmul(
                        pC[:, 256 * gi:256 * (gi + 1)],
                        wsoT[:, 1024 * k + 128 * m:1024 * k + 128 * (m + 1)],
                        selh[:, 256 * k:256 * (k + 1)],
                        start=(k == 0), stop=(k == 3)),
                        sem="pe" if (gi == 3 and k == 3) else None)
            q.wait("a", "pe", PE_g2[i])
            A_tbc[i] = q.op("a", lambda e, S: e.activation(
                tbc[:, :], pC[:, :], AF.Tanh), sem="ac")
            q.wait("v", "ac", A_aff[i])
            q.wait("v", "ac", A_u[i])
            if i > 0:
                q.wait("v", "pe", PE_ao[i - 1][1])
            q.drain("v")
            q.op("v", lambda e, S: e.scalar_tensor_tensor(
                tad[:, 512:1024], tad[:, 512:1024], 1.0, u_sb[:, :], OP.add, OP.mult))
            q.wait("v", "ac", A_tbc[i])
            q.op("v", lambda e, S: e.tensor_tensor(
                tbc[:, 0:512], tbc[:, 0:512], u_sb[:, :], OP.mult))
            q.drain("v")
            hof, hofp = 512 * par, 512 * (1 - par)
            for n in range(2):
                init = 0.0 if i == 0 else hsc[:, hofp + 256 * n + 255:hofp + 256 * n + 256]
                q.op("v", lambda e, S, n=n, init=init, hof=hof: e.tensor_tensor_scan(
                    hsc[:, hof + 256 * n:hof + 256 * (n + 1)],
                    tad[:, 256 * n:256 * (n + 1)],
                    tbc[:, 256 * n:256 * (n + 1)],
                    init, OP.mult, OP.add))
            q.drain("v")
            q.op("v", lambda e, S, hof=hof: e.tensor_tensor(
                tbc[:, 512:1024], tbc[:, 512:1024], hsc[:, hof:hof + 512], OP.mult))
            q.drain("v")
            for n in range(2):
                V_y[i] = q.op("v", lambda e, S, n=n: e.scalar_tensor_tensor(
                    ysb[:, 256 * n:256 * (n + 1)],
                    tad[:, 512 + 256 * n:512 + 256 * (n + 1)],
                    dp2[:, n:n + 1],
                    tbc[:, 512 + 256 * n:512 + 256 * (n + 1)],
                    OP.mult, OP.add), sem="dv")
            yield

            q.wait("t", "dv", V_y[i])
            q.wait("t", "ac", A_tbc[i])
            for s in range(NSUB):
                bank = pC if s == 0 else pD
                if s == 1:
                    if i >= 3:
                        q.wait("t", "ac", A_ffevD[i - 2])
                    elif i >= 1:
                        q.wait("t", "ac", A_aoev[i - 1][1])
                for n in range(2):
                    for jj in range(2):
                        PE_ao[i][s] = q.op("t",
                            lambda e, S, bank=bank, s=s, n=n, jj=jj: e.matmul(
                                bank[:, 512 * jj:512 * (jj + 1)],
                                ysb[:, 256 * n + 128 * s:256 * n + 128 * (s + 1)],
                                woutT[:, 1024 * n + 512 * jj:1024 * n + 512 * (jj + 1)],
                                start=(n == 0), stop=(n == 1)),
                            sem="pe" if (n == 1 and jj == 1) else None)
            q.wait("a", "pe", PE_ao[i][0])
            if i > 0:
                q.wait("a", "dv", P_x2[i - 1][0])
            A_aoev[i][0] = q.op("a", lambda e, S: e.activation(
                stgA[:, :], pC[:, 0:1024], AF.Identity, accum_out=sc("SUM2", 0)),
                sem="ac")
            q.wait("a", "pe", PE_ao[i][1])
            if i > 1 and SO_cnt[i - 2]:
                q.wait("a", "so", SO_cnt[i - 2])
            if i > 0:
                q.wait("a", "dv", P_x2[i - 1][1])
            A_aoev[i][1] = q.op("a", lambda e, S: e.activation(
                stgF[:, 0:1024], pD[:, 0:1024], AF.Identity, accum_out=sc("SUM2", 1)),
                sem="ac")
            q.drain("a")
            q.op("a", lambda e, S: e.activation(
                xin1tm[:, 0:1024], stgA[:, :], AF.Square, accum_out=sc("SSQ2", 0)),
                sem="ac")
            q.op("a", lambda e, S: e.activation(
                xin1tm[:, 1024:2048], stgF[:, 0:1024], AF.Square,
                accum_out=sc("SSQ2", 1)), sem="ac")
            a_sq2 = q.cnt["ac"]
            q.wait("g", "ac", a_sq2)
            g_ch2 = stat_chain("g", "2", 0, 2)
            yield

            yield

            q.wait("a", "gp", g_ch2)
            for s in range(NSUB):
                dst = stgA[:, :] if s == 0 else stgF[:, 0:1024]
                A_t[i][s] = q.op("a", lambda e, S, s=s, dst=dst: e.activation(
                    dst, dst, AF.Identity, bias=sc("B2", s), scale=sc("R2", s)),
                    sem="ac")
            for s in range(NSUB):
                q.wait("v", "ac", A_t[i][s])
                if i > 1:
                    q.wait("v", "gp", P_out[i - 2][1 - s])
                q.drain("v")
                src = stgA[:, :] if s == 0 else stgF[:, 0:1024]
                P_x2[i][s] = q.op("v", lambda e, S, s=s, src=src, i=i:
                                  e.tensor_tensor(x2s(i, s), xinv[:, i % 2, s, :],
                                                  src, OP.add), sem="dv")
            issue_x_dma(i + 2, wait_gp=P_x2[i][1], wait_ac=A_xin1[i])
            q.drain("v")
            for s in range(NSUB):
                q.op("v", lambda e, S, s=s, i=i: e.reduce_max(
                    sc("MX", s), x2s(i, s), AX))
                q.op("v", lambda e, S, s=s, i=i: e.tensor_reduce(
                    sc("MN", s), x2s(i, s), AX, OP.min))
            for s in range(NSUB):
                q.wait("a", "dv", P_x2[i][s])
                q.op("a", lambda e, S, s=s, i=i: e.activation(
                    xin1tm[:, 1024 * s:1024 * (s + 1)], x2s(i, s), AF.Square,
                    accum_out=sc("SSQ3", s)), sem="ac")
            a_sum3 = q.cnt["ac"]
            q.drain("v")
            q.op("v", lambda e, S: e.tensor_tensor(
                sc("QB", 0, 2), sc("R2", 0, 2), sc("SUM2", 0, 2), OP.mult))
            q.drain("v")
            q.op("v", lambda e, S: e.tensor_tensor(
                sc("QB", 0, 2), sc("QB", 0, 2), sc("SUM1", 0, 2), OP.add))
            q.drain("v")
            q.op("v", lambda e, S: e.tensor_scalar(
                sc("QC", 0, 2), sc("B2", 0, 2), float(D), None, OP.mult))
            q.drain("v")
            v_sum3 = q.op("v", lambda e, S: e.tensor_tensor(
                sc("SUM3", 0, 2), sc("QB", 0, 2), sc("QC", 0, 2), OP.add), sem="dv")
            emit_ln1_prep(i + 1, wait_dv=v_sum3)
            q.wait("v", "ac", a_sum3)
            q.drain("v")
            q.op("v", lambda e, S: e.tensor_scalar(
                sc("M3", 0, 2), sc("SUM3", 0, 2), -1.0 / D, None, OP.mult))
            q.op("v", lambda e, S: e.tensor_scalar(
                sc("E3", 0, 2), sc("SSQ3", 0, 2), 1.0 / D, None, OP.mult))
            q.drain("v")
            q.op("v", lambda e, S: e.tensor_tensor(
                sc("QB", 0, 2), sc("M3", 0, 2), sc("M3", 0, 2), OP.mult))
            q.drain("v")
            q.op("v", lambda e, S: e.tensor_tensor(
                sc("E3", 0, 2), sc("E3", 0, 2), sc("QB", 0, 2), OP.subtract))
            q.drain("v")
            q.op("v", lambda e, S: e.tensor_scalar(
                sc("E3", 0, 2), sc("E3", 0, 2), EPS, None, OP.add))
            q.drain("v")
            quake("v", sc("R3", 0, 2), sc("E3", 0, 2), seedi, 2, ("QA", "QB", "QC"))
            q.drain("v")
            q.op("v", lambda e, S: e.tensor_tensor(
                sc("TA", 0, 2), sc("MX", 0, 2), sc("M3", 0, 2), OP.add))
            q.op("v", lambda e, S: e.tensor_tensor(
                sc("TB", 0, 2), sc("MN", 0, 2), sc("M3", 0, 2), OP.add))
            q.drain("v")
            q.op("v", lambda e, S: e.tensor_tensor(
                sc("TA", 0, 2), sc("TA", 0, 2), sc("R3", 0, 2), OP.mult))
            q.op("v", lambda e, S: e.tensor_scalar(
                sc("TB", 0, 2), sc("TB", 0, 2), -1.0, None, OP.mult))
            q.drain("v")
            q.op("v", lambda e, S: e.tensor_tensor(
                sc("TB", 0, 2), sc("TB", 0, 2), sc("R3", 0, 2), OP.mult))
            q.drain("v")
            q.op("v", lambda e, S: e.tensor_tensor(
                sc("MABS", 0, 2), sc("TA", 0, 2), sc("TB", 0, 2), OP.max))
            q.drain("v")
            q.op("v", lambda e, S: e.tensor_scalar(
                sc("MABS", 0, 2), sc("MABS", 0, 2), 1e-5, None, OP.max))
            q.drain("v")
            q.op("v", lambda e, S: e.reciprocal(sc("SC1", 0, 2), sc("MABS", 0, 2)))
            q.drain("v")
            q.op("v", lambda e, S: e.tensor_scalar(
                sc("SC1", 0, 2), sc("SC1", 0, 2), 127.0, None, OP.mult))
            q.drain("v")
            q.op("v", lambda e, S: e.tensor_tensor(
                sc("A1", 0, 2), sc("R3", 0, 2), sc("SC1", 0, 2), OP.mult))
            q.drain("v")
            q.op("v", lambda e, S: e.tensor_tensor(
                sc("B1Q", 0, 2), sc("M3", 0, 2), sc("A1", 0, 2), OP.mult))
            q.drain("v")
            q.op("v", lambda e, S: e.tensor_scalar(
                sc("B1Q", 0, 2), sc("B1Q", 0, 2), RC, None, OP.add))
            if i == 0:
                q.wait("v", "dv", dv_w1)
            q.drain("v")
            V_q1sc[i] = q.op("v", lambda e, S: e.tensor_scalar(
                sc("G1", 0, 2), sc("MABS", 0, 2), wcons[:, 0:1], 1.0 / 127.0,
                OP.mult, OP.mult), sem="dv")
            yield

            q.wait("a", "dv", V_q1sc[i])
            q.wait("g", "dv", V_q1sc[i])
            if i > 0:
                q.wait("g", "pe", PE_T2[i - 1][1])
            if i == 0:
                q.wait("g", "swi", sw_w2)
            for s in range(NSUB):
                if s == 1:
                    q.wait("g", "pe", PE_T2[i][0])
                for h in range(2):
                    hi = 2 * s + h
                    if hi > 0:
                        q.wait("a", "gp", P_uq1[i][hi - 1])
                    a_an = q.op("a", lambda e, S, s=s, h=h, i=i: e.activation(
                        stgi[:, :], x2sh(i, s, h), AF.Identity,
                        scale=sc("A1", s), bias=sc("B1Q", s)), sem="ac")
                    q.wait("g", "ac", a_an)
                    q.drain("g")
                    bsl = stgb[:, 512 * h:512 * (h + 1)]
                    q.op("g", lambda e, S, bsl=bsl: e.tensor_scalar(
                        bsl, stgi[:, :],
                        onec, rccn, OP.mult, OP.add))
                    q.drain("g")
                    P_uq1[i][hi] = q.op("g", lambda e, S, s=s, bsl=bsl: e.tensor_scalar(
                        bsl, bsl,
                        sc("G1", s), zeroc, OP.mult, OP.add), sem="gp")
                q.wait("t", "gp", P_uq1[i][2 * s + 1])
                q.wait("t", "ac", A_aoev[i][0] if s == 0 else A_evT2[i][0])
                for cc in range(8):
                    PE_T2[i][s] = q.op("t", lambda e, S, s=s, cc=cc: e.matmul(
                        pC[:, 128 * cc:128 * (cc + 1)],
                        stgb[:, 128 * cc:128 * (cc + 1)],
                        identb[:, :], start=True, stop=True),
                        sem="pe" if cc == 7 else None)
                q.wait("a", "pe", PE_T2[i][s])
                if s == 0 and i > 0:
                    q.wait("a", "pe", PE_w1[i - 1][31])
                if s == 0:
                    dst = xqv[:, par, :, 129:257]
                else:
                    dst = xqv[:, 1 - par, :, 1:129]
                q.op("a", lambda e, S, dst=dst: e.activation(
                    dst, pCv[:, :, :], AF.Identity), sem="ac")
                A_evT2[i][s] = q.cnt["ac"]
            q.wait("v", "ac", A_evT2[i][1])
            if i > 0:
                q.wait("v", "pe", PE_w1[i - 1][31])
            q.drain("v")
            q.op("v", lambda e, S, par=par: e.tensor_copy(
                xqv[:, 1 - par, :, 0], xqv[:, par, :, 256]))
            V_halo[i] = q.op("v", lambda e, S, par=par: e.tensor_copy(
                xqv[:, par, :, 257], xqv[:, 1 - par, :, 1]), sem="dv")
            if i == ntr - 1:
                q.drain("v")
                V_halo[i + 1] = q.op("v", lambda e, S: e.memset(
                    xqv[:, ntr % 2, :, 129:258], 0.0), sem="dv")

        def win_gen(f):
            pf = f % 2
            have_a = f > 0
            have_b = f < ntr
            subs = ([0] if have_a else []) + ([1] if have_b else [])
            emit_z8b(f - 1)

            def issue_w2t(c):
                if c >= 32:
                    return
                if f == 0 and c < 5:
                    q.wait("sy", "swi", sw_w2)
                if c >= 5:
                    q.wait("sy", "pe", PE_w2c[f][c - 5])
                elif f > 0:
                    q.wait("sy", "pe", PE_w2c[f - 1][27 + c])
                swn = "sw%d" % (c % 5)
                SW_w2t[f][c] = (swn, q.op("sy", lambda e, S, c=c: e.dma_start(
                    out=w2sb[:, 1024 * (c % 5):1024 * (c % 5 + 1)],
                    in_=w2s[:, 1024 * c:1024 * (c + 1)]), sem=swn, k=16))

            for _c0 in range(5):
                issue_w2t(_c0)

            q.wait("t", "dv", V_halo[f])
            if f == 0:
                q.wait("t", "dv", dv_w1)
                q.wait("t", "dv", dv_pad0)
            if f > 0:
                q.wait("t", "ac", A_ffevE[f - 1])
            for c in range(32):
                off = 512 * (c % 3)
                if c >= 3:
                    q.wait("t", "ac", A_evh[f][c - 3])
                for k in range(8):
                    PE_w1[f][c] = q.op("t",
                        lambda e, S, c=c, k=k, off=off, pf=pf: e.matmul(
                            pE[:, off:off + 258],
                            w1t[:, 4096 * k + 128 * c:4096 * k + 128 * (c + 1)],
                            xqv[:, pf, k, :],
                            start=(k == 0), stop=(k == 7)),
                        sem="pe" if k == 7 else None)
                q.wait("a", "pe", PE_w1[f][c])
                if c >= 4:
                    q.wait("a", "dv", V_cv[f][c - 4])
                q.op("a", lambda e, S, c=c, off=off: e.activation(
                    hpre[:, 258 * (c % 4):258 * (c % 4 + 1)], pE[:, off:off + 258],
                    AF.Identity), sem="ac")
                if c % 2 == 0:
                    if c == 0 and f > 0:
                        q.wait("a", "pe", PE_w2[f - 1])
                    q.op("a", lambda e, S, c=c, off=off: e.activation(
                        h4[:, 256 * c:256 * (c + 1)], pE[:, off + 1:off + 257],
                        AF.Identity, scale=cwt[:, 3 * c + 1:3 * c + 2],
                        bias=cbt[:, c:c + 1]), sem="ac")
                A_evh[f][c] = q.cnt["ac"]
                if c % 4 == 3:
                    if c >= 7:
                        cp = c - 4
                        q.wait("a", "dv", V_cv[f][cp])
                        A_sil[f][cp - 3] = A_sil[f][cp] = q.op(
                            "a", lambda e, S, cp=cp: e.activation(
                                h4[:, 256 * (cp - 3):256 * (cp + 1)],
                                h4[:, 256 * (cp - 3):256 * (cp + 1)], AF.Silu),
                            sem="ac")
                    q.wait("v", "ac", A_evh[f][c])
                    if c == 3 and f > 0:
                        q.wait("v", "pe", PE_w2[f - 1])
                    for cc in range(c - 2, c + 1, 2):
                        hs = hpre[:, 258 * (cc % 4):258 * (cc % 4 + 1)]
                        hd = h4[:, 256 * cc:256 * (cc + 1)]
                        q.op("v", lambda e, S, cc=cc, hs=hs, hd=hd: e.tensor_scalar(
                            hd, hs[:, 1:257], cwt[:, 3 * cc + 1:3 * cc + 2],
                            cbt[:, cc:cc + 1], OP.mult, OP.add))
                    q.drain("v")
                    for cc in range(c - 3, c + 1):
                        hs = hpre[:, 258 * (cc % 4):258 * (cc % 4 + 1)]
                        hd = h4[:, 256 * cc:256 * (cc + 1)]
                        q.op("v", lambda e, S, cc=cc, hs=hs, hd=hd:
                             e.scalar_tensor_tensor(
                                 hd, hs[:, 0:256], cwt[:, 3 * cc:3 * cc + 1], hd,
                                 OP.mult, OP.add))
                    q.drain("v")
                    for cc in range(c - 3, c + 1):
                        hs = hpre[:, 258 * (cc % 4):258 * (cc % 4 + 1)]
                        hd = h4[:, 256 * cc:256 * (cc + 1)]
                        q.op("v", lambda e, S, cc=cc, hs=hs, hd=hd:
                             e.scalar_tensor_tensor(
                                 hd, hs[:, 2:258], cwt[:, 3 * cc + 2:3 * cc + 3],
                                 hd, OP.mult, OP.add),
                             sem="dv" if cc == c else None)
                    for cc in range(c - 3, c + 1):
                        V_cv[f][cc] = q.cnt["dv"]
                    if c >= 7:
                        cp = c - 4
                        q.wait("v", "ac", A_sil[f][cp])
                        q.drain("v")
                        for hh in range(4):
                            gsl = h4[:, 256 * (cp - 3 + hh):256 * (cp - 2 + hh)]
                            if cp == 3 and hh == 0:
                                q.op("v", lambda e, S, gsl=gsl: e.tensor_copy(
                                    accP[:, :], gsl))
                                q.op("v", lambda e, S, gsl=gsl: e.tensor_copy(
                                    accN[:, :], gsl))
                                q.drain("v")
                            else:
                                q.op("v", lambda e, S, gsl=gsl: e.tensor_tensor(
                                    accP[:, :], accP[:, :], gsl, OP.max))
                                q.op("v", lambda e, S, gsl=gsl: e.tensor_tensor(
                                    accN[:, :], accN[:, :], gsl, OP.min))
                                q.drain("v")
                if c % 8 == 7:
                    yield

            q.wait("a", "dv", V_cv[f][31])
            A_sil[f][28] = A_sil[f][31] = q.op(
                "a", lambda e, S: e.activation(
                    h4[:, 256 * 28:256 * 32], h4[:, 256 * 28:256 * 32], AF.Silu),
                sem="ac")
            q.wait("v", "ac", A_sil[f][31])
            q.drain("v")
            for cc in range(28, 32):
                gsl = h4[:, 256 * cc:256 * (cc + 1)]
                q.op("v", lambda e, S, gsl=gsl: e.tensor_tensor(
                    accP[:, :], accP[:, :], gsl, OP.max))
                q.op("v", lambda e, S, gsl=gsl: e.tensor_tensor(
                    accN[:, :], accN[:, :], gsl, OP.min))
                q.drain("v")
            v_mab = q.op("v", lambda e, S: e.scalar_tensor_tensor(
                accP[:, :], accN[:, :], -1.0, accP[:, :], OP.mult, OP.max),
                sem="dv")
            q.wait("t", "dv", v_mab)
            if f + 1 < ntr:
                q.wait("t", "ac", A_u[f + 1])
            else:
                q.wait("t", "dv", V_bc2[f - 1])
            for s in range(2):
                pe_mabT = q.op("t", lambda e, S, s=s: e.matmul(
                    pB[:, 128 * s:128 * (s + 1)],
                    accP[:, 128 * s:128 * (s + 1)],
                    identb[:, :], start=True, stop=True),
                    sem="pe" if s == 1 else None)
            q.wait("v", "pe", pe_mabT)
            for s in range(2):
                q.op("v", lambda e, S, s=s: e.reduce_max(
                    sc("MAB2", s), pB[:, 128 * s:128 * (s + 1)], AX))
            q.drain("v")
            q.op("v", lambda e, S: e.tensor_scalar(
                sc("MAB2", 0, 2), sc("MAB2", 0, 2), 1e-5, None, OP.max))
            q.drain("v")
            q.op("v", lambda e, S: e.reciprocal(sc("SC2", 0, 2), sc("MAB2", 0, 2)))
            q.drain("v")
            q.op("v", lambda e, S: e.tensor_scalar(
                sc("SC2", 0, 2), sc("SC2", 0, 2), 127.0, None, OP.mult))
            if f == 0:
                q.wait("v", "dv", dv_w2)
            q.op("v", lambda e, S: e.tensor_scalar(
                sc("G2", 0, 2), sc("MAB2", 0, 2), wcons[:, 2:3], 1.0 / 127.0,
                OP.mult, OP.mult))
            q.drain("v")
            v_sc2 = q.op("v", lambda e, S: e.tensor_copy(sc("QC"), sc("SC2")),
                         sem="dv")
            q.wait("t", "dv", v_sc2)
            for s in range(2):
                pe_r = q.op("t", lambda e, S, s=s: e.matmul(
                    pB[0:1, 256 + 128 * s:256 + 128 * (s + 1)], sc("SC2", s),
                    identf[:, :], start=True, stop=True),
                    sem="pe" if s == 1 else None)
            q.wait("v", "pe", pe_r)
            v_row = q.op("v", lambda e, S: e.tensor_copy(
                tad[0:1, 0:256], pB[0:1, 256:512]), sem="dv")
            q.wait("t", "dv", v_row)
            pe_bc = q.op("t", lambda e, S: e.matmul(
                pB[:, 256:512], ones[0:1, 0:128], tad[0:1, 0:256],
                start=True, stop=True), sem="pe")
            q.wait("v", "pe", pe_bc)
            V_bc2[f] = q.op("v", lambda e, S: e.tensor_copy(bc2[:, :], pB[:, 256:512]),
                            sem="dv")
            yield

            q.wait("t", "dv", V_bc2[f])
            if f + 1 < ntr:
                q.wait("t", "ac", A_aoev[f + 1][1])
            elif f > 0:
                q.wait("t", "ac", A_ffevD[f - 1])
            q.wait("t", "ac", A_evh[f][31])
            def emit_pair(p2):
                c = 2 * p2
                if p2 == 0:
                    q.wait("v", "dv", V_bc2[f])
                    q.drain("v")
                vtt = 0
                for cc in (c, c + 1):
                    hsl4 = h4[:, 256 * cc:256 * (cc + 1)]
                    vtt = q.op("v", lambda e, S, hsl4=hsl4: e.tensor_tensor(
                        hsl4, hsl4, bc2[:, :], OP.mult),
                        sem="dv" if cc == c + 1 else None)
                q.wait("a", "dv", vtt)
                if p2 >= 2:
                    q.wait("a", "gp", P_uq2[f][p2 - 2])
                elif f > 0:
                    q.wait("a", "gp", P_uq2[f - 1][14 + p2])
                ksl = k2i[:, 512 * (p2 % 2):512 * (p2 % 2 + 1)]
                psl = h4[:, 512 * p2:512 * (p2 + 1)]
                A_anc[f][p2] = q.op("a", lambda e, S, ksl=ksl, psl=psl:
                                    e.activation(ksl, psl, AF.Identity,
                                                 bias=rcc), sem="ac")
                q.wait("g", "ac", A_anc[f][p2])
                P_uq2[f][p2] = q.op("g", lambda e, S, ksl=ksl, psl=psl:
                                    e.tensor_scalar(psl, ksl, onec,
                                                    rccn, OP.mult, OP.add),
                                    sem="gp")

            for p2 in range(6):
                emit_pair(p2)
            for c in range(32):
                if c >= 12 and c % 2 == 0:
                    emit_pair(c // 2)
                q.wait("t", "gp", P_uq2[f][c // 2])
                q.wait("t", *SW_w2t[f][c])
                for s in subs:
                    bank = pD if s == 0 else pE
                    for jj in range(2):
                        coff = 1024 * (c % 5) + 512 * jj
                        mm = q.op("t", lambda e, S, c=c, s=s, jj=jj, bank=bank,
                                  coff=coff: e.matmul(
                                      bank[:, 512 * jj:512 * (jj + 1)],
                                      h4[:, 256 * c + 128 * s:256 * c + 128 * (s + 1)],
                                      w2sb[:, coff:coff + 512],
                                      start=(c == 0), stop=(c == 31)),
                                  sem="pe" if (s == subs[-1] and jj == 1) else None)
                PE_w2c[f][c] = mm
                issue_w2t(c + 5)
                if c == 11:
                    yield
            PE_w2[f] = PE_w2c[f][31]
            yield

            q.wait("a", "pe", PE_w2[f])
            if SO_cnt[f - 1]:
                q.wait("a", "so", SO_cnt[f - 1])
            if have_b:
                q.wait("a", "dv", P_x2[f][1])
                A_ffevE[f] = q.op("a", lambda e, S: e.activation(
                    stgF[:, 1024:2048], pE[:, 0:1024], AF.Identity,
                    accum_out=sc("SUMF", 1)), sem="ac")
            if have_a:
                A_ffevD[f] = q.op("a", lambda e, S: e.activation(
                    stgF[:, 0:1024], pD[:, 0:1024], AF.Identity,
                    accum_out=sc("SUMF", 0)), sem="ac")
            q.drain("a")
            q.wait("a", "dv", P_x2[min(f + 1, ntr - 1)][0])
            for s in subs:
                src_ = stgF[:, 1024 * s:1024 * (s + 1)]
                q.op("a", lambda e, S, s=s, src_=src_: e.activation(
                    stgA[:, :], src_, AF.Square, accum_out=sc("SSQF", s)), sem="ac")
                q.drain("a")
            a_sqf = q.cnt["ac"]
            q.wait("v", "ac", a_sqf)
            V_chF[f] = stat_chain("v", "F", subs[0], len(subs), var_scale_col="G2")

        def emit_z8b(f):
            if f < 0:
                return
            have_a = f > 0
            have_b = f < ntr
            subs = ([0] if have_a else []) + ([1] if have_b else [])
            q.wait("a", "dv", V_chF[f])
            a_apF = 0
            for s in subs:
                dst = stgF[:, 1024 * s:1024 * (s + 1)]
                a_apF = q.op("a", lambda e, S, s=s, dst=dst: e.activation(
                    dst, dst, AF.Identity, bias=sc("BF", s), scale=sc("RF", s)),
                    sem="ac")
            q.wait("g", "ac", a_apF)
            if have_a:
                q.wait("g", "dv", P_x2[f - 1][1])
            if have_b:
                q.wait("g", "dv", P_x2[f][0])
            q.drain("g")
            for s in subs:
                dst = stgF[:, 1024 * s:1024 * (s + 1)]
                x2src = x2s(f - 1, 1) if s == 0 else x2s(f, 0)
                P_out[f][s] = q.op("g", lambda e, S, dst=dst, x2src=x2src:
                                   e.tensor_tensor(dst, x2src, dst, OP.add),
                                   sem="gp")
                rows = ((f - 1) * TT + 128) if s == 0 else f * TT
                q.wait("sy", "gp", P_out[f][s])
                dstH = out_hbm[rows:rows + 128, :]
                SO_cnt[f] = q.op("sy", lambda e, S, dstH=dstH, dst=dst: e.dma_start(
                    out=dstH, in_=dst), sem="so", k=16)

        def run_interleaved(g1, g2):
            gens = [g1, g2]
            while gens:
                for g in list(gens):
                    try:
                        next(g)
                    except StopIteration:
                        gens.remove(g)

        for i in range(ntr):
            if i > 0:
                run_interleaved(attn_gen(i), win_gen(i - 1))
            else:
                for _ in attn_gen(0):
                    pass
        if ntr > 0:
            for _ in win_gen(ntr - 1):
                pass
            for _ in win_gen(ntr):
                pass
            emit_z8b(ntr)
        for _s, _v in q.cnt.items():
            q.wait("sy", _s, _v)

        with nc.Block() as block:
            @block.sync
            def _(eng):
                for fn in q.q["sy"]:
                    fn(eng, sems)

            @block.vector
            def _(eng):
                for fn in q.q["v"]:
                    fn(eng, sems)

            @block.scalar
            def _(eng):
                for fn in q.q["a"]:
                    fn(eng, sems)

            @block.tensor
            def _(eng):
                for fn in q.q["t"]:
                    fn(eng, sems)

            @block.gpsimd
            def _(eng):
                for fn in q.q["g"]:
                    fn(eng, sems)

    return nc


def _host_layout(inputs):
    f32 = np.float32

    def fm(w, nchunk):
        i, o = w.shape
        return np.ascontiguousarray(
            w.reshape(nchunk, 128, o).transpose(1, 0, 2).reshape(128, nchunk * o))

    W_in = np.asarray(inputs["L0_attn_W_in"], f32)
    W_sel = np.asarray(inputs["L0_attn_W_sel_in"], f32)
    W_so = np.asarray(inputs["L0_attn_W_sel_out"], f32)
    W_out = np.asarray(inputs["L0_attn_W_out"], f32)
    w1 = np.asarray(inputs["L0_ffn_w1"], f32)
    w2 = np.asarray(inputs["L0_ffn_w2"], f32)
    d_param = np.asarray(inputs["L0_attn_d_param"], f32)
    cw = np.asarray(inputs["L0_ffn_conv_w"], f32)[:, 0, :]
    cb = np.asarray(inputs["L0_ffn_conv_b"], f32)
    return {
        "winT": fm(np.ascontiguousarray(W_in.T), 8),
        "wselT": fm(np.ascontiguousarray(W_sel.T), 8),
        "wsoT": fm(np.ascontiguousarray(W_so.T), 4),
        "woutT": fm(np.ascontiguousarray(W_out.T), 2),
        "w1T": fm(np.ascontiguousarray(w1.T), 8),
        "w2T": fm(np.ascontiguousarray(w2.T), 32),
        "dp": np.ascontiguousarray(d_param.reshape(2, 128).T),
        "cw": np.ascontiguousarray(cw.reshape(32, 128, 3).transpose(1, 0, 2).reshape(128, 96)),
        "cb": np.ascontiguousarray(cb.reshape(32, 128).T),
    }


def kernel(**inputs):
    x = np.ascontiguousarray(np.asarray(inputs["L0_attn_x"], np.float32))
    wd = _host_layout(inputs)
    nc = build_program()
    in_maps = []
    for c in range(8):
        m = {"x": np.ascontiguousarray(x[c])}
        m.update(wd)
        in_maps.append(m)
    res = run_bass_kernel_spmd(nc, in_maps, core_ids=list(range(8)))
    return np.stack([np.asarray(r["out"]) for r in res.results], axis=0)
